# version 01
# baseline (speedup 1.0000x reference)
import numpy as np
from contextlib import ExitStack

import concourse.bass as bass
import concourse.mybir as mybir
from concourse.bass_utils import run_bass_kernel_spmd

F32 = mybir.dt.float32
BF16 = mybir.dt.bfloat16
AF = mybir.ActivationFunctionType
ALU = mybir.AluOpType
AX = mybir.AxisListType.X

T = 2048
D = 4096
KC = 32
W = 1024
WC = 8
QC = 24
NB = 16
H = 16
DH = 64
NE = 3
VIS = 614
XNW = T + 4
NCORE = 8


class SemC:
    def __init__(self, h):
        self.h = h
        self.n = 0

    def inc(self, k=1):
        self.n += k
        return self.n

    def tick(self, k=1):
        self.n += k
        return k


def build_nc(upto=7, dbg=(), ng=None):
    dbg = set(dbg)
    nc = bass.Bass("TRN2", target_bir_lowering=False, debug=False)

    def din(name, shape, dtype=F32):
        return nc.dram_tensor(name, list(shape), dtype, kind="ExternalInput").ap()

    x_tm = din("x_tm", (T, D))
    x_fm = din("x_fm", (D, T))
    conv_wt = [din(f"conv_w{i}", (D, 3 * W)) for i in range(3)]
    conv_bt = din("conv_bt", (128, QC))
    wq_f = din("wq_f", (W, W))
    wk_f = din("wk_f", (W, W))
    wv_f = din("wv_f", (W, W))
    wo_f = din("wo_f", (W, W))
    bqt = din("bqt", (128, WC))
    bkt = din("bkt", (128, WC))
    bot = din("bot", (128, WC))
    bv_row = din("bv_row", (1, W))
    proj_w = din("proj_w", (W, D))
    w1_e = [din(f"w1_{e}", (D, D)) for e in range(NE)]
    w2_e = [din(f"w2_{e}", (D, D)) for e in range(NE)]
    b1t = din("b1t", (128, NE * KC))
    gwa_t = din("gwa_t", (128, KC * NE))
    gwb_col = din("gwb_col", (NE, 1))
    gate_b_col = din("gate_b_col", (NE, 1))
    Bmat = din("Bmat", (4, D))
    eye = din("eye", (128, 128))
    mask4_d = din("mask4", (128, 512))
    ones_col = din("ones_col", (128, 1))
    ones_row = din("ones_row", (1, 128))

    out = nc.dram_tensor("out", [T, D], F32, kind="ExternalOutput").ap()

    def scratch(name, shape, dtype):
        kind = "ExternalOutput" if name in dbg else "Internal"
        return nc.dram_tensor(name, list(shape), dtype, kind=kind).ap()

    qkv_hbm = scratch("qkv_hbm", (3 * W, T), BF16)
    act_hbm = scratch("act_hbm", (NE * D, T), BF16)
    o2_hbm = scratch("o2_hbm", (W, T), BF16)
    cx_hbm = scratch("cx_hbm", (8, T), F32)
    dbg_small = None
    if "small" in dbg:
        dbg_small = nc.dram_tensor("dbg_small", [128, 4096], F32, kind="ExternalOutput").ap()
    dbg_xn = None
    if "xn" in dbg:
        dbg_xn = nc.dram_tensor("dbg_xn", [128, KC * XNW], BF16, kind="ExternalOutput").ap()
    dbg_qh = None
    if "qh" in dbg:
        dbg_qh = nc.dram_tensor("dbg_qh", [128, 3 * WC * T], BF16, kind="ExternalOutput").ap()
    dbg_o = None
    if "o" in dbg:
        dbg_o = nc.dram_tensor("dbg_o", [128, WC * T], BF16, kind="ExternalOutput").ap()

    es = ExitStack()
    with es:
        wl = SemC(es.enter_context(nc.semaphore("wl")))
        ll = SemC(es.enter_context(nc.semaphore("ll")))
        sl = SemC(es.enter_context(nc.semaphore("sl")))
        pe = SemC(es.enter_context(nc.semaphore("pe")))
        ac = SemC(es.enter_context(nc.semaphore("ac")))
        dv = SemC(es.enter_context(nc.semaphore("dv")))
        so = SemC(es.enter_context(nc.semaphore("so")))
        all_sems = [wl, ll, sl, pe, ac, dv, so]

        def snapshot():
            return [(s_.h, s_.n) for s_ in all_sems]

        def vbar(v, inst):
            inst.then_inc(dv.h, dv.tick())
            v.wait_ge(dv.h, dv.n)
            return inst

        def wait_snap(eng, snap):
            for h_, n_ in snap:
                if n_ > 0:
                    eng.wait_ge(h_, n_)

        sbuf = lambda name, shape, dt_: es.enter_context(nc.sbuf_tensor(name, list(shape), dt_))

        eye_f = sbuf("eye_f", (128, 128), F32)
        eye_b = sbuf("eye_b", (128, 128), BF16)
        onec_f = sbuf("onec_f", (128, 1), F32)
        onec_b = sbuf("onec_b", (128, 1), BF16)
        oner_f = sbuf("oner_f", (1, 128), F32)
        oner_b = sbuf("oner_b", (1, 128), BF16)
        conv_bt_sb = sbuf("conv_bt_sb", (128, QC), F32)
        bqs_sb = sbuf("bqs_sb", (128, WC), F32)
        bkt_sb = sbuf("bkt_sb", (128, WC), F32)
        bot_sb = sbuf("bot_sb", (128, WC), F32)
        b1t_sb = sbuf("b1t_sb", (128, NE * KC), F32)
        gwb_sb = sbuf("gwb_sb", (NE, 1), F32)
        gb_sb = sbuf("gb_sb", (NE, 1), F32)
        xn_stack = ExitStack()
        xn = xn_stack.enter_context(nc.sbuf_tensor("xn", [128, KC * XNW], BF16))

        with nc.sbuf_tensor("bqt_tmp", [128, WC], F32) as bqt_tmp, \
             nc.Block("p0") as blk:

            @blk.gpsimd
            def _(g):
                g.dma_start(out=eye_b[:], in_=eye[:, :]).then_inc(wl.h, wl.tick(16))
                g.dma_start(out=onec_b[:], in_=ones_col[:, :]).then_inc(wl.h, wl.tick(16))
                g.dma_start(out=oner_b[:], in_=ones_row[:, :]).then_inc(wl.h, wl.tick(16))

            @blk.sync
            def _(s):
                for dst, src in [(eye_f, eye), (onec_f, ones_col), (oner_f, ones_row),
                                 (conv_bt_sb, conv_bt), (bqt_tmp, bqt), (bkt_sb, bkt),
                                 (bot_sb, bot), (b1t_sb, b1t), (gwb_sb, gwb_col),
                                 (gb_sb, gate_b_col)]:
                    s.dma_start(out=dst[:], in_=src[:, :]).then_inc(ll.h, ll.tick(16))

            @blk.vector
            def _(v):
                v.wait_ge(ll.h, ll.n)
                v.tensor_scalar(bqs_sb[:], bqt_tmp[:], 0.125, None, op0=ALU.mult)
                last = None
                for kc in range(KC):
                    v.memset(xn[:, kc * XNW: kc * XNW + 2], 0.0)
                    last = v.memset(xn[:, kc * XNW + 2 + T: (kc + 1) * XNW], 0.0)
                last.then_inc(dv.h, dv.tick())

        p1 = ExitStack()
        with p1:
            p1sb = lambda name, shape, dt_: p1.enter_context(
                nc.sbuf_tensor(name, list(shape), dt_))
            logits = p1sb("logits", (NE, T), F32)
            macc = p1sb("macc", (128, 4), F32)
            mrow = p1sb("mrow", (128, 8), F32)
            p1w = ExitStack()
            p1wsb = lambda name, shape, dt_: p1w.enter_context(
                nc.sbuf_tensor(name, list(shape), dt_))
            bc_a = p1wsb("bc_a", (128, T), F32)
            bc_b = p1wsb("bc_b", (128, T), F32)
            rows = p1wsb("rows", (1, 3 * T), F32)
            gwa_sb = p1wsb("gwa_sb", (128, KC * NE), F32)
            p1a_bufs = ExitStack()
            xb = [p1a_bufs.enter_context(nc.sbuf_tensor(f"xb{i}", [128, T], BF16))
                  for i in range(2)]
            sq = [p1a_bufs.enter_context(nc.sbuf_tensor(f"sq{i}", [128, T], BF16))
                  for i in range(2)]

            wl0, pe0, ac0 = wl.n, pe.n, ac.n
            wl_xb = [wl0 + 16 * (k + 1) for k in range(KC)]
            pe_st = [pe0 + (k + 1) for k in range(KC)]
            ac_sq = [ac0 + (k + 1) for k in range(KC)]

            snap1a = snapshot()
            with nc.psum_tensor("ps_sums", [1, T], F32) as ps_sums, \
                 nc.psum_tensor("ps_sumsq", [1, T], F32) as ps_sumsq, \
                 nc.Block("p1a") as blk:

                @blk.gpsimd
                def _(g):
                    wait_snap(g, snap1a)
                    for kc in range(KC):
                        b = kc % 2
                        if kc >= 2:
                            g.wait_ge(pe.h, pe_st[kc - 2])
                        g.dma_start(out=xb[b][:], in_=x_fm[kc * 128:(kc + 1) * 128, :])\
                            .then_inc(wl.h, wl.tick(16))
                assert wl.n == wl_xb[-1]

                @blk.sync
                def _(s):
                    s.dma_start(out=gwa_sb[:], in_=gwa_t[:, :]).then_inc(ll.h, ll.tick(16))

                @blk.scalar
                def _(sc):
                    for kc in range(KC):
                        b = kc % 2
                        sc.wait_ge(wl.h, wl_xb[kc])
                        if kc >= 2:
                            sc.wait_ge(pe.h, pe_st[kc - 2])
                        sc.activation(sq[b][:], xb[b][:], AF.Square)\
                            .then_inc(ac.h, ac.tick())
                assert ac.n == ac_sq[-1]

                @blk.tensor
                def _(t):
                    for kc in range(KC):
                        b = kc % 2
                        t.wait_ge(wl.h, wl_xb[kc])
                        for tg in range(4):
                            t.matmul(ps_sums[0:1, tg * 512:(tg + 1) * 512],
                                     onec_b[:, 0:1], xb[b][:, tg * 512:(tg + 1) * 512],
                                     start=(kc == 0), stop=(kc == KC - 1))
                        t.wait_ge(ac.h, ac_sq[kc])
                        mm = None
                        for tg in range(4):
                            mm = t.matmul(ps_sumsq[0:1, tg * 512:(tg + 1) * 512],
                                          onec_b[:, 0:1], sq[b][:, tg * 512:(tg + 1) * 512],
                                          start=(kc == 0), stop=(kc == KC - 1))
                        mm.then_inc(pe.h, pe.tick())
                assert pe.n == pe_st[-1]

                @blk.vector
                def _(v):
                    v.memset(macc[:], 0.0).then_inc(dv.h, dv.tick())

                @blk.scalar
                def _(sc):
                    sc.wait_ge(pe.h, pe_st[-1])
                    sc.mul(rows[0:1, 0:T], ps_sums[0:1, :], 1.0 / D)
                    sc.mul(rows[0:1, T:2 * T], ps_sumsq[0:1, :], 1.0 / D)\
                        .then_inc(ac.h, ac.tick())
            ac_stats = ac.n

            with nc.psum_tensor("ps_bca", [128, T], F32) as ps_bca, \
                 nc.psum_tensor("ps_bcb", [128, T], F32) as ps_bcb, \
                 nc.Block("p1a2") as blk:

                @blk.vector
                def _(v):
                    v.wait_ge(ac.h, ac_stats)
                    vbar(v, v.tensor_tensor(rows[0:1, 2 * T:3 * T], rows[0:1, 0:T],
                                            rows[0:1, 0:T], op=ALU.mult))
                    vbar(v, v.tensor_tensor(rows[0:1, T:2 * T], rows[0:1, T:2 * T],
                                            rows[0:1, 2 * T:3 * T], op=ALU.subtract))
                    vbar(v, v.tensor_scalar(rows[0:1, T:2 * T], rows[0:1, T:2 * T],
                                            1e-5, None, op0=ALU.add))
                dv_var = dv.n

                @blk.scalar
                def _(sc):
                    sc.wait_ge(dv.h, dv_var)
                    sc.activation(rows[0:1, 2 * T:3 * T], rows[0:1, T:2 * T],
                                  AF.Sqrt).then_inc(ac.h, ac.tick())
                ac_sd = ac.n

                @blk.vector
                def _(v):
                    v.wait_ge(ac.h, ac_sd)
                    vbar(v, v.reciprocal(rows[0:1, T:2 * T], rows[0:1, 2 * T:3 * T]))
                    vbar(v, v.scalar_tensor_tensor(rows[0:1, 2 * T:3 * T], rows[0:1, 0:T],
                                                   -1.0, rows[0:1, T:2 * T],
                                                   op0=ALU.mult, op1=ALU.mult))
                dv_rstd = dv.n

                @blk.tensor
                def _(t):
                    t.wait_ge(dv.h, dv_rstd)
                    t.wait_ge(ac.h, ac_stats)
                    for tg in range(4):
                        t.matmul(ps_bca[:, tg * 512:(tg + 1) * 512], oner_f[0:1, :],
                                 rows[0:1, T + tg * 512: T + (tg + 1) * 512],
                                 start=True, stop=True)
                    mm = None
                    for tg in range(4):
                        mm = t.matmul(ps_bcb[:, tg * 512:(tg + 1) * 512], oner_f[0:1, :],
                                      rows[0:1, 2 * T + tg * 512: 2 * T + (tg + 1) * 512],
                                      start=True, stop=True)
                    mm.then_inc(pe.h, pe.tick())
                pe_bc = pe.n

                @blk.vector
                def _(v):
                    v.wait_ge(pe.h, pe_bc)
                    v.tensor_copy(bc_a[:], ps_bca[:])
                    v.tensor_copy(bc_b[:], ps_bcb[:]).then_inc(dv.h, dv.tick())
            dv_bc = dv.n

            p1a_bufs.close()
            xf = [p1w.enter_context(nc.sbuf_tensor(f"xf{i}", [128, T], F32))
                  for i in range(2)]
            xnf0 = p1w.enter_context(nc.sbuf_tensor("xnf0", [128, T], F32))
            xnf = [xnf0, xnf0]
            ll0, dv0, ac0, pe0 = ll.n, dv.n, ac.n, pe.n
            ll_xf = [ll0 + 16 * (k + 1) for k in range(KC)]
            dv_tt1 = [dv0 + 4 * k + 1 for k in range(KC)]
            dv_tt2 = [dv0 + 4 * k + 2 for k in range(KC)]
            dv_mod = [dv0 + 4 * k + 4 for k in range(KC)]
            ac_cp = [ac0 + (k + 1) for k in range(KC)]
            pe_gate = [pe0 + (k + 1) for k in range(KC)]

            with nc.psum_tensor("ps_gate", [NE, T], F32) as ps_gate, \
                 nc.psum_tensor("ps_mv", [1, 1], F32) as ps_mv, \
                 nc.psum_tensor("ps_mt", [1, 1], F32) as ps_mt, \
                 nc.psum_tensor("ps_r3", [NE, 1], F32) as ps_r3, \
                 nc.Block("p1b") as blk:

                @blk.sync
                def _(s):
                    s.wait_ge(pe.h, pe_bc)
                    for kc in range(KC):
                        b = kc % 2
                        if kc >= 2:
                            s.wait_ge(dv.h, dv_tt1[kc - 2])
                        s.dma_start(out=xf[b][:], in_=x_fm[kc * 128:(kc + 1) * 128, :])\
                            .then_inc(ll.h, ll.tick(16))
                assert ll.n == ll_xf[-1]

                @blk.vector
                def _(v):
                    v.wait_ge(dv.h, dv_bc)
                    for kc in range(KC):
                        b = kc % 2
                        v.wait_ge(ll.h, ll_xf[kc])
                        if kc >= 1:
                            v.wait_ge(ac.h, ac_cp[kc - 1])
                            v.wait_ge(pe.h, pe_gate[kc - 1])
                        vbar(v, v.tensor_tensor(xnf[b][:], xf[b][:], bc_a[:],
                                                op=ALU.mult))
                        vbar(v, v.tensor_tensor(xnf[b][:], xnf[b][:], bc_b[:],
                                                op=ALU.add))
                        v.tensor_reduce(mrow[:, 2:3], xnf[b][:, 0:VIS], axis=AX,
                                        op=ALU.add, apply_absolute_value=True)
                        vbar(v, v.tensor_reduce(mrow[:, 3:4], xnf[b][:, VIS:T], axis=AX,
                                                op=ALU.add, apply_absolute_value=True))
                        v.tensor_tensor(macc[:, 0:1], macc[:, 0:1], mrow[:, 2:3], op=ALU.add)
                        v.tensor_tensor(macc[:, 1:2], macc[:, 1:2], mrow[:, 3:4], op=ALU.add)\
                            .then_inc(dv.h, dv.tick())
                assert dv.n == dv_mod[-1]

                @blk.scalar
                def _(sc):
                    for kc in range(KC):
                        b = kc % 2
                        sc.wait_ge(dv.h, dv_tt2[kc])
                        sc.copy(xn[:, kc * XNW + 2: kc * XNW + 2 + T], xnf[b][:])\
                            .then_inc(ac.h, ac.tick())
                assert ac.n == ac_cp[-1]

                @blk.tensor
                def _(t):
                    for kc in range(KC):
                        b = kc % 2
                        t.wait_ge(dv.h, dv_tt2[kc])
                        mm = None
                        for tg in range(4):
                            mm = t.matmul(ps_gate[0:NE, tg * 512:(tg + 1) * 512],
                                          gwa_sb[:, kc * NE:(kc + 1) * NE],
                                          xnf[b][:, tg * 512:(tg + 1) * 512],
                                          start=(kc == 0), stop=(kc == KC - 1))
                        mm.then_inc(pe.h, pe.tick())
                assert pe.n == pe_gate[-1]

                @blk.tensor
                def _(t):
                    t.wait_ge(dv.h, dv_mod[-1])
                    t.matmul(ps_mv[0:1, 0:1], macc[:, 0:1], onec_f[:, 0:1],
                             start=True, stop=True)
                    t.matmul(ps_mt[0:1, 0:1], macc[:, 1:2], onec_f[:, 0:1],
                             start=True, stop=True).then_inc(pe.h, pe.tick())
                pe_mtot = pe.n

                @blk.scalar
                def _(sc):
                    sc.wait_ge(pe.h, pe_mtot)
                    sc.mul(mrow[0:1, 4:5], ps_mt[0:1, 0:1], 1.0 / ((T - VIS) * D))
                    sc.mul(mrow[0:1, 5:6], ps_mv[0:1, 0:1], 1.0 / (VIS * D))\
                        .then_inc(ac.h, ac.tick())
                ac_mm = ac.n

                @blk.vector
                def _(v):
                    v.wait_ge(ac.h, ac_mm)
                    v.tensor_scalar(mrow[0:1, 6:7], mrow[0:1, 5:6], 1e-8, None,
                                    op0=ALU.add).then_inc(dv.h, dv.tick())
                    v.wait_ge(dv.h, dv.n)
                    v.reciprocal(mrow[0:1, 7:8], mrow[0:1, 6:7]).then_inc(dv.h, dv.tick())
                    v.wait_ge(dv.h, dv.n)
                    v.tensor_tensor(mrow[0:1, 0:1], mrow[0:1, 4:5], mrow[0:1, 7:8],
                                    op=ALU.mult).then_inc(dv.h, dv.tick())
                dv_ratio = dv.n

                @blk.tensor
                def _(t):
                    t.wait_ge(dv.h, dv_ratio)
                    t.matmul(ps_r3[0:NE, 0:1], oner_f[0:1, 0:NE], mrow[0:1, 0:1],
                             start=True, stop=True).then_inc(pe.h, pe.tick())
                pe_r3 = pe.n

                @blk.vector
                def _(v):
                    v.wait_ge(pe.h, pe_r3)
                    v.tensor_copy(mrow[0:NE, 1:2], ps_r3[0:NE, 0:1])\
                        .then_inc(dv.h, dv.tick())
                    v.wait_ge(dv.h, dv.n)
                    v.scalar_tensor_tensor(mrow[0:NE, 2:3], gwb_sb[0:NE, 0:1],
                                           mrow[0:NE, 1:2], gb_sb[0:NE, 0:1],
                                           op0=ALU.mult, op1=ALU.add)\
                        .then_inc(dv.h, dv.tick())
                dv_b3 = dv.n

                @blk.vector
                def _(v):
                    v.wait_ge(dv.h, dv_b3)
                    v.wait_ge(pe.h, pe_gate[-1])
                    v.tensor_scalar(logits[0:NE, :], ps_gate[0:NE, :], mrow[0:NE, 2:3],
                                    None, op0=ALU.add).then_inc(dv.h, dv.tick())
                dv_logits = dv.n

                if dbg_small is not None:
                    @blk.sync
                    def _(s):
                        s.wait_ge(ac.h, ac.n)
                        s.wait_ge(dv.h, dv.n)
                        s.dma_start(out=dbg_small[0:NE, 0:T], in_=logits[:])\
                            .then_inc(so.h, so.tick(16))
                        s.dma_start(out=dbg_small[8:9, 0:8], in_=mrow[0:1, :])\
                            .then_inc(so.h, so.tick(16))
                        s.dma_start(out=dbg_small[16:17, 0:T], in_=bc_a[0:1, :])\
                            .then_inc(so.h, so.tick(16))
                        s.dma_start(out=dbg_small[17:18, 0:T], in_=bc_b[0:1, :])\
                            .then_inc(so.h, so.tick(16))

                if dbg_xn is not None:
                    @blk.sync
                    def _(s):
                        s.wait_ge(ac.h, ac_cp[-1])
                        s.dma_start(out=dbg_xn[:, :], in_=xn[:]).then_inc(so.h, so.tick(16))

            p1w.close()
            p2 = ExitStack()
            with p2:
                p2sb = lambda n_, s_, d_: p2.enter_context(nc.sbuf_tensor(n_, list(s_), d_))
                lg_tm = p2sb("lg_tm", (128, 64), F32)
                sc_tm = p2sb("sc_tm", (128, 64), F32)
                v01 = p2sb("v01", (128, 32), F32)
                ptmp = p2sb("ptmp", (128, 16), F32)
                es_t = p2sb("es_t", (128, 4), F32)
                oh = p2sb("oh", (2, 16), F32)
                m2_sb = p2sb("m2_sb", (2, 4), F32)
                vrows = p2sb("vrows", (2, T), F32)
                cxt = p2sb("cxt", (4, T), F32)
                csb2 = p2sb("csb2", (NE, T), F32)

                with nc.psum_tensor("ps_ta", [128, 4], F32) as ps_ta, \
                     nc.psum_tensor("ps_tb", [128, 4], F32) as ps_tb, \
                     nc.psum_tensor("ps_va", [2, 128], F32) as ps_va, \
                     nc.psum_tensor("ps_vb", [2, 128], F32) as ps_vb, \
                     nc.psum_tensor("ps_c", [NE, T], F32) as ps_c, \
                     nc.Block("p2") as blk:

                    pe0, ac0 = pe.n, ac.n
                    pe_ltr = [pe0 + (ti + 1) for ti in range(NB)]
                    ac_lev = [ac0 + (ti + 1) for ti in range(NB)]

                    @blk.tensor
                    def _(t):
                        t.wait_ge(dv.h, dv_logits)
                        for ti in range(NB):
                            ps = [ps_ta, ps_tb][ti % 2]
                            if ti >= 2:
                                t.wait_ge(ac.h, ac_lev[ti - 2])
                            t.transpose(ps[:, 0:NE], logits[0:NE, ti * 128:(ti + 1) * 128],
                                        eye_f[0:NE, 0:NE]).then_inc(pe.h, pe.tick())
                    assert pe.n == pe_ltr[-1]

                    @blk.scalar
                    def _(sc):
                        for ti in range(NB):
                            ps = [ps_ta, ps_tb][ti % 2]
                            sc.wait_ge(pe.h, pe_ltr[ti])
                            sc.copy(lg_tm[:, ti * 4: ti * 4 + NE], ps[:, 0:NE])\
                                .then_inc(ac.h, ac.tick())
                    assert ac.n == ac_lev[-1]

                    dv1 = dv.n
                    dv_nm = [dv1 + 11 * ti + 1 for ti in range(NB)]
                    dv_tl = [dv1 + 11 * ti + 11 for ti in range(NB)]
                    ac1 = ac.n
                    ac_exp = [ac1 + (ti + 1) for ti in range(NB)]

                    @blk.vector
                    def _(v):
                        for ti in range(NB):
                            lg = lg_tm[:, ti * 4: ti * 4 + NE]
                            v.wait_ge(ac.h, ac_lev[ti])
                            v.reduce_max(ptmp[:, 0:1], lg, axis=AX, negate=True)\
                                .then_inc(dv.h, dv.tick())
                            v.wait_ge(ac.h, ac_exp[ti])
                            vbar(v, v.reduce_sum(ptmp[:, 1:2], es_t[:, 0:NE], axis=AX))
                            vbar(v, v.reciprocal(ptmp[:, 2:3], ptmp[:, 1:2]))
                            vbar(v, v.tensor_scalar(sc_tm[:, ti * 4: ti * 4 + NE],
                                                    es_t[:, 0:NE], ptmp[:, 2:3], None,
                                                    op0=ALU.mult))
                            s3 = sc_tm[:, ti * 4: ti * 4 + NE]
                            v.reduce_max(ptmp[:, 3:4], s3, axis=AX)
                            v.tensor_reduce(ptmp[:, 4:5], s3, axis=AX, op=ALU.min)
                            vbar(v, v.reduce_sum(ptmp[:, 5:6], s3, axis=AX))
                            vbar(v, v.tensor_tensor(ptmp[:, 6:7], ptmp[:, 5:6],
                                                    ptmp[:, 3:4], op=ALU.subtract))
                            vbar(v, v.tensor_tensor(ptmp[:, 6:7], ptmp[:, 6:7],
                                                    ptmp[:, 4:5], op=ALU.subtract))
                            vbar(v, v.tensor_tensor(ptmp[:, 7:8], ptmp[:, 3:4],
                                                    ptmp[:, 6:7], op=ALU.add))
                            vbar(v, v.tensor_scalar(ptmp[:, 7:8], ptmp[:, 7:8], 1e-8,
                                                    None, op0=ALU.add))
                            vbar(v, v.reciprocal(ptmp[:, 8:9], ptmp[:, 7:8]))
                            v.tensor_tensor(v01[:, 2 * ti:2 * ti + 1], ptmp[:, 3:4],
                                            ptmp[:, 8:9], op=ALU.mult)
                            v.tensor_tensor(v01[:, 2 * ti + 1:2 * ti + 2], ptmp[:, 6:7],
                                            ptmp[:, 8:9], op=ALU.mult)\
                                .then_inc(dv.h, dv.tick())
                    assert dv.n == dv_tl[-1]

                    @blk.scalar
                    def _(sc):
                        for ti in range(NB):
                            lg = lg_tm[:, ti * 4: ti * 4 + NE]
                            sc.wait_ge(dv.h, dv_nm[ti])
                            if ti >= 1:
                                sc.wait_ge(dv.h, dv_tl[ti - 1])
                            sc.activation(es_t[:, 0:NE], lg, AF.Exp,
                                          bias=ptmp[:, 0:1]).then_inc(ac.h, ac.tick())
                    assert ac.n == ac_exp[-1]

                    @blk.vector
                    def _(v):
                        v.wait_ge(dv.h, dv_tl[NB - 1])
                        r0 = sc_tm[0:1, 0:NE]
                        vbar(v, v.reduce_max(ptmp[0:1, 9:10], r0, axis=AX))
                        vbar(v, v.tensor_reduce(ptmp[0:1, 10:11], r0, axis=AX,
                                                op=ALU.min))
                        vbar(v, v.tensor_scalar(oh[0:1, 0:NE], r0, ptmp[0:1, 9:10], None,
                                                op0=ALU.is_equal))
                        vbar(v, v.tensor_scalar(oh[0:1, 4:4 + NE], r0, ptmp[0:1, 10:11],
                                                None, op0=ALU.is_equal))
                        vbar(v, v.tensor_tensor(oh[0:1, 8:8 + NE], oh[0:1, 0:NE],
                                                oh[0:1, 4:4 + NE], op=ALU.add))
                        vbar(v, v.tensor_scalar(oh[0:1, 8:8 + NE], oh[0:1, 8:8 + NE],
                                                1.0, -1.0, op0=ALU.subtract,
                                                op1=ALU.mult))
                    dv_oh = dv.n

                    @blk.sync
                    def _(s):
                        s.wait_ge(dv.h, dv_oh)
                        s.dma_start(out=m2_sb[0:1, 0:NE], in_=oh[0:1, 0:NE])\
                            .then_inc(ll.h, ll.tick(16))
                        s.dma_start(out=m2_sb[1:2, 0:NE], in_=oh[0:1, 8:8 + NE])\
                            .then_inc(ll.h, ll.tick(16))
                    ll_m2 = ll.n

                    pe1 = pe.n
                    pe_vtr = [pe1 + (ti + 1) for ti in range(NB)]
                    ac2 = ac.n
                    ac_vev = [ac2 + (ti + 1) for ti in range(NB)]

                    @blk.tensor
                    def _(t):
                        for ti in range(NB):
                            ps = [ps_va, ps_vb][ti % 2]
                            t.wait_ge(dv.h, dv_tl[ti])
                            if ti >= 2:
                                t.wait_ge(ac.h, ac_vev[ti - 2])
                            t.transpose(ps[0:2, :], v01[:, 2 * ti:2 * ti + 2],
                                        eye_f[:, :]).then_inc(pe.h, pe.tick())
                    assert pe.n == pe_vtr[-1]

                    @blk.scalar
                    def _(sc):
                        for ti in range(NB):
                            ps = [ps_va, ps_vb][ti % 2]
                            sc.wait_ge(pe.h, pe_vtr[ti])
                            sc.copy(vrows[0:2, ti * 128:(ti + 1) * 128], ps[0:2, :])\
                                .then_inc(ac.h, ac.tick())
                    assert ac.n == ac_vev[-1]

                    @blk.tensor
                    def _(t):
                        t.wait_ge(ll.h, ll_m2)
                        t.wait_ge(ac.h, ac_vev[-1])
                        mm = None
                        for tg in range(4):
                            mm = t.matmul(ps_c[0:NE, tg * 512:(tg + 1) * 512],
                                          m2_sb[0:2, 0:NE],
                                          vrows[0:2, tg * 512:(tg + 1) * 512],
                                          start=True, stop=True)
                        mm.then_inc(pe.h, pe.tick())
                    pe_c = pe.n

                    @blk.vector
                    def _(v):
                        v.wait_ge(pe.h, pe_c)
                        v.tensor_copy(csb2[0:NE, :], ps_c[0:NE, :])
                        v.memset(cxt[0:1, :], 1.0).then_inc(dv.h, dv.tick())
                    dv_c = dv.n

                    @blk.sync
                    def _(s):
                        s.wait_ge(dv.h, dv_c)
                        s.dma_start(out=cxt[1:4, :], in_=csb2[0:NE, :])\
                            .then_inc(ll.h, ll.tick(16))
                        s.wait_ge(ll.h, ll.n)
                        s.dma_start(out=cx_hbm[0:4, :], in_=cxt[0:4, :])\
                            .then_inc(so.h, so.tick(16))
                        s.dma_start(out=cx_hbm[4:7, :], in_=csb2[0:NE, :])\
                            .then_inc(so.h, so.tick(16))
                        if dbg_small is not None:
                            s.dma_start(out=dbg_small[24:28, 0:T], in_=cxt[0:4, :])\
                                .then_inc(so.h, so.tick(16))
                            s.dma_start(out=dbg_small[32:34, 0:16], in_=oh[0:2, :])\
                                .then_inc(so.h, so.tick(16))
                            s.dma_start(out=dbg_small[36:38, 0:T], in_=vrows[0:2, :])\
                                .then_inc(so.h, so.tick(16))
                            s.dma_start(out=dbg_small[40:41, 0:64], in_=sc_tm[0:1, :])\
                                .then_inc(so.h, so.tick(16))

        if upto <= 2:
            xn_stack.close()
            _final_wait(nc, so)
            return nc

        p3 = ExitStack()
        with p3:
            csb_row = p3.enter_context(nc.sbuf_tensor("csb_row", [1, NE * T], F32))
            w1g = [p3.enter_context(nc.sbuf_tensor(f"w1g{i}", [128, KC * 128], BF16))
                   for i in range(2)]
            cbc = p3.enter_context(nc.sbuf_tensor("cbc", [128, T], F32))
            stg = [p3.enter_context(nc.sbuf_tensor(f"stg{i}", [128, T], F32))
                   for i in range(2)]
            abf = [p3.enter_context(nc.sbuf_tensor(f"abf{i}", [128, T], BF16))
                   for i in range(2)]
            FUNC_E = [AF.Tanh, AF.Gelu, AF.Silu]

            snap3 = snapshot()
            with nc.psum_tensor("ps_f0", [128, T], F32) as ps_f0, \
                 nc.psum_tensor("ps_f1", [128, T], F32) as ps_f1, \
                 nc.Block("p3") as blk:
                ps_f = [ps_f0, ps_f1]

                wl0, pe0, ac0, dv0, so0 = wl.n, pe.n, ac.n, dv.n, so.n
                ll_csb = ll.n + 48
                wl_w1, pe_ch, pe_cbc, dv_cbc, ac_ev, dv_tt, so_st = [], [], [], [], [], [], []
                wn, pn, an, dn, sn = wl0, pe0, ac0, dv0, so0
                for e in range(NE):
                    pn += 1
                    pe_cbc.append(pn)
                    dn += 1
                    dv_cbc.append(dn)
                    for fc in range(KC):
                        wn += 32
                        wl_w1.append(wn)
                        pn += 1
                        pe_ch.append(pn)
                        an += 1
                        ac_ev.append(an)
                        dn += 1
                        dv_tt.append(dn)
                        sn += 16
                        so_st.append(sn)

                @blk.gpsimd
                def _(g):
                    wait_snap(g, snap3)
                    for e in range(NE):
                        for fc in range(KC):
                            i = e * KC + fc
                            b = i % 2
                            if i >= 2:
                                g.wait_ge(pe.h, pe_ch[i - 2])
                            for hh in range(2):
                                src = w1_e[e][hh * 2048:(hh + 1) * 2048,
                                              fc * 128:(fc + 1) * 128]
                                src = src.rearrange("(kc p) c -> p kc c", p=128)
                                dst = w1g[b][:, hh * 16 * 128:(hh + 1) * 16 * 128]
                                dst = dst.rearrange("p (kc c) -> p kc c", kc=16)
                                g.dma_start(out=dst, in_=src).then_inc(wl.h, wl.tick(16))
                assert wl.n == wl_w1[-1]

                @blk.tensor
                def _(t):
                    wait_snap(t, snap3)
                    t.wait_ge(ll.h, ll_csb)
                    for e in range(NE):
                        if e > 0:
                            t.wait_ge(ac.h, ac_ev[e * KC - 1])
                            t.wait_ge(ac.h, ac_ev[e * KC - 2])
                        mm = None
                        for tg in range(4):
                            mm = t.matmul(ps_f0[:, tg * 512:(tg + 1) * 512], oner_f[0:1, :],
                                          csb_row[0:1, e * T + tg * 512: e * T + (tg + 1) * 512],
                                          start=True, stop=True)
                        mm.then_inc(pe.h, pe.tick())
                        assert pe.n == pe_cbc[e]
                        for fc in range(KC):
                            i = e * KC + fc
                            b = i % 2
                            t.wait_ge(wl.h, wl_w1[i])
                            if fc == 0:
                                t.wait_ge(dv.h, dv_cbc[e])
                            if i >= 2:
                                t.wait_ge(ac.h, ac_ev[i - 2])
                            mm = None
                            for kc in range(KC):
                                for tg in range(4):
                                    mm = t.matmul(
                                        ps_f[b][:, tg * 512:(tg + 1) * 512],
                                        w1g[b][:, kc * 128:(kc + 1) * 128],
                                        xn[:, kc * XNW + 2 + tg * 512:
                                           kc * XNW + 2 + (tg + 1) * 512],
                                        start=(kc == 0), stop=(kc == KC - 1))
                            mm.then_inc(pe.h, pe.tick())
                            assert pe.n == pe_ch[i]

                @blk.vector
                def _(v):
                    for e in range(NE):
                        v.wait_ge(pe.h, pe_cbc[e])
                        v.tensor_copy(cbc[:], ps_f0[:]).then_inc(dv.h, dv.tick())
                        v.wait_ge(dv.h, dv.n)
                        assert dv.n == dv_cbc[e]
                        for fc in range(KC):
                            i = e * KC + fc
                            b = i % 2
                            v.wait_ge(ac.h, ac_ev[i])
                            if i >= 2:
                                v.wait_ge(so.h, so_st[i - 2])
                            v.tensor_tensor(abf[b][:], stg[b][:], cbc[:], op=ALU.mult)\
                                .then_inc(dv.h, dv.tick())
                            assert dv.n == dv_tt[i]

                @blk.scalar
                def _(sc):
                    for e in range(NE):
                        for fc in range(KC):
                            i = e * KC + fc
                            b = i % 2
                            sc.wait_ge(pe.h, pe_ch[i])
                            if i >= 2:
                                sc.wait_ge(dv.h, dv_tt[i - 2])
                            sc.activation(stg[b][:], ps_f[b][:], FUNC_E[e],
                                          bias=b1t_sb[:, e * KC + fc: e * KC + fc + 1])\
                                .then_inc(ac.h, ac.tick())
                            assert ac.n == ac_ev[i]

                @blk.sync
                def _(s):
                    wait_snap(s, snap3)
                    for e in range(NE):
                        s.dma_start(out=csb_row[0:1, e * T:(e + 1) * T],
                                    in_=cx_hbm[4 + e:5 + e, :])\
                            .then_inc(ll.h, ll.tick(16))
                    assert ll.n == ll_csb
                    for e in range(NE):
                        for fc in range(KC):
                            i = e * KC + fc
                            b = i % 2
                            s.wait_ge(dv.h, dv_tt[i])
                            s.dma_start(out=act_hbm[(e * KC + fc) * 128:
                                                    (e * KC + fc + 1) * 128, :],
                                        in_=abf[b][:]).then_inc(so.h, so.tick(16))
                            assert so.n == so_st[i]

        if upto <= 3:
            xn_stack.close()
            _final_wait(nc, so)
            return nc

        p4 = ExitStack()
        with p4:
            cw = [p4.enter_context(nc.sbuf_tensor(f"cw{i}", [128, 3 * KC * 128], BF16))
                  for i in range(2)]
            qbf = [p4.enter_context(nc.sbuf_tensor(f"qbf{i}", [128, T], BF16))
                   for i in range(2)]

            snap4 = snapshot()
            with nc.psum_tensor("ps_c0", [128, T], F32) as ps_c0, \
                 nc.psum_tensor("ps_c1", [128, T], F32) as ps_c1, \
                 nc.Block("p4") as blk:
                ps_f = [ps_c0, ps_c1]

                wl0, pe0, dv0, so0 = wl.n, pe.n, dv.n, so.n
                wl_cw = [wl0 + 96 * (i + 1) for i in range(QC)]
                pe_ch4 = [pe0 + (i + 1) for i in range(QC)]
                dv_ev4 = [dv0 + (i + 1) for i in range(QC)]
                so_st4 = [so0 + 16 * (i + 1) for i in range(QC)]

                @blk.gpsimd
                def _(g):
                    wait_snap(g, snap4)
                    for fc in range(QC):
                        b = fc % 2
                        if fc >= 2:
                            g.wait_ge(pe.h, pe_ch4[fc - 2])
                        for tap in range(3):
                            for hh in range(2):
                                src = conv_wt[tap][hh * 2048:(hh + 1) * 2048,
                                                   fc * 128:(fc + 1) * 128]
                                src = src.rearrange("(kc p) c -> p kc c", p=128)
                                dst = cw[b][:, (tap * KC + hh * 16) * 128:
                                            (tap * KC + hh * 16 + 16) * 128]
                                dst = dst.rearrange("p (kc c) -> p kc c", kc=16)
                                g.dma_start(out=dst, in_=src).then_inc(wl.h, wl.tick(16))
                assert wl.n == wl_cw[-1]

                @blk.tensor
                def _(t):
                    wait_snap(t, snap4)
                    for fc in range(QC):
                        b = fc % 2
                        t.wait_ge(wl.h, wl_cw[fc])
                        if fc >= 2:
                            t.wait_ge(dv.h, dv_ev4[fc - 2])
                        mm = None
                        for kc in range(KC):
                            for tap in range(3):
                                for tg in range(4):
                                    mm = t.matmul(
                                        ps_f[b][:, tg * 512:(tg + 1) * 512],
                                        cw[b][:, (tap * KC + kc) * 128:
                                              (tap * KC + kc + 1) * 128],
                                        xn[:, kc * XNW + 1 + tap + tg * 512:
                                           kc * XNW + 1 + tap + (tg + 1) * 512],
                                        start=(kc == 0 and tap == 0),
                                        stop=(kc == KC - 1 and tap == 2))
                        mm.then_inc(pe.h, pe.tick())
                assert pe.n == pe_ch4[-1]

                @blk.vector
                def _(v):
                    for fc in range(QC):
                        b = fc % 2
                        v.wait_ge(pe.h, pe_ch4[fc])
                        if fc >= 2:
                            v.wait_ge(so.h, so_st4[fc - 2])
                        v.tensor_scalar(qbf[b][:], ps_f[b][:], conv_bt_sb[:, fc:fc + 1],
                                        None, op0=ALU.add).then_inc(dv.h, dv.tick())
                assert dv.n == dv_ev4[-1]

                @blk.sync
                def _(s):
                    for fc in range(QC):
                        b = fc % 2
                        s.wait_ge(dv.h, dv_ev4[fc])
                        s.dma_start(out=qkv_hbm[fc * 128:(fc + 1) * 128, :],
                                    in_=qbf[b][:]).then_inc(so.h, so.tick(16))
                assert so.n == so_st4[-1]

        if upto <= 4:
            xn_stack.close()
            _final_wait(nc, so)
            return nc

        xn_stack.close()
        p5 = ExitStack()
        with p5:
            p5sb = lambda n_, s_, d_: p5.enter_context(nc.sbuf_tensor(n_, list(s_), d_))
            src_sb = p5sb("src_sb", (128, WC * T), BF16)
            qh_sb = p5sb("qh_sb", (128, WC * T), BF16)
            kh_sb = p5sb("kh_sb", (128, WC * T), BF16)
            vh_sb = p5sb("vh_sb", (128, NB * W), BF16)
            wbuf = [p5sb(f"wbuf{i}", (128, WC * W), BF16) for i in range(2)]
            mask_sb = p5sb("mask_sb", (128, 512), F32)
            bv_b = p5sb("bv_b", (1, W), BF16)
            s_sb = [p5sb(f"s_sb{i}", (128, 512), F32) for i in range(2)]
            e_sb = [p5sb(f"e_sb{i}", (128, 512), F32) for i in range(2)]
            p_sb = [p5sb(f"p_sb{i}", (128, 512), BF16) for i in range(2)]
            pT_sb = [p5sb(f"pT_sb{i}", (128, 512), BF16) for i in range(2)]
            atmp = p5sb("atmp", (128, 12), F32)
            obf = [p5sb(f"obf{i}", (128, T), BF16) for i in range(2)]

            def oslice(h, blki):
                return src_sb[(h % 2) * 64:(h % 2) * 64 + 64,
                              (h // 2) * T + blki * 128:(h // 2) * T + blki * 128 + 128]

            def hslice(hsb, h, blki):
                return hsb[(h % 2) * 64:(h % 2) * 64 + 64,
                           (h // 2) * T + blki * 128:(h // 2) * T + blki * 128 + 128]

            snap5 = snapshot()
            with nc.psum_tensor("ps_q0", [128, T], F32) as ps_q0, \
                 nc.psum_tensor("ps_q1", [128, T], F32) as ps_q1, \
                 nc.Block("p5abc") as blk:
                ps_f = [ps_q0, ps_q1]

                ll0, wl0, pe0 = ll.n, wl.n, pe.n
                ac0b = ac.n
                ll_mask = ll0 + 16
                ll_src = [ll_mask + 16 * 8 * (p + 1) for p in range(3)]
                wl_w = [wl0 + 16 * (p + 2) for p in range(3)]
                dv0b = dv.n
                pe_q = [pe0 + (i + 1) for i in range(WC)]
                pe_k = [pe_q[-1] + (i + 1) for i in range(WC)]
                pe_v = [pe_k[-1] + (i + 1) for i in range(NB)]
                dv_seq = [dv0b + (i + 1) for i in range(2 * WC + NB)]

                @blk.sync
                def _(s):
                    wait_snap(s, snap5)
                    s.dma_start(out=mask_sb[:], in_=mask4_d[:, :]).then_inc(ll.h, ll.tick(16))
                    for part in range(3):
                        if part == 1:
                            s.wait_ge(pe.h, pe_q[-1])
                        if part == 2:
                            s.wait_ge(pe.h, pe_k[-1])
                        for kc in range(WC):
                            s.dma_start(out=src_sb[:, kc * T:(kc + 1) * T],
                                        in_=qkv_hbm[(part * WC + kc) * 128:
                                                    (part * WC + kc + 1) * 128, :])\
                                .then_inc(ll.h, ll.tick(16))
                assert ll.n == ll_src[-1]

                @blk.gpsimd
                def _(g):
                    wait_snap(g, snap5)
                    g.dma_start(out=bv_b[:], in_=bv_row[:, :]).then_inc(wl.h, wl.tick(16))
                    for part, wsrc in enumerate([wq_f, wk_f, wv_f]):
                        b = part % 2
                        if part == 2:
                            g.wait_ge(pe.h, pe_q[-1])
                        src = wsrc[:, :].rearrange("(kc p) c -> p kc c", p=128)
                        dst = wbuf[b][:].rearrange("p (kc c) -> p kc c", kc=WC)
                        g.dma_start(out=dst, in_=src).then_inc(wl.h, wl.tick(16))
                assert wl.n == wl_w[-1]

                @blk.tensor
                def _(t):
                    for part in range(2):
                        t.wait_ge(ll.h, ll_src[part])
                        t.wait_ge(wl.h, wl_w[part])
                        base_pe = [pe_q, pe_k][part]
                        for fc in range(WC):
                            i = part * WC + fc
                            b = i % 2
                            if i >= 2:
                                t.wait_ge(dv.h, dv_seq[i - 2])
                            mm = None
                            for kc in range(WC):
                                for tg in range(4):
                                    mm = t.matmul(
                                        ps_f[b][:, tg * 512:(tg + 1) * 512],
                                        wbuf[part][:, (kc * WC + fc) * 128:
                                                   (kc * WC + fc + 1) * 128],
                                        src_sb[:, kc * T + tg * 512:
                                               kc * T + (tg + 1) * 512],
                                        start=(kc == 0), stop=(kc == WC - 1))
                            mm.then_inc(pe.h, pe.tick())
                            assert pe.n == base_pe[fc]
                    t.wait_ge(ll.h, ll_src[2])
                    t.wait_ge(wl.h, wl_w[2])
                    for blki in range(NB):
                        i = 2 * WC + blki
                        b = i % 2
                        t.wait_ge(dv.h, dv_seq[i - 2])
                        mm = None
                        for nh in range(2):
                            for kc in range(WC):
                                t.matmul(ps_f[b][:, nh * 512:(nh + 1) * 512],
                                         src_sb[:, kc * T + blki * 128:
                                                kc * T + blki * 128 + 128],
                                         wbuf[0][:, kc * W + nh * 512:
                                                 kc * W + (nh + 1) * 512],
                                         start=(kc == 0), stop=False)
                            mm = t.matmul(ps_f[b][:, nh * 512:(nh + 1) * 512],
                                          oner_b[0:1, :],
                                          bv_b[0:1, nh * 512:(nh + 1) * 512],
                                          start=False, stop=True)
                        mm.then_inc(pe.h, pe.tick())
                        assert pe.n == pe_v[blki]

                @blk.vector
                def _(v):
                    for part in range(2):
                        base_pe = [pe_q, pe_k][part]
                        dst = [qh_sb, kh_sb][part]
                        for fc in range(WC):
                            i = part * WC + fc
                            b = i % 2
                            v.wait_ge(pe.h, base_pe[fc])
                            if part == 0:
                                v.tensor_scalar(dst[:, fc * T:(fc + 1) * T], ps_f[b][:],
                                                0.125, bqs_sb[:, fc:fc + 1],
                                                op0=ALU.mult, op1=ALU.add)\
                                    .then_inc(dv.h, dv.tick())
                            else:
                                v.tensor_scalar(dst[:, fc * T:(fc + 1) * T], ps_f[b][:],
                                                bkt_sb[:, fc:fc + 1], None, op0=ALU.add)\
                                    .then_inc(dv.h, dv.tick())
                            assert dv.n == dv_seq[i]
                    for blki in range(NB):
                        i = 2 * WC + blki
                        b = i % 2
                        v.wait_ge(pe.h, pe_v[blki])
                        v.tensor_copy(vh_sb[:, blki * W:(blki + 1) * W], ps_f[b][:, 0:W])\
                            .then_inc(dv.h, dv.tick())
                        assert dv.n == dv_seq[i]

            if dbg_qh is not None:
                with nc.Block("p5dbg") as blk:
                    @blk.sync
                    def _(s):
                        s.wait_ge(dv.h, dv.n)
                        s.dma_start(out=dbg_qh[:, 0:WC * T], in_=qh_sb[:])\
                            .then_inc(so.h, so.tick(16))
                        s.dma_start(out=dbg_qh[:, WC * T:2 * WC * T], in_=kh_sb[:])\
                            .then_inc(so.h, so.tick(16))
                        s.dma_start(out=dbg_qh[:, 2 * WC * T:3 * WC * T], in_=vh_sb[:])\
                            .then_inc(so.h, so.tick(16))

            if upto <= 4.5:
                _final_wait(nc, so)
                return nc

            with nc.psum_tensor("ps_sc0", [128, 128], F32) as ps_sc0, \
                 nc.psum_tensor("ps_sc1", [128, 128], F32) as ps_sc1, \
                 nc.psum_tensor("ps_sc2", [128, 128], F32) as ps_sc2, \
                 nc.psum_tensor("ps_sc3", [128, 128], F32) as ps_sc3, \
                 nc.psum_tensor("ps_p0", [128, 128], BF16) as ps_p0, \
                 nc.psum_tensor("ps_p1", [128, 128], BF16) as ps_p1, \
                 nc.Block("p5d") as blk:
                ps_sc = [ps_sc0, ps_sc1, ps_sc2, ps_sc3]
                ps_p = [ps_p0, ps_p1]

                NG = NB * 4 if ng is None else ng
                pe0b, ac0b, dv0b = pe.n, ac.n, dv.n
                pe_sc = [pe0b + 6 * i + 1 for i in range(NG)]
                pe_tr = [[pe0b + 6 * i + 2 + j for j in range(4)] for i in range(NG)]
                pe_oc = [pe0b + 6 * i + 6 for i in range(NG)]
                dv_mk = [dv0b + 5 * i + 2 for i in range(NG)]
                dv_pr = [dv0b + 5 * i + 5 for i in range(NG)]
                ac_ex = [ac0b + 4 * i + 1 for i in range(NG)]
                ac_t1 = [ac0b + 4 * i + 2 for i in range(NG)]
                ac_t2 = [ac0b + 4 * i + 3 for i in range(NG)]
                ac_oe = [ac0b + 4 * i + 4 for i in range(NG)]
                dv_last_v = dv_seq[-1]

                @blk.tensor
                def _(t):
                    t.wait_ge(dv.h, dv_last_v)
                    for i in range(NG):
                        blki, hg = i // 4, i % 4
                        b = i % 2
                        if i >= 1:
                            t.wait_ge(ac.h, ac_oe[i - 1])
                        mm = None
                        for j in range(4):
                            h = hg * 4 + j
                            mm = t.matmul(ps_sc[j][:, :],
                                          hslice(qh_sb, h, blki), hslice(kh_sb, h, blki),
                                          start=True, stop=True)
                        mm.then_inc(pe.h, pe.tick())
                        assert pe.n == pe_sc[i]
                        t.wait_ge(dv.h, dv_pr[i])
                        if i >= 1:
                            t.wait_ge(ac.h, ac_t2[i - 1])
                        for j in range(4):
                            pp = ps_p[j % 2]
                            if j >= 2:
                                t.wait_ge(ac.h, ac_t1[i])
                            t.transpose(pp[:, :], p_sb[b][:, j * 128:(j + 1) * 128],
                                        eye_b[:, :]).then_inc(pe.h, pe.tick())
                            assert pe.n == pe_tr[i][j]
                        t.wait_ge(ac.h, ac_t2[i])
                        mm = None
                        for j in range(4):
                            h = hg * 4 + j
                            mm = t.matmul(ps_sc[j][0:64, 0:128],
                                          vh_sb[:, blki * W + h * 64:
                                                blki * W + h * 64 + 64],
                                          pT_sb[b][:, j * 128:(j + 1) * 128],
                                          start=True, stop=True)
                        mm.then_inc(pe.h, pe.tick())
                        assert pe.n == pe_oc[i]

                @blk.vector
                def _(v):
                    v.wait_ge(ll.h, ll_mask)
                    for i in range(NG):
                        b = i % 2
                        v.wait_ge(pe.h, pe_sc[i])
                        if i >= 2:
                            v.wait_ge(ac.h, ac_ex[i - 2])
                        tt = None
                        for j in range(4):
                            tt = v.tensor_tensor(s_sb[b][:, j * 128:(j + 1) * 128],
                                                 ps_sc[j][:, :],
                                                 mask_sb[:, 0:128], op=ALU.add)
                        vbar(v, tt)
                        rm = None
                        for j in range(4):
                            rm = v.reduce_max(atmp[:, j:j + 1],
                                              s_sb[b][:, j * 128:(j + 1) * 128],
                                              axis=AX, negate=True)
                        rm.then_inc(dv.h, dv.tick())
                        assert dv.n == dv_mk[i]
                        v.wait_ge(ac.h, ac_ex[i])
                        if i >= 2:
                            v.wait_ge(pe.h, pe_tr[i - 2][3])
                        rs = None
                        for j in range(4):
                            rs = v.reduce_sum(atmp[:, 4 + j:5 + j],
                                              e_sb[b][:, j * 128:(j + 1) * 128], axis=AX)
                        rs.then_inc(dv.h, dv.tick())
                        v.wait_ge(dv.h, dv.n)
                        v.reciprocal(atmp[:, 8:12], atmp[:, 4:8])\
                            .then_inc(dv.h, dv.tick())
                        v.wait_ge(dv.h, dv.n)
                        mmv = None
                        for j in range(4):
                            mmv = v.tensor_scalar(p_sb[b][:, j * 128:(j + 1) * 128],
                                                  e_sb[b][:, j * 128:(j + 1) * 128],
                                                  atmp[:, 8 + j:9 + j], None, op0=ALU.mult)
                        mmv.then_inc(dv.h, dv.tick())
                        assert dv.n == dv_pr[i]

                @blk.scalar
                def _(sc):
                    for i in range(NG):
                        b = i % 2
                        blki, hg = i // 4, i % 4
                        sc.wait_ge(dv.h, dv_mk[i])
                        if i >= 2:
                            sc.wait_ge(dv.h, dv_pr[i - 2])
                        ee = None
                        for j in range(4):
                            ee = sc.activation(e_sb[b][:, j * 128:(j + 1) * 128],
                                               s_sb[b][:, j * 128:(j + 1) * 128],
                                               AF.Exp, bias=atmp[:, j:j + 1])
                        ee.then_inc(ac.h, ac.tick())
                        assert ac.n == ac_ex[i]
                        for j in range(4):
                            sc.wait_ge(pe.h, pe_tr[i][j])
                            cc = sc.copy(pT_sb[b][:, j * 128:(j + 1) * 128],
                                         ps_p[j % 2][:, :])
                            if j in (1, 3):
                                cc.then_inc(ac.h, ac.tick())
                        assert ac.n == ac_t2[i]
                        sc.wait_ge(pe.h, pe_oc[i])
                        cc = None
                        for j in range(4):
                            h = hg * 4 + j
                            cc = sc.copy(oslice(h, blki), ps_sc[j][0:64, 0:128])
                        cc.then_inc(ac.h, ac.tick())
                        assert ac.n == ac_oe[i]

            if dbg_o is not None:
                with nc.Block("p5dbg2") as blk:
                    @blk.sync
                    def _(s):
                        s.wait_ge(ac.h, ac.n)
                        s.dma_start(out=dbg_o[:, :], in_=src_sb[:])\
                            .then_inc(so.h, so.tick(16))

            if upto <= 4.7:
                _final_wait(nc, so)
                return nc

            with nc.psum_tensor("ps_w0", [128, T], F32) as ps_w0, \
                 nc.psum_tensor("ps_w1", [128, T], F32) as ps_w1, \
                 nc.Block("p5e") as blk:
                ps_f = [ps_w0, ps_w1]
                wl0, pe0, dv0c, so0 = wl.n, pe.n, dv.n, so.n
                pe_w = [pe0 + (i + 1) for i in range(WC)]
                ac_w = [dv0c + (i + 1) for i in range(WC)]
                so_w = [so0 + 16 * (i + 1) for i in range(WC)]

                @blk.gpsimd
                def _(g):
                    g.wait_ge(pe.h, pe0)
                    src = wo_f[:, :].rearrange("(kc p) c -> p kc c", p=128)
                    dst = wbuf[1][:].rearrange("p (kc c) -> p kc c", kc=WC)
                    g.dma_start(out=dst, in_=src).then_inc(wl.h, wl.tick(16))
                wl_wo = wl.n

                @blk.tensor
                def _(t):
                    t.wait_ge(wl.h, wl_wo)
                    t.wait_ge(ac.h, ac.n)
                    for fc in range(WC):
                        b = fc % 2
                        if fc >= 2:
                            t.wait_ge(dv.h, ac_w[fc - 2])
                        mm = None
                        for kc in range(WC):
                            for tg in range(4):
                                mm = t.matmul(ps_f[b][:, tg * 512:(tg + 1) * 512],
                                              wbuf[1][:, (kc * WC + fc) * 128:
                                                      (kc * WC + fc + 1) * 128],
                                              src_sb[:, kc * T + tg * 512:
                                                     kc * T + (tg + 1) * 512],
                                              start=(kc == 0), stop=(kc == WC - 1))
                        mm.then_inc(pe.h, pe.tick())
                        assert pe.n == pe_w[fc]

                @blk.vector
                def _(v):
                    for fc in range(WC):
                        b = fc % 2
                        v.wait_ge(pe.h, pe_w[fc])
                        if fc >= 2:
                            v.wait_ge(so.h, so_w[fc - 2])
                        v.tensor_scalar(obf[b][:], ps_f[b][:], bot_sb[:, fc:fc + 1],
                                        None, op0=ALU.add).then_inc(dv.h, dv.tick())
                        assert dv.n == ac_w[fc]

                @blk.sync
                def _(s):
                    for fc in range(WC):
                        b = fc % 2
                        s.wait_ge(dv.h, ac_w[fc])
                        s.dma_start(out=o2_hbm[fc * 128:(fc + 1) * 128, :],
                                    in_=obf[b][:]).then_inc(so.h, so.tick(16))
                assert so.n == so_w[-1]

        if upto <= 5:
            _final_wait(nc, so)
            return nc

        p7 = ExitStack()
        with p7:
            p7sb = lambda n_, s_, d_: p7.enter_context(nc.sbuf_tensor(n_, list(s_), d_))
            lhsb = [p7sb(f"lhsb{i}", (128, 8 * T), BF16) for i in range(2)]
            rhsb = [p7sb(f"rhsb{i}", (128, 8 * W), BF16) for i in range(2)]
            acc = p7sb("acc", (128, NB * W), F32)
            xbuf = [p7sb(f"xbuf{i}", (128, W), F32) for i in range(2)]
            oxb = [p7sb(f"oxb{i}", (128, W), F32) for i in range(2)]
            B_sb = p7sb("B_sb", (4, W), F32)
            cxp = p7sb("cxp", (4, T), F32)

            NGRP = 14
            snap7 = snapshot()
            with nc.psum_tensor("ps_t0", [128, W], F32) as ps_t0, \
                 nc.psum_tensor("ps_t1", [128, W], F32) as ps_t1, \
                 nc.psum_tensor("ps_t2", [128, W], F32) as ps_t2, \
                 nc.Block("p7") as blk:
                ps_t = [ps_t0, ps_t1, ps_t2]

                sl0, wl0, ll0, pe0, dv0, so0 = sl.n, wl.n, ll.n, pe.n, dv.n, so.n
                sl_lhs, wl_rhs = {}, {}
                pe_g, dv_g = {}, {}
                ll_B, ll_x, dv_fin, so_out = {}, {}, {}, {}
                sln, wln, lln, pen, dvn, son = sl0, wl0, ll0, pe0, dv0, so0
                lln += 16
                ll_cxp = lln
                gidx = 0
                for fh in range(4):
                    lln += 16
                    ll_B[fh] = lln
                    for g in range(NGRP):
                        if g < 13:
                            sln += 16 * 8
                            sl_lhs[(fh, g)] = sln
                            wln += 16
                            wl_rhs[(fh, g)] = wln
                        for tt in range(NB):
                            pen += 1
                            pe_g[gidx] = pen
                            dvn += 1
                            dv_g[gidx] = dvn
                            gidx += 1
                    for tt in range(NB):
                        lln += 16
                        ll_x[(fh, tt)] = lln
                        dvn += 1
                        dv_fin[(fh, tt)] = dvn
                        son += 16
                        so_out[(fh, tt)] = son
                NGI = gidx

                def grp_src(g):
                    if g < 12:
                        e, kk = g // 4, g % 4
                        lhs_rows = [(e * KC + kk * 8 + i) * 128 for i in range(8)]
                        return ("act", lhs_rows, w2_e[e], kk * 8)
                    else:
                        lhs_rows = [i * 128 for i in range(8)]
                        return ("o2", lhs_rows, proj_w, 0)

                @blk.scalar
                def _(sc):
                    wait_snap(sc, snap7)
                    for fh in range(4):
                        for g in range(13):
                            lg = fh * 13 + g
                            gb = lg % 2
                            if lg >= 2:
                                fh2, g2 = divmod(lg - 2, 13)
                                sc.wait_ge(pe.h, pe_g[(fh2 * NGRP + g2) * NB + NB - 1])
                            kind, rows_, _, _ = grp_src(g)
                            srcten = act_hbm if kind == "act" else o2_hbm
                            for i2, r0 in enumerate(rows_):
                                sc.dma_start(out=lhsb[gb][:, i2 * T:(i2 + 1) * T],
                                             in_=srcten[r0:r0 + 128, :])\
                                    .then_inc(sl.h, sl.tick(16))
                            assert sl.n == sl_lhs[(fh, g)]

                @blk.gpsimd
                def _(g_):
                    wait_snap(g_, snap7)
                    for fh in range(4):
                        for g in range(13):
                            lg = fh * 13 + g
                            gb = lg % 2
                            if lg >= 2:
                                fh2, g2 = divmod(lg - 2, 13)
                                g_.wait_ge(pe.h, pe_g[(fh2 * NGRP + g2) * NB + NB - 1])
                            _, _, wten, kbase = grp_src(g)
                            src = wten[kbase * 128:(kbase + 8) * 128,
                                       fh * W:(fh + 1) * W]
                            src = src.rearrange("(kc p) c -> p kc c", p=128)
                            dst = rhsb[gb][:].rearrange("p (kc c) -> p kc c", kc=8)
                            g_.dma_start(out=dst, in_=src).then_inc(wl.h, wl.tick(16))
                            assert wl.n == wl_rhs[(fh, g)]

                @blk.sync
                def _(s):
                    wait_snap(s, snap7)
                    s.dma_start(out=cxp[0:4, :], in_=cx_hbm[0:4, :]).then_inc(ll.h, ll.tick(16))
                    for fh in range(4):
                        if fh > 0:
                            s.wait_ge(pe.h, pe_g[((fh - 1) * NGRP + 13) * NB + NB - 1])
                        s.dma_start(out=B_sb[0:4, :],
                                    in_=Bmat[0:4, fh * W:(fh + 1) * W])\
                            .then_inc(ll.h, ll.tick(16))
                        assert ll.n == ll_B[fh]
                        for tt in range(NB):
                            bx = tt % 2
                            if fh > 0 or tt >= 2:
                                pf, pt = (fh, tt - 2) if tt >= 2 else (fh - 1, tt + NB - 2)
                                s.wait_ge(dv.h, dv_fin[(pf, pt)])
                            s.dma_start(out=xbuf[bx][:],
                                        in_=x_tm[tt * 128:(tt + 1) * 128,
                                                 fh * W:(fh + 1) * W])\
                                .then_inc(ll.h, ll.tick(16))
                            assert ll.n == ll_x[(fh, tt)]
                            s.wait_ge(dv.h, dv_fin[(fh, tt)])
                            s.dma_start(out=out[tt * 128:(tt + 1) * 128,
                                                fh * W:(fh + 1) * W],
                                        in_=oxb[bx][:]).then_inc(so.h, so.tick(16))
                            assert so.n == so_out[(fh, tt)]

                @blk.tensor
                def _(t):
                    wait_snap(t, snap7)
                    t.wait_ge(ll.h, ll_cxp)
                    for fh in range(4):
                        for g in range(NGRP):
                            gb = (fh * 13 + g) % 2 if g < 13 else 0
                            if g < 13:
                                t.wait_ge(sl.h, sl_lhs[(fh, g)])
                                t.wait_ge(wl.h, wl_rhs[(fh, g)])
                            else:
                                t.wait_ge(ll.h, ll_B[fh])
                            for tt in range(NB):
                                gi = (fh * NGRP + g) * NB + tt
                                slot = gi % 3
                                if gi >= 3:
                                    t.wait_ge(dv.h, dv_g[gi - 3])
                                mm = None
                                if g < 13:
                                    for kc in range(8):
                                        for nh in range(2):
                                            mm = t.matmul(
                                                ps_t[slot][:, nh * 512:(nh + 1) * 512],
                                                lhsb[gb][:, kc * T + tt * 128:
                                                         kc * T + tt * 128 + 128],
                                                rhsb[gb][:, kc * W + nh * 512:
                                                         kc * W + (nh + 1) * 512],
                                                start=(kc == 0), stop=(kc == 7))
                                else:
                                    for nh in range(2):
                                        mm = t.matmul(
                                            ps_t[slot][:, nh * 512:(nh + 1) * 512],
                                            cxp[0:4, tt * 128:(tt + 1) * 128],
                                            B_sb[0:4, nh * 512:(nh + 1) * 512],
                                            start=True, stop=True)
                                mm.then_inc(pe.h, pe.tick())
                                assert pe.n == pe_g[gi]

                @blk.vector
                def _(v):
                    wait_snap(v, snap7)
                    for fh in range(4):
                        for g in range(NGRP):
                            for tt in range(NB):
                                gi = (fh * NGRP + g) * NB + tt
                                slot = gi % 3
                                v.wait_ge(pe.h, pe_g[gi])
                                aslice = acc[:, tt * W:(tt + 1) * W]
                                if g == 0:
                                    v.tensor_copy(aslice, ps_t[slot][:])\
                                        .then_inc(dv.h, dv.tick())
                                else:
                                    v.tensor_tensor(aslice, ps_t[slot][:], aslice,
                                                    op=ALU.add).then_inc(dv.h, dv.tick())
                                assert dv.n == dv_g[gi]
                        for tt in range(NB):
                            bx = tt % 2
                            v.wait_ge(ll.h, ll_x[(fh, tt)])
                            if fh > 0 or tt >= 2:
                                pf, pt = (fh, tt - 2) if tt >= 2 else (fh - 1, tt + NB - 2)
                                v.wait_ge(so.h, so_out[(pf, pt)])
                            v.tensor_tensor(oxb[bx][:], acc[:, tt * W:(tt + 1) * W],
                                            xbuf[bx][:], op=ALU.add)\
                                .then_inc(dv.h, dv.tick())
                            assert dv.n == dv_fin[(fh, tt)]

        _final_wait(nc, so)
        return nc


def _final_wait(nc, so):
    with nc.Block("fin") as blk:
        @blk.sync
        def _(s):
            s.wait_ge(so.h, so.n)



def _prep_common(inputs):
    f = np.float32
    cw = np.ascontiguousarray(np.asarray(inputs["L0_attn_conv_w"], f))
    conv_b = np.asarray(inputs["L0_attn_conv_b"], f)
    wq = np.asarray(inputs["L0_attn_wq"], f).reshape(W, W)
    wk = np.asarray(inputs["L0_attn_wk"], f).reshape(W, W)
    wv = np.asarray(inputs["L0_attn_wv"], f).reshape(W, W)
    bq = np.asarray(inputs["L0_attn_bq"], f).reshape(-1)
    bk = np.asarray(inputs["L0_attn_bk"], f).reshape(-1)
    bv = np.asarray(inputs["L0_attn_bv"], f).reshape(-1)
    wo = np.asarray(inputs["L0_attn_wo"], f).reshape(W, W)
    bo = np.asarray(inputs["L0_attn_bo"], f).reshape(-1)
    pw = np.ascontiguousarray(np.asarray(inputs["L0_attn_proj_w"], f))
    pb = np.asarray(inputs["L0_attn_proj_b"], f).reshape(-1)
    gw = np.asarray(inputs["L0_ffn_gate_w"], f)
    gb = np.asarray(inputs["L0_ffn_gate_b"], f).reshape(-1)
    w1 = np.asarray(inputs["L0_ffn_w1"], f)
    b1 = np.asarray(inputs["L0_ffn_b1"], f)
    w2 = np.asarray(inputs["L0_ffn_w2"], f)
    b2 = np.asarray(inputs["L0_ffn_b2"], f)

    tril = np.tril(np.ones((128, 128), dtype=bool))
    mask = np.where(tril, np.float32(0.0), np.float32(-1.0e9))

    common = {
        "conv_w0": np.ascontiguousarray(cw[0]),
        "conv_w1": np.ascontiguousarray(cw[1]),
        "conv_w2": np.ascontiguousarray(cw[2]),
        "conv_bt": np.ascontiguousarray(conv_b.reshape(QC, 128).T),
        "wq_f": wq, "wk_f": wk, "wv_f": wv, "wo_f": wo,
        "bqt": np.ascontiguousarray(bq.reshape(WC, 128).T),
        "bkt": np.ascontiguousarray(bk.reshape(WC, 128).T),
        "bot": np.ascontiguousarray(bo.reshape(WC, 128).T),
        "bv_row": np.ascontiguousarray(bv.reshape(1, W)),
        "proj_w": pw,
        "w1_0": np.ascontiguousarray(w1[0]),
        "w1_1": np.ascontiguousarray(w1[1]),
        "w1_2": np.ascontiguousarray(w1[2]),
        "w2_0": np.ascontiguousarray(w2[0]),
        "w2_1": np.ascontiguousarray(w2[1]),
        "w2_2": np.ascontiguousarray(w2[2]),
        "b1t": np.ascontiguousarray(
            b1.reshape(NE, KC, 128).transpose(2, 0, 1).reshape(128, NE * KC)),
        "gwa_t": np.ascontiguousarray(
            gw[:D].reshape(KC, 128, NE).transpose(1, 0, 2).reshape(128, KC * NE)),
        "gwb_col": np.ascontiguousarray(gw[D].reshape(NE, 1)),
        "gate_b_col": np.ascontiguousarray(gb.reshape(NE, 1)),
        "Bmat": np.ascontiguousarray(np.stack([pb, b2[0], b2[1], b2[2]])),
        "eye": np.eye(128, dtype=f),
        "mask4": np.ascontiguousarray(np.tile(mask, (1, 4))),
        "ones_col": np.ones((128, 1), f),
        "ones_row": np.ones((1, 128), f),
    }
    return common


def kernel(**inputs):
    x = np.ascontiguousarray(np.asarray(inputs["L0_attn_inputs"], np.float32))
    common = _prep_common(inputs)
    in_maps = []
    for c in range(NCORE):
        m = dict(common)
        m["x_tm"] = np.ascontiguousarray(x[c])
        m["x_fm"] = np.ascontiguousarray(x[c].T)
        in_maps.append(m)
    nc = build_nc()
    res = run_bass_kernel_spmd(nc, in_maps, core_ids=list(range(NCORE)))
    return np.stack([res.results[c]["out"] for c in range(NCORE)], axis=0)


# version 02
# speedup vs baseline: 1.0149x; 1.0149x over previous
import numpy as np
from contextlib import ExitStack

import concourse.bass as bass
import concourse.mybir as mybir
from concourse.bass_utils import run_bass_kernel_spmd

F32 = mybir.dt.float32
BF16 = mybir.dt.bfloat16
AF = mybir.ActivationFunctionType
ALU = mybir.AluOpType
AX = mybir.AxisListType.X

T = 2048
D = 4096
KC = 32
W = 1024
WC = 8
QC = 24
NB = 16
H = 16
DH = 64
NE = 3
VIS = 614
XNW = T + 4
NCORE = 8


class SemC:
    def __init__(self, h):
        self.h = h
        self.n = 0

    def inc(self, k=1):
        self.n += k
        return self.n

    def tick(self, k=1):
        self.n += k
        return k


def build_nc(upto=7, dbg=(), ng=None):
    dbg = set(dbg)
    nc = bass.Bass("TRN2", target_bir_lowering=False, debug=False)

    def din(name, shape, dtype=F32):
        return nc.dram_tensor(name, list(shape), dtype, kind="ExternalInput").ap()

    x_tm = din("x_tm", (T, D))
    x_fm = din("x_fm", (D, T))
    conv_wt = [din(f"conv_w{i}", (D, 3 * W)) for i in range(3)]
    conv_bt = din("conv_bt", (128, QC))
    wq_f = din("wq_f", (W, W))
    wk_f = din("wk_f", (W, W))
    wv_f = din("wv_f", (W, W))
    wo_f = din("wo_f", (W, W))
    bqt = din("bqt", (128, WC))
    bkt = din("bkt", (128, WC))
    bot = din("bot", (128, WC))
    bv_row = din("bv_row", (1, W))
    proj_w = din("proj_w", (W, D))
    w1_e = [din(f"w1_{e}", (D, D)) for e in range(NE)]
    w2_e = [din(f"w2_{e}", (D, D)) for e in range(NE)]
    b1t = din("b1t", (128, NE * KC))
    gwa_t = din("gwa_t", (128, KC * NE))
    gwb_col = din("gwb_col", (NE, 1))
    gate_b_col = din("gate_b_col", (NE, 1))
    Bmat = din("Bmat", (4, D))
    eye = din("eye", (128, 128))
    mask4_d = din("mask4", (128, 512))
    ones_col = din("ones_col", (128, 1))
    ones_row = din("ones_row", (1, 128))

    out = nc.dram_tensor("out", [T, D], F32, kind="ExternalOutput").ap()

    def scratch(name, shape, dtype):
        kind = "ExternalOutput" if name in dbg else "Internal"
        return nc.dram_tensor(name, list(shape), dtype, kind=kind).ap()

    qkv_hbm = scratch("qkv_hbm", (3 * W, T), BF16)
    act_hbm = scratch("act_hbm", (NE * D, T), BF16)
    o2_hbm = scratch("o2_hbm", (W, T), BF16)
    cx_hbm = scratch("cx_hbm", (8, T), F32)
    dbg_small = None
    if "small" in dbg:
        dbg_small = nc.dram_tensor("dbg_small", [128, 4096], F32, kind="ExternalOutput").ap()
    dbg_xn = None
    if "xn" in dbg:
        dbg_xn = nc.dram_tensor("dbg_xn", [128, KC * XNW], BF16, kind="ExternalOutput").ap()
    dbg_qh = None
    if "qh" in dbg:
        dbg_qh = nc.dram_tensor("dbg_qh", [128, 3 * WC * T], BF16, kind="ExternalOutput").ap()
    dbg_o = None
    if "o" in dbg:
        dbg_o = nc.dram_tensor("dbg_o", [128, WC * T], BF16, kind="ExternalOutput").ap()

    es = ExitStack()
    with es:
        wl = SemC(es.enter_context(nc.semaphore("wl")))
        ll = SemC(es.enter_context(nc.semaphore("ll")))
        sl = SemC(es.enter_context(nc.semaphore("sl")))
        pe = SemC(es.enter_context(nc.semaphore("pe")))
        ac = SemC(es.enter_context(nc.semaphore("ac")))
        dv = SemC(es.enter_context(nc.semaphore("dv")))
        so = SemC(es.enter_context(nc.semaphore("so")))
        all_sems = [wl, ll, sl, pe, ac, dv, so]

        def snapshot():
            return [(s_.h, s_.n) for s_ in all_sems]

        def vbar(v, inst):
            inst.then_inc(dv.h, dv.tick())
            v.wait_ge(dv.h, dv.n)
            return inst

        def wait_snap(eng, snap):
            for h_, n_ in snap:
                if n_ > 0:
                    eng.wait_ge(h_, n_)

        sbuf = lambda name, shape, dt_: es.enter_context(nc.sbuf_tensor(name, list(shape), dt_))

        eye_f = sbuf("eye_f", (128, 128), F32)
        eye_b = sbuf("eye_b", (128, 128), BF16)
        onec_f = sbuf("onec_f", (128, 1), F32)
        onec_b = sbuf("onec_b", (128, 1), BF16)
        oner_f = sbuf("oner_f", (1, 128), F32)
        oner_b = sbuf("oner_b", (1, 128), BF16)
        conv_bt_sb = sbuf("conv_bt_sb", (128, QC), F32)
        bqs_sb = sbuf("bqs_sb", (128, WC), F32)
        bkt_sb = sbuf("bkt_sb", (128, WC), F32)
        bot_sb = sbuf("bot_sb", (128, WC), F32)
        b1t_sb = sbuf("b1t_sb", (128, NE * KC), F32)
        gwb_sb = sbuf("gwb_sb", (NE, 1), F32)
        gb_sb = sbuf("gb_sb", (NE, 1), F32)
        xn_stack = ExitStack()
        xn = xn_stack.enter_context(nc.sbuf_tensor("xn", [128, KC * XNW], BF16))

        with nc.sbuf_tensor("bqt_tmp", [128, WC], F32) as bqt_tmp, \
             nc.Block("p0") as blk:

            @blk.gpsimd
            def _(g):
                g.dma_start(out=eye_b[:], in_=eye[:, :]).then_inc(wl.h, wl.tick(16))
                g.dma_start(out=onec_b[:], in_=ones_col[:, :]).then_inc(wl.h, wl.tick(16))
                g.dma_start(out=oner_b[:], in_=ones_row[:, :]).then_inc(wl.h, wl.tick(16))

            @blk.sync
            def _(s):
                for dst, src in [(eye_f, eye), (onec_f, ones_col), (oner_f, ones_row),
                                 (conv_bt_sb, conv_bt), (bqt_tmp, bqt), (bkt_sb, bkt),
                                 (bot_sb, bot), (b1t_sb, b1t), (gwb_sb, gwb_col),
                                 (gb_sb, gate_b_col)]:
                    s.dma_start(out=dst[:], in_=src[:, :]).then_inc(ll.h, ll.tick(16))

            @blk.vector
            def _(v):
                v.wait_ge(ll.h, ll.n)
                v.tensor_scalar(bqs_sb[:], bqt_tmp[:], 0.125, None, op0=ALU.mult)
                last = None
                for kc in range(KC):
                    v.memset(xn[:, kc * XNW: kc * XNW + 2], 0.0)
                    last = v.memset(xn[:, kc * XNW + 2 + T: (kc + 1) * XNW], 0.0)
                last.then_inc(dv.h, dv.tick())

        p1 = ExitStack()
        with p1:
            p1sb = lambda name, shape, dt_: p1.enter_context(
                nc.sbuf_tensor(name, list(shape), dt_))
            logits = p1sb("logits", (NE, T), F32)
            macc = p1sb("macc", (128, 4), F32)
            mrow = p1sb("mrow", (128, 8), F32)
            p1w = ExitStack()
            p1wsb = lambda name, shape, dt_: p1w.enter_context(
                nc.sbuf_tensor(name, list(shape), dt_))
            bc_a = p1wsb("bc_a", (128, T), F32)
            bc_b = p1wsb("bc_b", (128, T), F32)
            rows = p1wsb("rows", (1, 3 * T), F32)
            gwa_sb = p1wsb("gwa_sb", (128, KC * NE), F32)
            p1a_bufs = ExitStack()
            xb = [p1a_bufs.enter_context(nc.sbuf_tensor(f"xb{i}", [128, T], BF16))
                  for i in range(2)]
            sq = [p1a_bufs.enter_context(nc.sbuf_tensor(f"sq{i}", [128, T], BF16))
                  for i in range(2)]

            wl0, pe0, ac0 = wl.n, pe.n, ac.n
            wl_xb = [wl0 + 16 * (k + 1) for k in range(KC)]
            pe_st = [pe0 + (k + 1) for k in range(KC)]
            ac_sq = [ac0 + (k + 1) for k in range(KC)]

            snap1a = snapshot()
            with nc.psum_tensor("ps_sums", [1, T], F32) as ps_sums, \
                 nc.psum_tensor("ps_sumsq", [1, T], F32) as ps_sumsq, \
                 nc.Block("p1a") as blk:

                @blk.gpsimd
                def _(g):
                    wait_snap(g, snap1a)
                    for kc in range(KC):
                        b = kc % 2
                        if kc >= 2:
                            g.wait_ge(pe.h, pe_st[kc - 2])
                        g.dma_start(out=xb[b][:], in_=x_fm[kc * 128:(kc + 1) * 128, :])\
                            .then_inc(wl.h, wl.tick(16))
                assert wl.n == wl_xb[-1]

                @blk.sync
                def _(s):
                    s.dma_start(out=gwa_sb[:], in_=gwa_t[:, :]).then_inc(ll.h, ll.tick(16))

                @blk.scalar
                def _(sc):
                    for kc in range(KC):
                        b = kc % 2
                        sc.wait_ge(wl.h, wl_xb[kc])
                        if kc >= 2:
                            sc.wait_ge(pe.h, pe_st[kc - 2])
                        sc.activation(sq[b][:], xb[b][:], AF.Square)\
                            .then_inc(ac.h, ac.tick())
                assert ac.n == ac_sq[-1]

                @blk.tensor
                def _(t):
                    for kc in range(KC):
                        b = kc % 2
                        t.wait_ge(wl.h, wl_xb[kc])
                        for tg in range(4):
                            t.matmul(ps_sums[0:1, tg * 512:(tg + 1) * 512],
                                     onec_b[:, 0:1], xb[b][:, tg * 512:(tg + 1) * 512],
                                     start=(kc == 0), stop=(kc == KC - 1))
                        t.wait_ge(ac.h, ac_sq[kc])
                        mm = None
                        for tg in range(4):
                            mm = t.matmul(ps_sumsq[0:1, tg * 512:(tg + 1) * 512],
                                          onec_b[:, 0:1], sq[b][:, tg * 512:(tg + 1) * 512],
                                          start=(kc == 0), stop=(kc == KC - 1))
                        mm.then_inc(pe.h, pe.tick())
                assert pe.n == pe_st[-1]

                @blk.vector
                def _(v):
                    v.memset(macc[:], 0.0).then_inc(dv.h, dv.tick())

                @blk.scalar
                def _(sc):
                    sc.wait_ge(pe.h, pe_st[-1])
                    sc.mul(rows[0:1, 0:T], ps_sums[0:1, :], 1.0 / D)
                    sc.mul(rows[0:1, T:2 * T], ps_sumsq[0:1, :], 1.0 / D)\
                        .then_inc(ac.h, ac.tick())
            ac_stats = ac.n

            with nc.psum_tensor("ps_bca", [128, T], F32) as ps_bca, \
                 nc.psum_tensor("ps_bcb", [128, T], F32) as ps_bcb, \
                 nc.Block("p1a2") as blk:

                @blk.vector
                def _(v):
                    v.wait_ge(ac.h, ac_stats)
                    vbar(v, v.tensor_tensor(rows[0:1, 2 * T:3 * T], rows[0:1, 0:T],
                                            rows[0:1, 0:T], op=ALU.mult))
                    vbar(v, v.tensor_tensor(rows[0:1, T:2 * T], rows[0:1, T:2 * T],
                                            rows[0:1, 2 * T:3 * T], op=ALU.subtract))
                    vbar(v, v.tensor_scalar(rows[0:1, T:2 * T], rows[0:1, T:2 * T],
                                            1e-5, None, op0=ALU.add))
                dv_var = dv.n

                @blk.scalar
                def _(sc):
                    sc.wait_ge(dv.h, dv_var)
                    sc.activation(rows[0:1, 2 * T:3 * T], rows[0:1, T:2 * T],
                                  AF.Sqrt).then_inc(ac.h, ac.tick())
                ac_sd = ac.n

                @blk.vector
                def _(v):
                    v.wait_ge(ac.h, ac_sd)
                    vbar(v, v.reciprocal(rows[0:1, T:2 * T], rows[0:1, 2 * T:3 * T]))
                    vbar(v, v.scalar_tensor_tensor(rows[0:1, 2 * T:3 * T], rows[0:1, 0:T],
                                                   -1.0, rows[0:1, T:2 * T],
                                                   op0=ALU.mult, op1=ALU.mult))
                dv_rstd = dv.n

                @blk.tensor
                def _(t):
                    t.wait_ge(dv.h, dv_rstd)
                    t.wait_ge(ac.h, ac_stats)
                    for tg in range(4):
                        t.matmul(ps_bca[:, tg * 512:(tg + 1) * 512], oner_f[0:1, :],
                                 rows[0:1, T + tg * 512: T + (tg + 1) * 512],
                                 start=True, stop=True)
                    mm = None
                    for tg in range(4):
                        mm = t.matmul(ps_bcb[:, tg * 512:(tg + 1) * 512], oner_f[0:1, :],
                                      rows[0:1, 2 * T + tg * 512: 2 * T + (tg + 1) * 512],
                                      start=True, stop=True)
                    mm.then_inc(pe.h, pe.tick())
                pe_bc = pe.n

                @blk.vector
                def _(v):
                    v.wait_ge(pe.h, pe_bc)
                    v.tensor_copy(bc_a[:], ps_bca[:])
                    v.tensor_copy(bc_b[:], ps_bcb[:]).then_inc(dv.h, dv.tick())
            dv_bc = dv.n

            p1a_bufs.close()
            xf = [p1w.enter_context(nc.sbuf_tensor(f"xf{i}", [128, T], F32))
                  for i in range(2)]
            xnf0 = p1w.enter_context(nc.sbuf_tensor("xnf0", [128, T], F32))
            xnf = [xnf0, xnf0]
            ll0, dv0, ac0, pe0 = ll.n, dv.n, ac.n, pe.n
            ll_xf = [ll0 + 16 * (k + 1) for k in range(KC)]
            dv_tt1 = [dv0 + 4 * k + 1 for k in range(KC)]
            dv_tt2 = [dv0 + 4 * k + 2 for k in range(KC)]
            dv_mod = [dv0 + 4 * k + 4 for k in range(KC)]
            ac_cp = [ac0 + (k + 1) for k in range(KC)]
            pe_gate = [pe0 + (k + 1) for k in range(KC)]

            with nc.psum_tensor("ps_gate", [NE, T], F32) as ps_gate, \
                 nc.psum_tensor("ps_mv", [1, 1], F32) as ps_mv, \
                 nc.psum_tensor("ps_mt", [1, 1], F32) as ps_mt, \
                 nc.psum_tensor("ps_r3", [NE, 1], F32) as ps_r3, \
                 nc.Block("p1b") as blk:

                @blk.sync
                def _(s):
                    s.wait_ge(pe.h, pe_bc)
                    for kc in range(KC):
                        b = kc % 2
                        if kc >= 2:
                            s.wait_ge(dv.h, dv_tt1[kc - 2])
                        s.dma_start(out=xf[b][:], in_=x_fm[kc * 128:(kc + 1) * 128, :])\
                            .then_inc(ll.h, ll.tick(16))
                assert ll.n == ll_xf[-1]

                @blk.vector
                def _(v):
                    v.wait_ge(dv.h, dv_bc)
                    for kc in range(KC):
                        b = kc % 2
                        v.wait_ge(ll.h, ll_xf[kc])
                        if kc >= 1:
                            v.wait_ge(ac.h, ac_cp[kc - 1])
                            v.wait_ge(pe.h, pe_gate[kc - 1])
                        vbar(v, v.tensor_tensor(xnf[b][:], xf[b][:], bc_a[:],
                                                op=ALU.mult))
                        vbar(v, v.tensor_tensor(xnf[b][:], xnf[b][:], bc_b[:],
                                                op=ALU.add))
                        v.tensor_reduce(mrow[:, 2:3], xnf[b][:, 0:VIS], axis=AX,
                                        op=ALU.add, apply_absolute_value=True)
                        vbar(v, v.tensor_reduce(mrow[:, 3:4], xnf[b][:, VIS:T], axis=AX,
                                                op=ALU.add, apply_absolute_value=True))
                        v.tensor_tensor(macc[:, 0:1], macc[:, 0:1], mrow[:, 2:3], op=ALU.add)
                        v.tensor_tensor(macc[:, 1:2], macc[:, 1:2], mrow[:, 3:4], op=ALU.add)\
                            .then_inc(dv.h, dv.tick())
                assert dv.n == dv_mod[-1]

                @blk.scalar
                def _(sc):
                    for kc in range(KC):
                        b = kc % 2
                        sc.wait_ge(dv.h, dv_tt2[kc])
                        sc.copy(xn[:, kc * XNW + 2: kc * XNW + 2 + T], xnf[b][:])\
                            .then_inc(ac.h, ac.tick())
                assert ac.n == ac_cp[-1]

                @blk.tensor
                def _(t):
                    for kc in range(KC):
                        b = kc % 2
                        t.wait_ge(dv.h, dv_tt2[kc])
                        mm = None
                        for tg in range(4):
                            mm = t.matmul(ps_gate[0:NE, tg * 512:(tg + 1) * 512],
                                          gwa_sb[:, kc * NE:(kc + 1) * NE],
                                          xnf[b][:, tg * 512:(tg + 1) * 512],
                                          start=(kc == 0), stop=(kc == KC - 1))
                        mm.then_inc(pe.h, pe.tick())
                assert pe.n == pe_gate[-1]

                @blk.tensor
                def _(t):
                    t.wait_ge(dv.h, dv_mod[-1])
                    t.matmul(ps_mv[0:1, 0:1], macc[:, 0:1], onec_f[:, 0:1],
                             start=True, stop=True)
                    t.matmul(ps_mt[0:1, 0:1], macc[:, 1:2], onec_f[:, 0:1],
                             start=True, stop=True).then_inc(pe.h, pe.tick())
                pe_mtot = pe.n

                @blk.scalar
                def _(sc):
                    sc.wait_ge(pe.h, pe_mtot)
                    sc.mul(mrow[0:1, 4:5], ps_mt[0:1, 0:1], 1.0 / ((T - VIS) * D))
                    sc.mul(mrow[0:1, 5:6], ps_mv[0:1, 0:1], 1.0 / (VIS * D))\
                        .then_inc(ac.h, ac.tick())
                ac_mm = ac.n

                @blk.vector
                def _(v):
                    v.wait_ge(ac.h, ac_mm)
                    v.tensor_scalar(mrow[0:1, 6:7], mrow[0:1, 5:6], 1e-8, None,
                                    op0=ALU.add).then_inc(dv.h, dv.tick())
                    v.wait_ge(dv.h, dv.n)
                    v.reciprocal(mrow[0:1, 7:8], mrow[0:1, 6:7]).then_inc(dv.h, dv.tick())
                    v.wait_ge(dv.h, dv.n)
                    v.tensor_tensor(mrow[0:1, 0:1], mrow[0:1, 4:5], mrow[0:1, 7:8],
                                    op=ALU.mult).then_inc(dv.h, dv.tick())
                dv_ratio = dv.n

                @blk.tensor
                def _(t):
                    t.wait_ge(dv.h, dv_ratio)
                    t.matmul(ps_r3[0:NE, 0:1], oner_f[0:1, 0:NE], mrow[0:1, 0:1],
                             start=True, stop=True).then_inc(pe.h, pe.tick())
                pe_r3 = pe.n

                @blk.vector
                def _(v):
                    v.wait_ge(pe.h, pe_r3)
                    v.tensor_copy(mrow[0:NE, 1:2], ps_r3[0:NE, 0:1])\
                        .then_inc(dv.h, dv.tick())
                    v.wait_ge(dv.h, dv.n)
                    v.scalar_tensor_tensor(mrow[0:NE, 2:3], gwb_sb[0:NE, 0:1],
                                           mrow[0:NE, 1:2], gb_sb[0:NE, 0:1],
                                           op0=ALU.mult, op1=ALU.add)\
                        .then_inc(dv.h, dv.tick())
                dv_b3 = dv.n

                @blk.vector
                def _(v):
                    v.wait_ge(dv.h, dv_b3)
                    v.wait_ge(pe.h, pe_gate[-1])
                    v.tensor_scalar(logits[0:NE, :], ps_gate[0:NE, :], mrow[0:NE, 2:3],
                                    None, op0=ALU.add).then_inc(dv.h, dv.tick())
                dv_logits = dv.n

                if dbg_small is not None:
                    @blk.sync
                    def _(s):
                        s.wait_ge(ac.h, ac.n)
                        s.wait_ge(dv.h, dv.n)
                        s.dma_start(out=dbg_small[0:NE, 0:T], in_=logits[:])\
                            .then_inc(so.h, so.tick(16))
                        s.dma_start(out=dbg_small[8:9, 0:8], in_=mrow[0:1, :])\
                            .then_inc(so.h, so.tick(16))
                        s.dma_start(out=dbg_small[16:17, 0:T], in_=bc_a[0:1, :])\
                            .then_inc(so.h, so.tick(16))
                        s.dma_start(out=dbg_small[17:18, 0:T], in_=bc_b[0:1, :])\
                            .then_inc(so.h, so.tick(16))

                if dbg_xn is not None:
                    @blk.sync
                    def _(s):
                        s.wait_ge(ac.h, ac_cp[-1])
                        s.dma_start(out=dbg_xn[:, :], in_=xn[:]).then_inc(so.h, so.tick(16))

            p1w.close()
            p2 = ExitStack()
            with p2:
                p2sb = lambda n_, s_, d_: p2.enter_context(nc.sbuf_tensor(n_, list(s_), d_))
                lg_tm = p2sb("lg_tm", (128, 64), F32)
                sc_tm = p2sb("sc_tm", (128, 64), F32)
                v01 = p2sb("v01", (128, 32), F32)
                ptmp = p2sb("ptmp", (128, 16), F32)
                es_t = p2sb("es_t", (128, 4), F32)
                oh = p2sb("oh", (2, 16), F32)
                m2_sb = p2sb("m2_sb", (2, 4), F32)
                vrows = p2sb("vrows", (2, T), F32)
                cxt = p2sb("cxt", (4, T), F32)
                csb2 = p2sb("csb2", (NE, T), F32)

                with nc.psum_tensor("ps_ta", [128, 4], F32) as ps_ta, \
                     nc.psum_tensor("ps_tb", [128, 4], F32) as ps_tb, \
                     nc.psum_tensor("ps_va", [2, 128], F32) as ps_va, \
                     nc.psum_tensor("ps_vb", [2, 128], F32) as ps_vb, \
                     nc.psum_tensor("ps_c", [NE, T], F32) as ps_c, \
                     nc.Block("p2") as blk:

                    pe0, ac0 = pe.n, ac.n
                    pe_ltr = [pe0 + (ti + 1) for ti in range(NB)]
                    ac_lev = [ac0 + (ti + 1) for ti in range(NB)]

                    @blk.tensor
                    def _(t):
                        t.wait_ge(dv.h, dv_logits)
                        for ti in range(NB):
                            ps = [ps_ta, ps_tb][ti % 2]
                            if ti >= 2:
                                t.wait_ge(ac.h, ac_lev[ti - 2])
                            t.transpose(ps[:, 0:NE], logits[0:NE, ti * 128:(ti + 1) * 128],
                                        eye_f[0:NE, 0:NE]).then_inc(pe.h, pe.tick())
                    assert pe.n == pe_ltr[-1]

                    @blk.scalar
                    def _(sc):
                        for ti in range(NB):
                            ps = [ps_ta, ps_tb][ti % 2]
                            sc.wait_ge(pe.h, pe_ltr[ti])
                            sc.copy(lg_tm[:, ti * 4: ti * 4 + NE], ps[:, 0:NE])\
                                .then_inc(ac.h, ac.tick())
                    assert ac.n == ac_lev[-1]

                    dv1 = dv.n
                    dv_nm = [dv1 + 11 * ti + 1 for ti in range(NB)]
                    dv_tl = [dv1 + 11 * ti + 11 for ti in range(NB)]
                    ac1 = ac.n
                    ac_exp = [ac1 + (ti + 1) for ti in range(NB)]

                    @blk.vector
                    def _(v):
                        for ti in range(NB):
                            lg = lg_tm[:, ti * 4: ti * 4 + NE]
                            v.wait_ge(ac.h, ac_lev[ti])
                            v.reduce_max(ptmp[:, 0:1], lg, axis=AX, negate=True)\
                                .then_inc(dv.h, dv.tick())
                            v.wait_ge(ac.h, ac_exp[ti])
                            vbar(v, v.reduce_sum(ptmp[:, 1:2], es_t[:, 0:NE], axis=AX))
                            vbar(v, v.reciprocal(ptmp[:, 2:3], ptmp[:, 1:2]))
                            vbar(v, v.tensor_scalar(sc_tm[:, ti * 4: ti * 4 + NE],
                                                    es_t[:, 0:NE], ptmp[:, 2:3], None,
                                                    op0=ALU.mult))
                            s3 = sc_tm[:, ti * 4: ti * 4 + NE]
                            v.reduce_max(ptmp[:, 3:4], s3, axis=AX)
                            v.tensor_reduce(ptmp[:, 4:5], s3, axis=AX, op=ALU.min)
                            vbar(v, v.reduce_sum(ptmp[:, 5:6], s3, axis=AX))
                            vbar(v, v.tensor_tensor(ptmp[:, 6:7], ptmp[:, 5:6],
                                                    ptmp[:, 3:4], op=ALU.subtract))
                            vbar(v, v.tensor_tensor(ptmp[:, 6:7], ptmp[:, 6:7],
                                                    ptmp[:, 4:5], op=ALU.subtract))
                            vbar(v, v.tensor_tensor(ptmp[:, 7:8], ptmp[:, 3:4],
                                                    ptmp[:, 6:7], op=ALU.add))
                            vbar(v, v.tensor_scalar(ptmp[:, 7:8], ptmp[:, 7:8], 1e-8,
                                                    None, op0=ALU.add))
                            vbar(v, v.reciprocal(ptmp[:, 8:9], ptmp[:, 7:8]))
                            v.tensor_tensor(v01[:, 2 * ti:2 * ti + 1], ptmp[:, 3:4],
                                            ptmp[:, 8:9], op=ALU.mult)
                            v.tensor_tensor(v01[:, 2 * ti + 1:2 * ti + 2], ptmp[:, 6:7],
                                            ptmp[:, 8:9], op=ALU.mult)\
                                .then_inc(dv.h, dv.tick())
                    assert dv.n == dv_tl[-1]

                    @blk.scalar
                    def _(sc):
                        for ti in range(NB):
                            lg = lg_tm[:, ti * 4: ti * 4 + NE]
                            sc.wait_ge(dv.h, dv_nm[ti])
                            if ti >= 1:
                                sc.wait_ge(dv.h, dv_tl[ti - 1])
                            sc.activation(es_t[:, 0:NE], lg, AF.Exp,
                                          bias=ptmp[:, 0:1]).then_inc(ac.h, ac.tick())
                    assert ac.n == ac_exp[-1]

                    @blk.vector
                    def _(v):
                        v.wait_ge(dv.h, dv_tl[NB - 1])
                        r0 = sc_tm[0:1, 0:NE]
                        vbar(v, v.reduce_max(ptmp[0:1, 9:10], r0, axis=AX))
                        vbar(v, v.tensor_reduce(ptmp[0:1, 10:11], r0, axis=AX,
                                                op=ALU.min))
                        vbar(v, v.tensor_scalar(oh[0:1, 0:NE], r0, ptmp[0:1, 9:10], None,
                                                op0=ALU.is_equal))
                        vbar(v, v.tensor_scalar(oh[0:1, 4:4 + NE], r0, ptmp[0:1, 10:11],
                                                None, op0=ALU.is_equal))
                        vbar(v, v.tensor_tensor(oh[0:1, 8:8 + NE], oh[0:1, 0:NE],
                                                oh[0:1, 4:4 + NE], op=ALU.add))
                        vbar(v, v.tensor_scalar(oh[0:1, 8:8 + NE], oh[0:1, 8:8 + NE],
                                                1.0, -1.0, op0=ALU.subtract,
                                                op1=ALU.mult))
                    dv_oh = dv.n

                    @blk.sync
                    def _(s):
                        s.wait_ge(dv.h, dv_oh)
                        s.dma_start(out=m2_sb[0:1, 0:NE], in_=oh[0:1, 0:NE])\
                            .then_inc(ll.h, ll.tick(16))
                        s.dma_start(out=m2_sb[1:2, 0:NE], in_=oh[0:1, 8:8 + NE])\
                            .then_inc(ll.h, ll.tick(16))
                    ll_m2 = ll.n

                    pe1 = pe.n
                    pe_vtr = [pe1 + (ti + 1) for ti in range(NB)]
                    ac2 = ac.n
                    ac_vev = [ac2 + (ti + 1) for ti in range(NB)]

                    @blk.tensor
                    def _(t):
                        for ti in range(NB):
                            ps = [ps_va, ps_vb][ti % 2]
                            t.wait_ge(dv.h, dv_tl[ti])
                            if ti >= 2:
                                t.wait_ge(ac.h, ac_vev[ti - 2])
                            t.transpose(ps[0:2, :], v01[:, 2 * ti:2 * ti + 2],
                                        eye_f[:, :]).then_inc(pe.h, pe.tick())
                    assert pe.n == pe_vtr[-1]

                    @blk.scalar
                    def _(sc):
                        for ti in range(NB):
                            ps = [ps_va, ps_vb][ti % 2]
                            sc.wait_ge(pe.h, pe_vtr[ti])
                            sc.copy(vrows[0:2, ti * 128:(ti + 1) * 128], ps[0:2, :])\
                                .then_inc(ac.h, ac.tick())
                    assert ac.n == ac_vev[-1]

                    @blk.tensor
                    def _(t):
                        t.wait_ge(ll.h, ll_m2)
                        t.wait_ge(ac.h, ac_vev[-1])
                        mm = None
                        for tg in range(4):
                            mm = t.matmul(ps_c[0:NE, tg * 512:(tg + 1) * 512],
                                          m2_sb[0:2, 0:NE],
                                          vrows[0:2, tg * 512:(tg + 1) * 512],
                                          start=True, stop=True)
                        mm.then_inc(pe.h, pe.tick())
                    pe_c = pe.n

                    @blk.vector
                    def _(v):
                        v.wait_ge(pe.h, pe_c)
                        v.tensor_copy(csb2[0:NE, :], ps_c[0:NE, :])
                        v.memset(cxt[0:1, :], 1.0).then_inc(dv.h, dv.tick())
                    dv_c = dv.n

                    @blk.sync
                    def _(s):
                        s.wait_ge(dv.h, dv_c)
                        s.dma_start(out=cxt[1:4, :], in_=csb2[0:NE, :])\
                            .then_inc(ll.h, ll.tick(16))
                        s.wait_ge(ll.h, ll.n)
                        s.dma_start(out=cx_hbm[0:4, :], in_=cxt[0:4, :])\
                            .then_inc(so.h, so.tick(16))
                        s.dma_start(out=cx_hbm[4:7, :], in_=csb2[0:NE, :])\
                            .then_inc(so.h, so.tick(16))
                        if dbg_small is not None:
                            s.dma_start(out=dbg_small[24:28, 0:T], in_=cxt[0:4, :])\
                                .then_inc(so.h, so.tick(16))
                            s.dma_start(out=dbg_small[32:34, 0:16], in_=oh[0:2, :])\
                                .then_inc(so.h, so.tick(16))
                            s.dma_start(out=dbg_small[36:38, 0:T], in_=vrows[0:2, :])\
                                .then_inc(so.h, so.tick(16))
                            s.dma_start(out=dbg_small[40:41, 0:64], in_=sc_tm[0:1, :])\
                                .then_inc(so.h, so.tick(16))

        if upto <= 2:
            xn_stack.close()
            _final_wait(nc, so)
            return nc

        p3 = ExitStack()
        with p3:
            csb_row = p3.enter_context(nc.sbuf_tensor("csb_row", [1, NE * T], F32))
            w1g = [p3.enter_context(nc.sbuf_tensor(f"w1g{i}", [128, KC * 128], BF16))
                   for i in range(2)]
            cbc = p3.enter_context(nc.sbuf_tensor("cbc", [128, T], F32))
            stg = [p3.enter_context(nc.sbuf_tensor(f"stg{i}", [128, T], F32))
                   for i in range(2)]
            abf = [p3.enter_context(nc.sbuf_tensor(f"abf{i}", [128, T], BF16))
                   for i in range(2)]
            FUNC_E = [AF.Tanh, AF.Gelu, AF.Silu]

            snap3 = snapshot()
            with nc.psum_tensor("ps_f0", [128, T], F32) as ps_f0, \
                 nc.psum_tensor("ps_f1", [128, T], F32) as ps_f1, \
                 nc.Block("p3") as blk:
                ps_f = [ps_f0, ps_f1]

                wl0, pe0, ac0, dv0, so0 = wl.n, pe.n, ac.n, dv.n, so.n
                ll_csb = ll.n + 48
                wl_w1, pe_ch, pe_cbc, dv_cbc, ac_ev, dv_tt, so_st = [], [], [], [], [], [], []
                wn, pn, an, dn, sn = wl0, pe0, ac0, dv0, so0
                for e in range(NE):
                    pn += 1
                    pe_cbc.append(pn)
                    dn += 1
                    dv_cbc.append(dn)
                    for fc in range(KC):
                        wn += 32
                        wl_w1.append(wn)
                        pn += 1
                        pe_ch.append(pn)
                        an += 1
                        ac_ev.append(an)
                        dn += 1
                        dv_tt.append(dn)
                        sn += 16
                        so_st.append(sn)

                @blk.gpsimd
                def _(g):
                    wait_snap(g, snap3)
                    for e in range(NE):
                        for fc in range(KC):
                            i = e * KC + fc
                            b = i % 2
                            if i >= 2:
                                g.wait_ge(pe.h, pe_ch[i - 2])
                            for hh in range(2):
                                src = w1_e[e][hh * 2048:(hh + 1) * 2048,
                                              fc * 128:(fc + 1) * 128]
                                src = src.rearrange("(kc p) c -> p kc c", p=128)
                                dst = w1g[b][:, hh * 16 * 128:(hh + 1) * 16 * 128]
                                dst = dst.rearrange("p (kc c) -> p kc c", kc=16)
                                g.dma_start(out=dst, in_=src).then_inc(wl.h, wl.tick(16))
                assert wl.n == wl_w1[-1]

                @blk.tensor
                def _(t):
                    wait_snap(t, snap3)
                    t.wait_ge(ll.h, ll_csb)
                    for e in range(NE):
                        if e > 0:
                            t.wait_ge(ac.h, ac_ev[e * KC - 1])
                            t.wait_ge(ac.h, ac_ev[e * KC - 2])
                        mm = None
                        for tg in range(4):
                            mm = t.matmul(ps_f0[:, tg * 512:(tg + 1) * 512], oner_f[0:1, :],
                                          csb_row[0:1, e * T + tg * 512: e * T + (tg + 1) * 512],
                                          start=True, stop=True)
                        mm.then_inc(pe.h, pe.tick())
                        assert pe.n == pe_cbc[e]
                        for fc in range(KC):
                            i = e * KC + fc
                            b = i % 2
                            t.wait_ge(wl.h, wl_w1[i])
                            if fc == 0:
                                t.wait_ge(dv.h, dv_cbc[e])
                            if i >= 2:
                                t.wait_ge(ac.h, ac_ev[i - 2])
                            mm = None
                            for kc in range(KC):
                                for tg in range(4):
                                    mm = t.matmul(
                                        ps_f[b][:, tg * 512:(tg + 1) * 512],
                                        w1g[b][:, kc * 128:(kc + 1) * 128],
                                        xn[:, kc * XNW + 2 + tg * 512:
                                           kc * XNW + 2 + (tg + 1) * 512],
                                        start=(kc == 0), stop=(kc == KC - 1))
                            mm.then_inc(pe.h, pe.tick())
                            assert pe.n == pe_ch[i]

                @blk.vector
                def _(v):
                    for e in range(NE):
                        v.wait_ge(pe.h, pe_cbc[e])
                        v.tensor_copy(cbc[:], ps_f0[:]).then_inc(dv.h, dv.tick())
                        v.wait_ge(dv.h, dv.n)
                        assert dv.n == dv_cbc[e]
                        for fc in range(KC):
                            i = e * KC + fc
                            b = i % 2
                            v.wait_ge(ac.h, ac_ev[i])
                            if i >= 2:
                                v.wait_ge(so.h, so_st[i - 2])
                            v.tensor_tensor(abf[b][:], stg[b][:], cbc[:], op=ALU.mult)\
                                .then_inc(dv.h, dv.tick())
                            assert dv.n == dv_tt[i]

                @blk.scalar
                def _(sc):
                    for e in range(NE):
                        for fc in range(KC):
                            i = e * KC + fc
                            b = i % 2
                            sc.wait_ge(pe.h, pe_ch[i])
                            if i >= 2:
                                sc.wait_ge(dv.h, dv_tt[i - 2])
                            sc.activation(stg[b][:], ps_f[b][:], FUNC_E[e],
                                          bias=b1t_sb[:, e * KC + fc: e * KC + fc + 1])\
                                .then_inc(ac.h, ac.tick())
                            assert ac.n == ac_ev[i]

                @blk.sync
                def _(s):
                    wait_snap(s, snap3)
                    for e in range(NE):
                        s.dma_start(out=csb_row[0:1, e * T:(e + 1) * T],
                                    in_=cx_hbm[4 + e:5 + e, :])\
                            .then_inc(ll.h, ll.tick(16))
                    assert ll.n == ll_csb
                    for e in range(NE):
                        for fc in range(KC):
                            i = e * KC + fc
                            b = i % 2
                            s.wait_ge(dv.h, dv_tt[i])
                            s.dma_start(out=act_hbm[(e * KC + fc) * 128:
                                                    (e * KC + fc + 1) * 128, :],
                                        in_=abf[b][:]).then_inc(so.h, so.tick(16))
                            assert so.n == so_st[i]

        if upto <= 3:
            xn_stack.close()
            _final_wait(nc, so)
            return nc

        p4 = ExitStack()
        with p4:
            cw = [p4.enter_context(nc.sbuf_tensor(f"cw{i}", [128, 3 * KC * 128], BF16))
                  for i in range(2)]
            qbf = [p4.enter_context(nc.sbuf_tensor(f"qbf{i}", [128, T], BF16))
                   for i in range(2)]

            snap4 = snapshot()
            with nc.psum_tensor("ps_c0", [128, T], F32) as ps_c0, \
                 nc.psum_tensor("ps_c1", [128, T], F32) as ps_c1, \
                 nc.Block("p4") as blk:
                ps_f = [ps_c0, ps_c1]

                wl0, pe0, dv0, so0 = wl.n, pe.n, dv.n, so.n
                wl_cw = [wl0 + 96 * (i + 1) for i in range(QC)]
                pe_ch4 = [pe0 + (i + 1) for i in range(QC)]
                dv_ev4 = [dv0 + (i + 1) for i in range(QC)]
                so_st4 = [so0 + 16 * (i + 1) for i in range(QC)]

                @blk.gpsimd
                def _(g):
                    wait_snap(g, snap4)
                    for fc in range(QC):
                        b = fc % 2
                        if fc >= 2:
                            g.wait_ge(pe.h, pe_ch4[fc - 2])
                        for tap in range(3):
                            for hh in range(2):
                                src = conv_wt[tap][hh * 2048:(hh + 1) * 2048,
                                                   fc * 128:(fc + 1) * 128]
                                src = src.rearrange("(kc p) c -> p kc c", p=128)
                                dst = cw[b][:, (tap * KC + hh * 16) * 128:
                                            (tap * KC + hh * 16 + 16) * 128]
                                dst = dst.rearrange("p (kc c) -> p kc c", kc=16)
                                g.dma_start(out=dst, in_=src).then_inc(wl.h, wl.tick(16))
                assert wl.n == wl_cw[-1]

                @blk.tensor
                def _(t):
                    wait_snap(t, snap4)
                    for fc in range(QC):
                        b = fc % 2
                        t.wait_ge(wl.h, wl_cw[fc])
                        if fc >= 2:
                            t.wait_ge(dv.h, dv_ev4[fc - 2])
                        mm = None
                        for kc in range(KC):
                            for tap in range(3):
                                for tg in range(4):
                                    mm = t.matmul(
                                        ps_f[b][:, tg * 512:(tg + 1) * 512],
                                        cw[b][:, (tap * KC + kc) * 128:
                                              (tap * KC + kc + 1) * 128],
                                        xn[:, kc * XNW + 1 + tap + tg * 512:
                                           kc * XNW + 1 + tap + (tg + 1) * 512],
                                        start=(kc == 0 and tap == 0),
                                        stop=(kc == KC - 1 and tap == 2))
                        mm.then_inc(pe.h, pe.tick())
                assert pe.n == pe_ch4[-1]

                @blk.vector
                def _(v):
                    for fc in range(QC):
                        b = fc % 2
                        v.wait_ge(pe.h, pe_ch4[fc])
                        if fc >= 2:
                            v.wait_ge(so.h, so_st4[fc - 2])
                        v.tensor_scalar(qbf[b][:], ps_f[b][:], conv_bt_sb[:, fc:fc + 1],
                                        None, op0=ALU.add).then_inc(dv.h, dv.tick())
                assert dv.n == dv_ev4[-1]

                @blk.sync
                def _(s):
                    for fc in range(QC):
                        b = fc % 2
                        s.wait_ge(dv.h, dv_ev4[fc])
                        s.dma_start(out=qkv_hbm[fc * 128:(fc + 1) * 128, :],
                                    in_=qbf[b][:]).then_inc(so.h, so.tick(16))
                assert so.n == so_st4[-1]

        if upto <= 4:
            xn_stack.close()
            _final_wait(nc, so)
            return nc

        xn_stack.close()
        p5 = ExitStack()
        with p5:
            p5sb = lambda n_, s_, d_: p5.enter_context(nc.sbuf_tensor(n_, list(s_), d_))
            src_sb = p5sb("src_sb", (128, WC * T), BF16)
            qh_sb = p5sb("qh_sb", (128, WC * T), BF16)
            kh_sb = p5sb("kh_sb", (128, WC * T), BF16)
            vh_sb = p5sb("vh_sb", (128, NB * W), BF16)
            wbuf = [p5sb(f"wbuf{i}", (128, WC * W), BF16) for i in range(2)]
            mask_sb = p5sb("mask_sb", (128, 512), F32)
            bv_b = p5sb("bv_b", (1, W), BF16)
            s_sb = [p5sb(f"s_sb{i}", (128, 512), F32) for i in range(2)]
            e_sb = [p5sb(f"e_sb{i}", (128, 512), F32) for i in range(2)]
            p_f = [p5sb(f"p_f{i}", (128, 512), F32) for i in range(2)]
            pT_sb = [p5sb(f"pT_sb{i}", (128, 512), BF16) for i in range(2)]
            atmp = p5sb("atmp", (128, 12), F32)
            obf = [p5sb(f"obf{i}", (128, T), BF16) for i in range(2)]

            def oslice(h, blki):
                return src_sb[(h % 2) * 64:(h % 2) * 64 + 64,
                              (h // 2) * T + blki * 128:(h // 2) * T + blki * 128 + 128]

            def hslice(hsb, h, blki):
                return hsb[(h % 2) * 64:(h % 2) * 64 + 64,
                           (h // 2) * T + blki * 128:(h // 2) * T + blki * 128 + 128]

            snap5 = snapshot()
            with nc.psum_tensor("ps_q0", [128, T], F32) as ps_q0, \
                 nc.psum_tensor("ps_q1", [128, T], F32) as ps_q1, \
                 nc.Block("p5abc") as blk:
                ps_f = [ps_q0, ps_q1]

                ll0, wl0, pe0 = ll.n, wl.n, pe.n
                ac0b = ac.n
                ll_mask = ll0 + 16
                ll_src = [ll_mask + 16 * 8 * (p + 1) for p in range(3)]
                wl_w = [wl0 + 16 * (p + 2) for p in range(3)]
                dv0b = dv.n
                pe_q = [pe0 + (i + 1) for i in range(WC)]
                pe_k = [pe_q[-1] + (i + 1) for i in range(WC)]
                pe_v = [pe_k[-1] + (i + 1) for i in range(NB)]
                dv_seq = [dv0b + (i + 1) for i in range(2 * WC + NB)]

                @blk.sync
                def _(s):
                    wait_snap(s, snap5)
                    s.dma_start(out=mask_sb[:], in_=mask4_d[:, :]).then_inc(ll.h, ll.tick(16))
                    for part in range(3):
                        if part == 1:
                            s.wait_ge(pe.h, pe_q[-1])
                        if part == 2:
                            s.wait_ge(pe.h, pe_k[-1])
                        for kc in range(WC):
                            s.dma_start(out=src_sb[:, kc * T:(kc + 1) * T],
                                        in_=qkv_hbm[(part * WC + kc) * 128:
                                                    (part * WC + kc + 1) * 128, :])\
                                .then_inc(ll.h, ll.tick(16))
                assert ll.n == ll_src[-1]

                @blk.gpsimd
                def _(g):
                    wait_snap(g, snap5)
                    g.dma_start(out=bv_b[:], in_=bv_row[:, :]).then_inc(wl.h, wl.tick(16))
                    for part, wsrc in enumerate([wq_f, wk_f, wv_f]):
                        b = part % 2
                        if part == 2:
                            g.wait_ge(pe.h, pe_q[-1])
                        src = wsrc[:, :].rearrange("(kc p) c -> p kc c", p=128)
                        dst = wbuf[b][:].rearrange("p (kc c) -> p kc c", kc=WC)
                        g.dma_start(out=dst, in_=src).then_inc(wl.h, wl.tick(16))
                assert wl.n == wl_w[-1]

                @blk.tensor
                def _(t):
                    for part in range(2):
                        t.wait_ge(ll.h, ll_src[part])
                        t.wait_ge(wl.h, wl_w[part])
                        base_pe = [pe_q, pe_k][part]
                        for fc in range(WC):
                            i = part * WC + fc
                            b = i % 2
                            if i >= 2:
                                t.wait_ge(dv.h, dv_seq[i - 2])
                            mm = None
                            for kc in range(WC):
                                for tg in range(4):
                                    mm = t.matmul(
                                        ps_f[b][:, tg * 512:(tg + 1) * 512],
                                        wbuf[part][:, (kc * WC + fc) * 128:
                                                   (kc * WC + fc + 1) * 128],
                                        src_sb[:, kc * T + tg * 512:
                                               kc * T + (tg + 1) * 512],
                                        start=(kc == 0), stop=(kc == WC - 1))
                            mm.then_inc(pe.h, pe.tick())
                            assert pe.n == base_pe[fc]
                    t.wait_ge(ll.h, ll_src[2])
                    t.wait_ge(wl.h, wl_w[2])
                    for blki in range(NB):
                        i = 2 * WC + blki
                        b = i % 2
                        t.wait_ge(dv.h, dv_seq[i - 2])
                        mm = None
                        for nh in range(2):
                            for kc in range(WC):
                                t.matmul(ps_f[b][:, nh * 512:(nh + 1) * 512],
                                         src_sb[:, kc * T + blki * 128:
                                                kc * T + blki * 128 + 128],
                                         wbuf[0][:, kc * W + nh * 512:
                                                 kc * W + (nh + 1) * 512],
                                         start=(kc == 0), stop=False)
                            mm = t.matmul(ps_f[b][:, nh * 512:(nh + 1) * 512],
                                          oner_b[0:1, :],
                                          bv_b[0:1, nh * 512:(nh + 1) * 512],
                                          start=False, stop=True)
                        mm.then_inc(pe.h, pe.tick())
                        assert pe.n == pe_v[blki]

                @blk.vector
                def _(v):
                    for part in range(2):
                        base_pe = [pe_q, pe_k][part]
                        dst = [qh_sb, kh_sb][part]
                        for fc in range(WC):
                            i = part * WC + fc
                            b = i % 2
                            v.wait_ge(pe.h, base_pe[fc])
                            if part == 0:
                                v.tensor_scalar(dst[:, fc * T:(fc + 1) * T], ps_f[b][:],
                                                0.125, bqs_sb[:, fc:fc + 1],
                                                op0=ALU.mult, op1=ALU.add)\
                                    .then_inc(dv.h, dv.tick())
                            else:
                                v.tensor_scalar(dst[:, fc * T:(fc + 1) * T], ps_f[b][:],
                                                bkt_sb[:, fc:fc + 1], None, op0=ALU.add)\
                                    .then_inc(dv.h, dv.tick())
                            assert dv.n == dv_seq[i]
                    for blki in range(NB):
                        i = 2 * WC + blki
                        b = i % 2
                        v.wait_ge(pe.h, pe_v[blki])
                        v.tensor_copy(vh_sb[:, blki * W:(blki + 1) * W], ps_f[b][:, 0:W])\
                            .then_inc(dv.h, dv.tick())
                        assert dv.n == dv_seq[i]

            if dbg_qh is not None:
                with nc.Block("p5dbg") as blk:
                    @blk.sync
                    def _(s):
                        s.wait_ge(dv.h, dv.n)
                        s.dma_start(out=dbg_qh[:, 0:WC * T], in_=qh_sb[:])\
                            .then_inc(so.h, so.tick(16))
                        s.dma_start(out=dbg_qh[:, WC * T:2 * WC * T], in_=kh_sb[:])\
                            .then_inc(so.h, so.tick(16))
                        s.dma_start(out=dbg_qh[:, 2 * WC * T:3 * WC * T], in_=vh_sb[:])\
                            .then_inc(so.h, so.tick(16))

            if upto <= 4.5:
                _final_wait(nc, so)
                return nc

            with nc.psum_tensor("ps_a0", [128, 128], F32) as ps_a0, \
                 nc.psum_tensor("ps_a1", [128, 128], F32) as ps_a1, \
                 nc.psum_tensor("ps_a2", [128, 128], F32) as ps_a2, \
                 nc.psum_tensor("ps_a3", [128, 128], F32) as ps_a3, \
                 nc.psum_tensor("ps_b0", [128, 128], F32) as ps_b0, \
                 nc.psum_tensor("ps_b1", [128, 128], F32) as ps_b1, \
                 nc.psum_tensor("ps_b2", [128, 128], F32) as ps_b2, \
                 nc.psum_tensor("ps_b3", [128, 128], F32) as ps_b3, \
                 nc.Block("p5d") as blk:
                ps_set = [[ps_a0, ps_a1, ps_a2, ps_a3], [ps_b0, ps_b1, ps_b2, ps_b3]]

                NG = NB * 4 if ng is None else ng
                pe0b, ac0b, dv0b = pe.n, ac.n, dv.n
                pe_sc, pe_tr, pe_oc = {}, {}, {}
                c = pe0b
                for i in range(NG):
                    c += 1
                    pe_sc[i] = c
                    if i >= 1:
                        g = i - 1
                        pe_tr[g] = [c + 1 + j for j in range(4)]
                        c += 5
                        pe_oc[g] = c
                g = NG - 1
                pe_tr[g] = [c + 1 + j for j in range(4)]
                c += 5
                pe_oc[g] = c
                ac_ex, ac_t1, ac_t2, ac_oe = {}, {}, {}, {}
                c = ac0b
                for i in range(NG):
                    c += 1
                    ac_ex[i] = c
                    if i >= 1:
                        g = i - 1
                        ac_t1[g] = c + 1
                        ac_t2[g] = c + 2
                        ac_oe[g] = c + 3
                        c += 3
                g = NG - 1
                ac_t1[g] = c + 1
                ac_t2[g] = c + 2
                ac_oe[g] = c + 3
                c += 3
                dv_mk = [dv0b + 5 * i + 2 for i in range(NG)]
                dv_pr = [dv0b + 5 * i + 5 for i in range(NG)]
                dv_last_v = dv_seq[-1]

                @blk.tensor
                def _(t):
                    t.wait_ge(dv.h, dv_last_v)

                    def emit_tro(g):
                        Sg = ps_set[g % 2]
                        blkg, hgg = g // 4, g % 4
                        t.wait_ge(dv.h, dv_pr[g])
                        for j in range(4):
                            t.transpose(Sg[j][:, :], p_f[g % 2][:, j * 128:(j + 1) * 128],
                                        eye_f[:, :]).then_inc(pe.h, pe.tick())
                            assert pe.n == pe_tr[g][j]
                        t.wait_ge(ac.h, ac_t2[g])
                        mm = None
                        for j in range(4):
                            h = hgg * 4 + j
                            mm = t.matmul(Sg[j][0:64, 0:128],
                                          vh_sb[:, blkg * W + h * 64:
                                                blkg * W + h * 64 + 64],
                                          pT_sb[g % 2][:, j * 128:(j + 1) * 128],
                                          start=True, stop=True)
                        mm.then_inc(pe.h, pe.tick())
                        assert pe.n == pe_oc[g]

                    for i in range(NG):
                        blki, hg = i // 4, i % 4
                        S = ps_set[i % 2]
                        if i >= 2:
                            t.wait_ge(ac.h, ac_oe[i - 2])
                        mm = None
                        for j in range(4):
                            h = hg * 4 + j
                            mm = t.matmul(S[j][:, :],
                                          hslice(qh_sb, h, blki), hslice(kh_sb, h, blki),
                                          start=True, stop=True)
                        mm.then_inc(pe.h, pe.tick())
                        assert pe.n == pe_sc[i]
                        if i >= 1:
                            emit_tro(i - 1)
                    emit_tro(NG - 1)

                @blk.vector
                def _(v):
                    v.wait_ge(ll.h, ll_mask)
                    for i in range(NG):
                        b = i % 2
                        S = ps_set[i % 2]
                        v.wait_ge(pe.h, pe_sc[i])
                        if i >= 2:
                            v.wait_ge(ac.h, ac_ex[i - 2])
                        tt = None
                        for j in range(4):
                            tt = v.tensor_tensor(s_sb[b][:, j * 128:(j + 1) * 128],
                                                 S[j][:, :], mask_sb[:, 0:128],
                                                 op=ALU.add)
                        vbar(v, tt)
                        rm = None
                        for j in range(4):
                            rm = v.reduce_max(atmp[:, j:j + 1],
                                              s_sb[b][:, j * 128:(j + 1) * 128],
                                              axis=AX, negate=True)
                        rm.then_inc(dv.h, dv.tick())
                        assert dv.n == dv_mk[i]
                        v.wait_ge(ac.h, ac_ex[i])
                        if i >= 2:
                            v.wait_ge(pe.h, pe_tr[i - 2][3])
                        rs = None
                        for j in range(4):
                            rs = v.reduce_sum(atmp[:, 4 + j:5 + j],
                                              e_sb[b][:, j * 128:(j + 1) * 128], axis=AX)
                        rs.then_inc(dv.h, dv.tick())
                        v.wait_ge(dv.h, dv.n)
                        v.reciprocal(atmp[:, 8:12], atmp[:, 4:8])\
                            .then_inc(dv.h, dv.tick())
                        v.wait_ge(dv.h, dv.n)
                        mmv = None
                        for j in range(4):
                            mmv = v.tensor_scalar(p_f[b][:, j * 128:(j + 1) * 128],
                                                  e_sb[b][:, j * 128:(j + 1) * 128],
                                                  atmp[:, 8 + j:9 + j], None, op0=ALU.mult)
                        mmv.then_inc(dv.h, dv.tick())
                        assert dv.n == dv_pr[i]

                @blk.scalar
                def _(sc):
                    def emit_evacs(g):
                        Sg = ps_set[g % 2]
                        blkg, hgg = g // 4, g % 4
                        for jp in range(2):
                            sc.wait_ge(pe.h, pe_tr[g][2 * jp + 1])
                            cc = None
                            for j in (2 * jp, 2 * jp + 1):
                                cc = sc.copy(pT_sb[g % 2][:, j * 128:(j + 1) * 128],
                                             Sg[j][:, :])
                            cc.then_inc(ac.h, ac.tick())
                        assert ac.n == ac_t2[g]
                        sc.wait_ge(pe.h, pe_oc[g])
                        cc = None
                        for j in range(4):
                            h = hgg * 4 + j
                            cc = sc.copy(oslice(h, blkg), Sg[j][0:64, 0:128])
                        cc.then_inc(ac.h, ac.tick())
                        assert ac.n == ac_oe[g]

                    for i in range(NG):
                        b = i % 2
                        sc.wait_ge(dv.h, dv_mk[i])
                        if i >= 2:
                            sc.wait_ge(dv.h, dv_pr[i - 2])
                        ee = None
                        for j in range(4):
                            ee = sc.activation(e_sb[b][:, j * 128:(j + 1) * 128],
                                               s_sb[b][:, j * 128:(j + 1) * 128],
                                               AF.Exp, bias=atmp[:, j:j + 1])
                        ee.then_inc(ac.h, ac.tick())
                        assert ac.n == ac_ex[i]
                        if i >= 1:
                            emit_evacs(i - 1)
                    emit_evacs(NG - 1)

            if dbg_o is not None:
                with nc.Block("p5dbg2") as blk:
                    @blk.sync
                    def _(s):
                        s.wait_ge(ac.h, ac.n)
                        s.dma_start(out=dbg_o[:, :], in_=src_sb[:])\
                            .then_inc(so.h, so.tick(16))

            if upto <= 4.7:
                _final_wait(nc, so)
                return nc

            with nc.psum_tensor("ps_w0", [128, T], F32) as ps_w0, \
                 nc.psum_tensor("ps_w1", [128, T], F32) as ps_w1, \
                 nc.Block("p5e") as blk:
                ps_f = [ps_w0, ps_w1]
                wl0, pe0, dv0c, so0 = wl.n, pe.n, dv.n, so.n
                pe_w = [pe0 + (i + 1) for i in range(WC)]
                ac_w = [dv0c + (i + 1) for i in range(WC)]
                so_w = [so0 + 16 * (i + 1) for i in range(WC)]

                @blk.gpsimd
                def _(g):
                    g.wait_ge(pe.h, pe0)
                    src = wo_f[:, :].rearrange("(kc p) c -> p kc c", p=128)
                    dst = wbuf[1][:].rearrange("p (kc c) -> p kc c", kc=WC)
                    g.dma_start(out=dst, in_=src).then_inc(wl.h, wl.tick(16))
                wl_wo = wl.n

                @blk.tensor
                def _(t):
                    t.wait_ge(wl.h, wl_wo)
                    t.wait_ge(ac.h, ac.n)
                    for fc in range(WC):
                        b = fc % 2
                        if fc >= 2:
                            t.wait_ge(dv.h, ac_w[fc - 2])
                        mm = None
                        for kc in range(WC):
                            for tg in range(4):
                                mm = t.matmul(ps_f[b][:, tg * 512:(tg + 1) * 512],
                                              wbuf[1][:, (kc * WC + fc) * 128:
                                                      (kc * WC + fc + 1) * 128],
                                              src_sb[:, kc * T + tg * 512:
                                                     kc * T + (tg + 1) * 512],
                                              start=(kc == 0), stop=(kc == WC - 1))
                        mm.then_inc(pe.h, pe.tick())
                        assert pe.n == pe_w[fc]

                @blk.vector
                def _(v):
                    for fc in range(WC):
                        b = fc % 2
                        v.wait_ge(pe.h, pe_w[fc])
                        if fc >= 2:
                            v.wait_ge(so.h, so_w[fc - 2])
                        v.tensor_scalar(obf[b][:], ps_f[b][:], bot_sb[:, fc:fc + 1],
                                        None, op0=ALU.add).then_inc(dv.h, dv.tick())
                        assert dv.n == ac_w[fc]

                @blk.sync
                def _(s):
                    for fc in range(WC):
                        b = fc % 2
                        s.wait_ge(dv.h, ac_w[fc])
                        s.dma_start(out=o2_hbm[fc * 128:(fc + 1) * 128, :],
                                    in_=obf[b][:]).then_inc(so.h, so.tick(16))
                assert so.n == so_w[-1]

        if upto <= 5:
            _final_wait(nc, so)
            return nc

        p7 = ExitStack()
        with p7:
            p7sb = lambda n_, s_, d_: p7.enter_context(nc.sbuf_tensor(n_, list(s_), d_))
            lhsb = [p7sb(f"lhsb{i}", (128, 8 * T), BF16) for i in range(2)]
            rhsb = [p7sb(f"rhsb{i}", (128, 8 * W), BF16) for i in range(2)]
            acc = p7sb("acc", (128, NB * W), F32)
            xbuf = [p7sb(f"xbuf{i}", (128, W), F32) for i in range(2)]
            oxb = [p7sb(f"oxb{i}", (128, W), F32) for i in range(2)]
            B_sb = p7sb("B_sb", (4, W), BF16)
            cxp = p7sb("cxp", (4, T), BF16)

            NGRP = 14
            snap7 = snapshot()
            with nc.psum_tensor("ps_t0", [128, W], F32) as ps_t0, \
                 nc.psum_tensor("ps_t1", [128, W], F32) as ps_t1, \
                 nc.psum_tensor("ps_t2", [128, W], F32) as ps_t2, \
                 nc.Block("p7") as blk:
                ps_t = [ps_t0, ps_t1, ps_t2]

                sl0, wl0, ll0, pe0, dv0, so0 = sl.n, wl.n, ll.n, pe.n, dv.n, so.n
                sl_lhs, wl_rhs = {}, {}
                pe_g, dv_g = {}, {}
                ll_B, ll_x, dv_fin, so_out = {}, {}, {}, {}
                sln, wln, lln, pen, dvn, son = sl0, wl0, ll0, pe0, dv0, so0
                wln += 16
                wl_cxp = wln
                gidx = 0
                for fh in range(4):
                    for g in range(13):
                        sln += 16 * 8
                        sl_lhs[(fh, g)] = sln
                        wln += 16
                        wl_rhs[(fh, g)] = wln
                    wln += 16
                    ll_B[fh] = wln
                    for g in range(NGRP):
                        for tt in range(NB):
                            pen += 1
                            pe_g[gidx] = pen
                            dvn += 1
                            dv_g[gidx] = dvn
                            gidx += 1
                    for tt in range(NB):
                        lln += 16
                        ll_x[(fh, tt)] = lln
                        dvn += 1
                        dv_fin[(fh, tt)] = dvn
                        son += 16
                        so_out[(fh, tt)] = son
                NGI = gidx

                def grp_src(g):
                    if g < 12:
                        e, kk = g // 4, g % 4
                        lhs_rows = [(e * KC + kk * 8 + i) * 128 for i in range(8)]
                        return ("act", lhs_rows, w2_e[e], kk * 8)
                    else:
                        lhs_rows = [i * 128 for i in range(8)]
                        return ("o2", lhs_rows, proj_w, 0)

                @blk.scalar
                def _(sc):
                    wait_snap(sc, snap7)
                    for fh in range(4):
                        for g in range(13):
                            lg = fh * 13 + g
                            gb = lg % 2
                            if lg >= 2:
                                fh2, g2 = divmod(lg - 2, 13)
                                sc.wait_ge(pe.h, pe_g[(fh2 * NGRP + g2) * NB + NB - 1])
                            kind, rows_, _, _ = grp_src(g)
                            srcten = act_hbm if kind == "act" else o2_hbm
                            for i2, r0 in enumerate(rows_):
                                sc.dma_start(out=lhsb[gb][:, i2 * T:(i2 + 1) * T],
                                             in_=srcten[r0:r0 + 128, :])\
                                    .then_inc(sl.h, sl.tick(16))
                            assert sl.n == sl_lhs[(fh, g)]

                @blk.gpsimd
                def _(g_):
                    wait_snap(g_, snap7)
                    g_.dma_start(out=cxp[0:4, :], in_=cx_hbm[0:4, :])\
                        .then_inc(wl.h, wl.tick(16))
                    assert wl.n == wl_cxp
                    for fh in range(4):
                        for g in range(13):
                            lg = fh * 13 + g
                            gb = lg % 2
                            if lg >= 2:
                                fh2, g2 = divmod(lg - 2, 13)
                                g_.wait_ge(pe.h, pe_g[(fh2 * NGRP + g2) * NB + NB - 1])
                            _, _, wten, kbase = grp_src(g)
                            src = wten[kbase * 128:(kbase + 8) * 128,
                                       fh * W:(fh + 1) * W]
                            src = src.rearrange("(kc p) c -> p kc c", p=128)
                            dst = rhsb[gb][:].rearrange("p (kc c) -> p kc c", kc=8)
                            g_.dma_start(out=dst, in_=src).then_inc(wl.h, wl.tick(16))
                            assert wl.n == wl_rhs[(fh, g)]
                        if fh > 0:
                            g_.wait_ge(pe.h, pe_g[((fh - 1) * NGRP + 13) * NB + NB - 1])
                        g_.dma_start(out=B_sb[0:4, :],
                                     in_=Bmat[0:4, fh * W:(fh + 1) * W])\
                            .then_inc(wl.h, wl.tick(16))
                        assert wl.n == ll_B[fh]

                @blk.sync
                def _(s):
                    wait_snap(s, snap7)
                    for fh in range(4):
                        for tt in range(NB):
                            bx = tt % 2
                            if fh > 0 or tt >= 2:
                                pf, pt = (fh, tt - 2) if tt >= 2 else (fh - 1, tt + NB - 2)
                                s.wait_ge(dv.h, dv_fin[(pf, pt)])
                            s.dma_start(out=xbuf[bx][:],
                                        in_=x_tm[tt * 128:(tt + 1) * 128,
                                                 fh * W:(fh + 1) * W])\
                                .then_inc(ll.h, ll.tick(16))
                            assert ll.n == ll_x[(fh, tt)]
                            s.wait_ge(dv.h, dv_fin[(fh, tt)])
                            s.dma_start(out=out[tt * 128:(tt + 1) * 128,
                                                fh * W:(fh + 1) * W],
                                        in_=oxb[bx][:]).then_inc(so.h, so.tick(16))
                            assert so.n == so_out[(fh, tt)]

                @blk.tensor
                def _(t):
                    wait_snap(t, snap7)
                    t.wait_ge(wl.h, wl_cxp)
                    for fh in range(4):
                        for g in range(NGRP):
                            gb = (fh * 13 + g) % 2 if g < 13 else 0
                            if g < 13:
                                t.wait_ge(sl.h, sl_lhs[(fh, g)])
                                t.wait_ge(wl.h, wl_rhs[(fh, g)])
                            else:
                                t.wait_ge(wl.h, ll_B[fh])
                            for tt in range(NB):
                                gi = (fh * NGRP + g) * NB + tt
                                slot = gi % 3
                                if gi >= 3:
                                    t.wait_ge(dv.h, dv_g[gi - 3])
                                mm = None
                                if g < 13:
                                    for kc in range(8):
                                        for nh in range(2):
                                            mm = t.matmul(
                                                ps_t[slot][:, nh * 512:(nh + 1) * 512],
                                                lhsb[gb][:, kc * T + tt * 128:
                                                         kc * T + tt * 128 + 128],
                                                rhsb[gb][:, kc * W + nh * 512:
                                                         kc * W + (nh + 1) * 512],
                                                start=(kc == 0), stop=(kc == 7))
                                else:
                                    for nh in range(2):
                                        mm = t.matmul(
                                            ps_t[slot][:, nh * 512:(nh + 1) * 512],
                                            cxp[0:4, tt * 128:(tt + 1) * 128],
                                            B_sb[0:4, nh * 512:(nh + 1) * 512],
                                            start=True, stop=True)
                                mm.then_inc(pe.h, pe.tick())
                                assert pe.n == pe_g[gi]

                @blk.vector
                def _(v):
                    wait_snap(v, snap7)
                    for fh in range(4):
                        for g in range(NGRP):
                            for tt in range(NB):
                                gi = (fh * NGRP + g) * NB + tt
                                slot = gi % 3
                                v.wait_ge(pe.h, pe_g[gi])
                                aslice = acc[:, tt * W:(tt + 1) * W]
                                if g == 0:
                                    v.tensor_copy(aslice, ps_t[slot][:])\
                                        .then_inc(dv.h, dv.tick())
                                else:
                                    v.tensor_tensor(aslice, ps_t[slot][:], aslice,
                                                    op=ALU.add).then_inc(dv.h, dv.tick())
                                assert dv.n == dv_g[gi]
                        for tt in range(NB):
                            bx = tt % 2
                            v.wait_ge(ll.h, ll_x[(fh, tt)])
                            if fh > 0 or tt >= 2:
                                pf, pt = (fh, tt - 2) if tt >= 2 else (fh - 1, tt + NB - 2)
                                v.wait_ge(so.h, so_out[(pf, pt)])
                            v.tensor_tensor(oxb[bx][:], acc[:, tt * W:(tt + 1) * W],
                                            xbuf[bx][:], op=ALU.add)\
                                .then_inc(dv.h, dv.tick())
                            assert dv.n == dv_fin[(fh, tt)]

        _final_wait(nc, so)
        return nc


def _final_wait(nc, so):
    with nc.Block("fin") as blk:
        @blk.sync
        def _(s):
            s.wait_ge(so.h, so.n)



def _prep_common(inputs):
    f = np.float32
    cw = np.ascontiguousarray(np.asarray(inputs["L0_attn_conv_w"], f))
    conv_b = np.asarray(inputs["L0_attn_conv_b"], f)
    wq = np.asarray(inputs["L0_attn_wq"], f).reshape(W, W)
    wk = np.asarray(inputs["L0_attn_wk"], f).reshape(W, W)
    wv = np.asarray(inputs["L0_attn_wv"], f).reshape(W, W)
    bq = np.asarray(inputs["L0_attn_bq"], f).reshape(-1)
    bk = np.asarray(inputs["L0_attn_bk"], f).reshape(-1)
    bv = np.asarray(inputs["L0_attn_bv"], f).reshape(-1)
    wo = np.asarray(inputs["L0_attn_wo"], f).reshape(W, W)
    bo = np.asarray(inputs["L0_attn_bo"], f).reshape(-1)
    pw = np.ascontiguousarray(np.asarray(inputs["L0_attn_proj_w"], f))
    pb = np.asarray(inputs["L0_attn_proj_b"], f).reshape(-1)
    gw = np.asarray(inputs["L0_ffn_gate_w"], f)
    gb = np.asarray(inputs["L0_ffn_gate_b"], f).reshape(-1)
    w1 = np.asarray(inputs["L0_ffn_w1"], f)
    b1 = np.asarray(inputs["L0_ffn_b1"], f)
    w2 = np.asarray(inputs["L0_ffn_w2"], f)
    b2 = np.asarray(inputs["L0_ffn_b2"], f)

    tril = np.tril(np.ones((128, 128), dtype=bool))
    mask = np.where(tril, np.float32(0.0), np.float32(-1.0e9))

    common = {
        "conv_w0": np.ascontiguousarray(cw[0]),
        "conv_w1": np.ascontiguousarray(cw[1]),
        "conv_w2": np.ascontiguousarray(cw[2]),
        "conv_bt": np.ascontiguousarray(conv_b.reshape(QC, 128).T),
        "wq_f": wq, "wk_f": wk, "wv_f": wv, "wo_f": wo,
        "bqt": np.ascontiguousarray(bq.reshape(WC, 128).T),
        "bkt": np.ascontiguousarray(bk.reshape(WC, 128).T),
        "bot": np.ascontiguousarray(bo.reshape(WC, 128).T),
        "bv_row": np.ascontiguousarray(bv.reshape(1, W)),
        "proj_w": pw,
        "w1_0": np.ascontiguousarray(w1[0]),
        "w1_1": np.ascontiguousarray(w1[1]),
        "w1_2": np.ascontiguousarray(w1[2]),
        "w2_0": np.ascontiguousarray(w2[0]),
        "w2_1": np.ascontiguousarray(w2[1]),
        "w2_2": np.ascontiguousarray(w2[2]),
        "b1t": np.ascontiguousarray(
            b1.reshape(NE, KC, 128).transpose(2, 0, 1).reshape(128, NE * KC)),
        "gwa_t": np.ascontiguousarray(
            gw[:D].reshape(KC, 128, NE).transpose(1, 0, 2).reshape(128, KC * NE)),
        "gwb_col": np.ascontiguousarray(gw[D].reshape(NE, 1)),
        "gate_b_col": np.ascontiguousarray(gb.reshape(NE, 1)),
        "Bmat": np.ascontiguousarray(np.stack([pb, b2[0], b2[1], b2[2]])),
        "eye": np.eye(128, dtype=f),
        "mask4": np.ascontiguousarray(np.tile(mask, (1, 4))),
        "ones_col": np.ones((128, 1), f),
        "ones_row": np.ones((1, 128), f),
    }
    return common


def kernel(**inputs):
    x = np.ascontiguousarray(np.asarray(inputs["L0_attn_inputs"], np.float32))
    common = _prep_common(inputs)
    in_maps = []
    for c in range(NCORE):
        m = dict(common)
        m["x_tm"] = np.ascontiguousarray(x[c])
        m["x_fm"] = np.ascontiguousarray(x[c].T)
        in_maps.append(m)
    nc = build_nc()
    res = run_bass_kernel_spmd(nc, in_maps, core_ids=list(range(NCORE)))
    return np.stack([res.results[c]["out"] for c in range(NCORE)], axis=0)


# version 03
# speedup vs baseline: 1.0502x; 1.0348x over previous
import numpy as np
from contextlib import ExitStack

import concourse.bass as bass
import concourse.mybir as mybir
from concourse.bass_utils import run_bass_kernel_spmd

F32 = mybir.dt.float32
BF16 = mybir.dt.bfloat16
AF = mybir.ActivationFunctionType
ALU = mybir.AluOpType
AX = mybir.AxisListType.X

T = 2048
D = 4096
KC = 32
W = 1024
WC = 8
QC = 24
NB = 16
H = 16
DH = 64
NE = 3
VIS = 614
XNW = T + 4
NCORE = 8


class SemC:
    def __init__(self, h):
        self.h = h
        self.n = 0

    def inc(self, k=1):
        self.n += k
        return self.n

    def tick(self, k=1):
        self.n += k
        return k


def build_nc(upto=7, dbg=(), ng=None):
    dbg = set(dbg)
    nc = bass.Bass("TRN2", target_bir_lowering=False, debug=False)

    def din(name, shape, dtype=F32):
        return nc.dram_tensor(name, list(shape), dtype, kind="ExternalInput").ap()

    x_tm = din("x_tm", (T, D))
    x_fm = din("x_fm", (D, T))
    conv_wt = [din(f"conv_w{i}", (D, 3 * W)) for i in range(3)]
    conv_bt = din("conv_bt", (128, QC))
    wq_f = din("wq_f", (W, W))
    wk_f = din("wk_f", (W, W))
    wv_f = din("wv_f", (W, W))
    wo_f = din("wo_f", (W, W))
    bqt = din("bqt", (128, WC))
    bkt = din("bkt", (128, WC))
    bot = din("bot", (128, WC))
    bv_row = din("bv_row", (1, W))
    proj_w = din("proj_w", (W, D))
    w1_e = [din(f"w1_{e}", (D, D)) for e in range(NE)]
    w2_e = [din(f"w2_{e}", (D, D)) for e in range(NE)]
    b1t = din("b1t", (128, NE * KC))
    gwa_t = din("gwa_t", (128, KC * NE))
    gwb_col = din("gwb_col", (NE, 1))
    gate_b_col = din("gate_b_col", (NE, 1))
    Bmat = din("Bmat", (4, D))
    eye = din("eye", (128, 128))
    mask4_d = din("mask4", (128, 512))
    ones_col = din("ones_col", (128, 1))
    ones_row = din("ones_row", (1, 128))

    out = nc.dram_tensor("out", [T, D], F32, kind="ExternalOutput").ap()

    def scratch(name, shape, dtype):
        kind = "ExternalOutput" if name in dbg else "Internal"
        return nc.dram_tensor(name, list(shape), dtype, kind=kind).ap()

    qkv_hbm = scratch("qkv_hbm", (3 * W, T), BF16)
    act_hbm = scratch("act_hbm", (NE * D, T), BF16)
    o2_hbm = scratch("o2_hbm", (W, T), BF16)
    cx_hbm = scratch("cx_hbm", (8, T), F32)
    dbg_small = None
    if "small" in dbg:
        dbg_small = nc.dram_tensor("dbg_small", [128, 4096], F32, kind="ExternalOutput").ap()
    dbg_xn = None
    if "xn" in dbg:
        dbg_xn = nc.dram_tensor("dbg_xn", [128, KC * XNW], BF16, kind="ExternalOutput").ap()
    dbg_qh = None
    if "qh" in dbg:
        dbg_qh = nc.dram_tensor("dbg_qh", [128, 3 * WC * T], BF16, kind="ExternalOutput").ap()
    dbg_o = None
    if "o" in dbg:
        dbg_o = nc.dram_tensor("dbg_o", [128, WC * T], BF16, kind="ExternalOutput").ap()

    es = ExitStack()
    with es:
        wl = SemC(es.enter_context(nc.semaphore("wl")))
        ll = SemC(es.enter_context(nc.semaphore("ll")))
        sl = SemC(es.enter_context(nc.semaphore("sl")))
        pe = SemC(es.enter_context(nc.semaphore("pe")))
        ac = SemC(es.enter_context(nc.semaphore("ac")))
        dv = SemC(es.enter_context(nc.semaphore("dv")))
        so = SemC(es.enter_context(nc.semaphore("so")))
        all_sems = [wl, ll, sl, pe, ac, dv, so]

        def snapshot():
            return [(s_.h, s_.n) for s_ in all_sems]

        def vbar(v, inst):
            inst.then_inc(dv.h, dv.tick())
            v.wait_ge(dv.h, dv.n)
            return inst

        def wait_snap(eng, snap):
            for h_, n_ in snap:
                if n_ > 0:
                    eng.wait_ge(h_, n_)

        sbuf = lambda name, shape, dt_: es.enter_context(nc.sbuf_tensor(name, list(shape), dt_))

        eye_f = sbuf("eye_f", (128, 128), F32)
        eye_b = sbuf("eye_b", (128, 128), BF16)
        onec_f = sbuf("onec_f", (128, 1), F32)
        onec_b = sbuf("onec_b", (128, 1), BF16)
        oner_f = sbuf("oner_f", (1, 128), F32)
        oner_b = sbuf("oner_b", (1, 128), BF16)
        conv_bt_sb = sbuf("conv_bt_sb", (128, QC), F32)
        bqs_sb = sbuf("bqs_sb", (128, WC), F32)
        bkt_sb = sbuf("bkt_sb", (128, WC), F32)
        bot_sb = sbuf("bot_sb", (128, WC), F32)
        b1t_sb = sbuf("b1t_sb", (128, NE * KC), F32)
        gwb_sb = sbuf("gwb_sb", (NE, 1), F32)
        gb_sb = sbuf("gb_sb", (NE, 1), F32)
        xn_stack = ExitStack()
        xn = xn_stack.enter_context(nc.sbuf_tensor("xn", [128, KC * XNW], BF16))

        with nc.sbuf_tensor("bqt_tmp", [128, WC], F32) as bqt_tmp, \
             nc.Block("p0") as blk:

            @blk.gpsimd
            def _(g):
                g.dma_start(out=eye_b[:], in_=eye[:, :]).then_inc(wl.h, wl.tick(16))
                g.dma_start(out=onec_b[:], in_=ones_col[:, :]).then_inc(wl.h, wl.tick(16))
                g.dma_start(out=oner_b[:], in_=ones_row[:, :]).then_inc(wl.h, wl.tick(16))

            @blk.sync
            def _(s):
                for dst, src in [(eye_f, eye), (onec_f, ones_col), (oner_f, ones_row),
                                 (conv_bt_sb, conv_bt), (bqt_tmp, bqt), (bkt_sb, bkt),
                                 (bot_sb, bot), (b1t_sb, b1t), (gwb_sb, gwb_col),
                                 (gb_sb, gate_b_col)]:
                    s.dma_start(out=dst[:], in_=src[:, :]).then_inc(ll.h, ll.tick(16))

            @blk.vector
            def _(v):
                v.wait_ge(ll.h, ll.n)
                v.tensor_scalar(bqs_sb[:], bqt_tmp[:], 0.125, None, op0=ALU.mult)
                last = None
                for kc in range(KC):
                    v.memset(xn[:, kc * XNW: kc * XNW + 2], 0.0)
                    last = v.memset(xn[:, kc * XNW + 2 + T: (kc + 1) * XNW], 0.0)
                last.then_inc(dv.h, dv.tick())

        p1 = ExitStack()
        with p1:
            p1sb = lambda name, shape, dt_: p1.enter_context(
                nc.sbuf_tensor(name, list(shape), dt_))
            logits = p1sb("logits", (NE, T), F32)
            macc = p1sb("macc", (128, 4), F32)
            mrow = p1sb("mrow", (128, 8), F32)
            p1w = ExitStack()
            p1wsb = lambda name, shape, dt_: p1w.enter_context(
                nc.sbuf_tensor(name, list(shape), dt_))
            bc_a = p1wsb("bc_a", (128, T), F32)
            bc_b = p1wsb("bc_b", (128, T), F32)
            gwa_sb = p1wsb("gwa_sb", (128, KC * NE), F32)
            p1a_bufs = ExitStack()
            rows = p1a_bufs.enter_context(nc.sbuf_tensor("rows", [1, 3 * T], F32))
            xb = [p1a_bufs.enter_context(nc.sbuf_tensor(f"xb{i}", [128, T], BF16))
                  for i in range(2)]
            sq = [p1a_bufs.enter_context(nc.sbuf_tensor(f"sq{i}", [128, T], BF16))
                  for i in range(2)]

            wl0, pe0, ac0 = wl.n, pe.n, ac.n
            wl_xb = [wl0 + 16 * (k + 1) for k in range(KC)]
            pe_st = [pe0 + (k + 1) for k in range(KC)]
            ac_sq = [ac0 + (k + 1) for k in range(KC)]

            snap1a = snapshot()
            with nc.psum_tensor("ps_sums", [1, T], F32) as ps_sums, \
                 nc.psum_tensor("ps_sumsq", [1, T], F32) as ps_sumsq, \
                 nc.Block("p1a") as blk:

                @blk.gpsimd
                def _(g):
                    wait_snap(g, snap1a)
                    for kc in range(KC):
                        b = kc % 2
                        if kc >= 2:
                            g.wait_ge(pe.h, pe_st[kc - 2])
                        g.dma_start(out=xb[b][:], in_=x_fm[kc * 128:(kc + 1) * 128, :])\
                            .then_inc(wl.h, wl.tick(16))
                assert wl.n == wl_xb[-1]

                @blk.sync
                def _(s):
                    s.dma_start(out=gwa_sb[:], in_=gwa_t[:, :]).then_inc(ll.h, ll.tick(16))

                @blk.scalar
                def _(sc):
                    for kc in range(KC):
                        b = kc % 2
                        sc.wait_ge(wl.h, wl_xb[kc])
                        if kc >= 2:
                            sc.wait_ge(pe.h, pe_st[kc - 2])
                        sc.activation(sq[b][:], xb[b][:], AF.Square)\
                            .then_inc(ac.h, ac.tick())
                assert ac.n == ac_sq[-1]

                @blk.tensor
                def _(t):
                    for kc in range(KC):
                        b = kc % 2
                        t.wait_ge(wl.h, wl_xb[kc])
                        for tg in range(4):
                            t.matmul(ps_sums[0:1, tg * 512:(tg + 1) * 512],
                                     onec_b[:, 0:1], xb[b][:, tg * 512:(tg + 1) * 512],
                                     start=(kc == 0), stop=(kc == KC - 1))
                        t.wait_ge(ac.h, ac_sq[kc])
                        mm = None
                        for tg in range(4):
                            mm = t.matmul(ps_sumsq[0:1, tg * 512:(tg + 1) * 512],
                                          onec_b[:, 0:1], sq[b][:, tg * 512:(tg + 1) * 512],
                                          start=(kc == 0), stop=(kc == KC - 1))
                        mm.then_inc(pe.h, pe.tick())
                assert pe.n == pe_st[-1]

                @blk.vector
                def _(v):
                    v.memset(macc[:], 0.0).then_inc(dv.h, dv.tick())

                @blk.scalar
                def _(sc):
                    sc.wait_ge(pe.h, pe_st[-1])
                    sc.mul(rows[0:1, 0:T], ps_sums[0:1, :], 1.0 / D)
                    sc.mul(rows[0:1, T:2 * T], ps_sumsq[0:1, :], 1.0 / D)\
                        .then_inc(ac.h, ac.tick())
            ac_stats = ac.n

            with nc.psum_tensor("ps_bca", [128, T], F32) as ps_bca, \
                 nc.psum_tensor("ps_bcb", [128, T], F32) as ps_bcb, \
                 nc.Block("p1a2") as blk:

                @blk.vector
                def _(v):
                    v.wait_ge(ac.h, ac_stats)
                    vbar(v, v.tensor_tensor(rows[0:1, 2 * T:3 * T], rows[0:1, 0:T],
                                            rows[0:1, 0:T], op=ALU.mult))
                    vbar(v, v.tensor_tensor(rows[0:1, T:2 * T], rows[0:1, T:2 * T],
                                            rows[0:1, 2 * T:3 * T], op=ALU.subtract))
                    vbar(v, v.tensor_scalar(rows[0:1, T:2 * T], rows[0:1, T:2 * T],
                                            1e-5, None, op0=ALU.add))
                dv_var = dv.n

                @blk.scalar
                def _(sc):
                    sc.wait_ge(dv.h, dv_var)
                    sc.activation(rows[0:1, 2 * T:3 * T], rows[0:1, T:2 * T],
                                  AF.Sqrt).then_inc(ac.h, ac.tick())
                ac_sd = ac.n

                @blk.vector
                def _(v):
                    v.wait_ge(ac.h, ac_sd)
                    vbar(v, v.reciprocal(rows[0:1, T:2 * T], rows[0:1, 2 * T:3 * T]))
                    vbar(v, v.scalar_tensor_tensor(rows[0:1, 2 * T:3 * T], rows[0:1, 0:T],
                                                   -1.0, rows[0:1, T:2 * T],
                                                   op0=ALU.mult, op1=ALU.mult))
                dv_rstd = dv.n

                @blk.tensor
                def _(t):
                    t.wait_ge(dv.h, dv_rstd)
                    t.wait_ge(ac.h, ac_stats)
                    for tg in range(4):
                        t.matmul(ps_bca[:, tg * 512:(tg + 1) * 512], oner_f[0:1, :],
                                 rows[0:1, T + tg * 512: T + (tg + 1) * 512],
                                 start=True, stop=True)
                    mm = None
                    for tg in range(4):
                        mm = t.matmul(ps_bcb[:, tg * 512:(tg + 1) * 512], oner_f[0:1, :],
                                      rows[0:1, 2 * T + tg * 512: 2 * T + (tg + 1) * 512],
                                      start=True, stop=True)
                    mm.then_inc(pe.h, pe.tick())
                pe_bc = pe.n

                @blk.vector
                def _(v):
                    v.wait_ge(pe.h, pe_bc)
                    v.tensor_copy(bc_a[:], ps_bca[:])
                    v.tensor_copy(bc_b[:], ps_bcb[:]).then_inc(dv.h, dv.tick())
            dv_bc = dv.n

            p1a_bufs.close()
            xf = [p1w.enter_context(nc.sbuf_tensor(f"xf{i}", [128, T], F32))
                  for i in range(2)]
            xnf = [p1w.enter_context(nc.sbuf_tensor(f"xnf{i}", [128, T], F32))
                   for i in range(2)]
            ll0, dv0, ac0, pe0 = ll.n, dv.n, ac.n, pe.n
            ll_xf = [ll0 + 16 * (k + 1) for k in range(KC)]
            dv_tt1 = [dv0 + 4 * k + 1 for k in range(KC)]
            dv_tt2 = [dv0 + 4 * k + 2 for k in range(KC)]
            dv_mod = [dv0 + 4 * k + 4 for k in range(KC)]
            ac_cp = [ac0 + (k + 1) for k in range(KC)]
            pe_gate = [pe0 + (k + 1) for k in range(KC)]

            with nc.psum_tensor("ps_gate", [NE, T], F32) as ps_gate, \
                 nc.psum_tensor("ps_mv", [1, 1], F32) as ps_mv, \
                 nc.psum_tensor("ps_mt", [1, 1], F32) as ps_mt, \
                 nc.psum_tensor("ps_r3", [NE, 1], F32) as ps_r3, \
                 nc.Block("p1b") as blk:

                @blk.sync
                def _(s):
                    s.wait_ge(pe.h, pe_bc)
                    for kc in range(KC):
                        b = kc % 2
                        if kc >= 2:
                            s.wait_ge(dv.h, dv_tt1[kc - 2])
                        s.dma_start(out=xf[b][:], in_=x_fm[kc * 128:(kc + 1) * 128, :])\
                            .then_inc(ll.h, ll.tick(16))
                assert ll.n == ll_xf[-1]

                @blk.vector
                def _(v):
                    v.wait_ge(dv.h, dv_bc)
                    for kc in range(KC):
                        b = kc % 2
                        v.wait_ge(ll.h, ll_xf[kc])
                        if kc >= 2:
                            v.wait_ge(ac.h, ac_cp[kc - 2])
                            v.wait_ge(pe.h, pe_gate[kc - 2])
                        vbar(v, v.tensor_tensor(xnf[b][:], xf[b][:], bc_a[:],
                                                op=ALU.mult))
                        vbar(v, v.tensor_tensor(xnf[b][:], xnf[b][:], bc_b[:],
                                                op=ALU.add))
                        v.tensor_reduce(mrow[:, 2:3], xnf[b][:, 0:VIS], axis=AX,
                                        op=ALU.add, apply_absolute_value=True)
                        vbar(v, v.tensor_reduce(mrow[:, 3:4], xnf[b][:, VIS:T], axis=AX,
                                                op=ALU.add, apply_absolute_value=True))
                        v.tensor_tensor(macc[:, 0:1], macc[:, 0:1], mrow[:, 2:3], op=ALU.add)
                        v.tensor_tensor(macc[:, 1:2], macc[:, 1:2], mrow[:, 3:4], op=ALU.add)\
                            .then_inc(dv.h, dv.tick())
                assert dv.n == dv_mod[-1]

                @blk.scalar
                def _(sc):
                    for kc in range(KC):
                        b = kc % 2
                        sc.wait_ge(dv.h, dv_tt2[kc])
                        sc.copy(xn[:, kc * XNW + 2: kc * XNW + 2 + T], xnf[b][:])\
                            .then_inc(ac.h, ac.tick())
                assert ac.n == ac_cp[-1]

                @blk.tensor
                def _(t):
                    for kc in range(KC):
                        b = kc % 2
                        t.wait_ge(dv.h, dv_tt2[kc])
                        mm = None
                        for tg in range(4):
                            mm = t.matmul(ps_gate[0:NE, tg * 512:(tg + 1) * 512],
                                          gwa_sb[:, kc * NE:(kc + 1) * NE],
                                          xnf[b][:, tg * 512:(tg + 1) * 512],
                                          start=(kc == 0), stop=(kc == KC - 1))
                        mm.then_inc(pe.h, pe.tick())
                assert pe.n == pe_gate[-1]

                @blk.tensor
                def _(t):
                    t.wait_ge(dv.h, dv_mod[-1])
                    t.matmul(ps_mv[0:1, 0:1], macc[:, 0:1], onec_f[:, 0:1],
                             start=True, stop=True)
                    t.matmul(ps_mt[0:1, 0:1], macc[:, 1:2], onec_f[:, 0:1],
                             start=True, stop=True).then_inc(pe.h, pe.tick())
                pe_mtot = pe.n

                @blk.scalar
                def _(sc):
                    sc.wait_ge(pe.h, pe_mtot)
                    sc.mul(mrow[0:1, 4:5], ps_mt[0:1, 0:1], 1.0 / ((T - VIS) * D))
                    sc.mul(mrow[0:1, 5:6], ps_mv[0:1, 0:1], 1.0 / (VIS * D))\
                        .then_inc(ac.h, ac.tick())
                ac_mm = ac.n

                @blk.vector
                def _(v):
                    v.wait_ge(ac.h, ac_mm)
                    v.tensor_scalar(mrow[0:1, 6:7], mrow[0:1, 5:6], 1e-8, None,
                                    op0=ALU.add).then_inc(dv.h, dv.tick())
                    v.wait_ge(dv.h, dv.n)
                    v.reciprocal(mrow[0:1, 7:8], mrow[0:1, 6:7]).then_inc(dv.h, dv.tick())
                    v.wait_ge(dv.h, dv.n)
                    v.tensor_tensor(mrow[0:1, 0:1], mrow[0:1, 4:5], mrow[0:1, 7:8],
                                    op=ALU.mult).then_inc(dv.h, dv.tick())
                dv_ratio = dv.n

                @blk.tensor
                def _(t):
                    t.wait_ge(dv.h, dv_ratio)
                    t.matmul(ps_r3[0:NE, 0:1], oner_f[0:1, 0:NE], mrow[0:1, 0:1],
                             start=True, stop=True).then_inc(pe.h, pe.tick())
                pe_r3 = pe.n

                @blk.vector
                def _(v):
                    v.wait_ge(pe.h, pe_r3)
                    v.tensor_copy(mrow[0:NE, 1:2], ps_r3[0:NE, 0:1])\
                        .then_inc(dv.h, dv.tick())
                    v.wait_ge(dv.h, dv.n)
                    v.scalar_tensor_tensor(mrow[0:NE, 2:3], gwb_sb[0:NE, 0:1],
                                           mrow[0:NE, 1:2], gb_sb[0:NE, 0:1],
                                           op0=ALU.mult, op1=ALU.add)\
                        .then_inc(dv.h, dv.tick())
                dv_b3 = dv.n

                @blk.vector
                def _(v):
                    v.wait_ge(dv.h, dv_b3)
                    v.wait_ge(pe.h, pe_gate[-1])
                    v.tensor_scalar(logits[0:NE, :], ps_gate[0:NE, :], mrow[0:NE, 2:3],
                                    None, op0=ALU.add).then_inc(dv.h, dv.tick())
                dv_logits = dv.n

                if dbg_small is not None:
                    @blk.sync
                    def _(s):
                        s.wait_ge(ac.h, ac.n)
                        s.wait_ge(dv.h, dv.n)
                        s.dma_start(out=dbg_small[0:NE, 0:T], in_=logits[:])\
                            .then_inc(so.h, so.tick(16))
                        s.dma_start(out=dbg_small[8:9, 0:8], in_=mrow[0:1, :])\
                            .then_inc(so.h, so.tick(16))
                        s.dma_start(out=dbg_small[16:17, 0:T], in_=bc_a[0:1, :])\
                            .then_inc(so.h, so.tick(16))
                        s.dma_start(out=dbg_small[17:18, 0:T], in_=bc_b[0:1, :])\
                            .then_inc(so.h, so.tick(16))

                if dbg_xn is not None:
                    @blk.sync
                    def _(s):
                        s.wait_ge(ac.h, ac_cp[-1])
                        s.dma_start(out=dbg_xn[:, :], in_=xn[:]).then_inc(so.h, so.tick(16))

            p1w.close()
            p2 = ExitStack()
            with p2:
                p2sb = lambda n_, s_, d_: p2.enter_context(nc.sbuf_tensor(n_, list(s_), d_))
                lg_tm = p2sb("lg_tm", (128, 64), F32)
                sc_tm = p2sb("sc_tm", (128, 64), F32)
                v01 = p2sb("v01", (128, 32), F32)
                ptmp = p2sb("ptmp", (128, 16), F32)
                es_t = p2sb("es_t", (128, 4), F32)
                oh = p2sb("oh", (2, 16), F32)
                m2_sb = p2sb("m2_sb", (2, 4), F32)
                vrows = p2sb("vrows", (2, T), F32)
                cxt = p2sb("cxt", (4, T), F32)
                csb2 = p2sb("csb2", (NE, T), F32)

                with nc.psum_tensor("ps_ta", [128, 4], F32) as ps_ta, \
                     nc.psum_tensor("ps_tb", [128, 4], F32) as ps_tb, \
                     nc.psum_tensor("ps_va", [2, 128], F32) as ps_va, \
                     nc.psum_tensor("ps_vb", [2, 128], F32) as ps_vb, \
                     nc.psum_tensor("ps_c", [NE, T], F32) as ps_c, \
                     nc.Block("p2") as blk:

                    pe0, ac0 = pe.n, ac.n
                    pe_ltr = [pe0 + (ti + 1) for ti in range(NB)]
                    ac_lev = [ac0 + (ti + 1) for ti in range(NB)]

                    @blk.tensor
                    def _(t):
                        t.wait_ge(dv.h, dv_logits)
                        for ti in range(NB):
                            ps = [ps_ta, ps_tb][ti % 2]
                            if ti >= 2:
                                t.wait_ge(ac.h, ac_lev[ti - 2])
                            t.transpose(ps[:, 0:NE], logits[0:NE, ti * 128:(ti + 1) * 128],
                                        eye_f[0:NE, 0:NE]).then_inc(pe.h, pe.tick())
                    assert pe.n == pe_ltr[-1]

                    @blk.scalar
                    def _(sc):
                        for ti in range(NB):
                            ps = [ps_ta, ps_tb][ti % 2]
                            sc.wait_ge(pe.h, pe_ltr[ti])
                            sc.copy(lg_tm[:, ti * 4: ti * 4 + NE], ps[:, 0:NE])\
                                .then_inc(ac.h, ac.tick())
                    assert ac.n == ac_lev[-1]

                    dv1 = dv.n
                    dv_nm = [dv1 + 11 * ti + 1 for ti in range(NB)]
                    dv_tl = [dv1 + 11 * ti + 11 for ti in range(NB)]
                    ac1 = ac.n
                    ac_exp = [ac1 + (ti + 1) for ti in range(NB)]

                    @blk.vector
                    def _(v):
                        for ti in range(NB):
                            lg = lg_tm[:, ti * 4: ti * 4 + NE]
                            v.wait_ge(ac.h, ac_lev[ti])
                            v.reduce_max(ptmp[:, 0:1], lg, axis=AX, negate=True)\
                                .then_inc(dv.h, dv.tick())
                            v.wait_ge(ac.h, ac_exp[ti])
                            vbar(v, v.reduce_sum(ptmp[:, 1:2], es_t[:, 0:NE], axis=AX))
                            vbar(v, v.reciprocal(ptmp[:, 2:3], ptmp[:, 1:2]))
                            vbar(v, v.tensor_scalar(sc_tm[:, ti * 4: ti * 4 + NE],
                                                    es_t[:, 0:NE], ptmp[:, 2:3], None,
                                                    op0=ALU.mult))
                            s3 = sc_tm[:, ti * 4: ti * 4 + NE]
                            v.reduce_max(ptmp[:, 3:4], s3, axis=AX)
                            v.tensor_reduce(ptmp[:, 4:5], s3, axis=AX, op=ALU.min)
                            vbar(v, v.reduce_sum(ptmp[:, 5:6], s3, axis=AX))
                            vbar(v, v.tensor_tensor(ptmp[:, 6:7], ptmp[:, 5:6],
                                                    ptmp[:, 3:4], op=ALU.subtract))
                            vbar(v, v.tensor_tensor(ptmp[:, 6:7], ptmp[:, 6:7],
                                                    ptmp[:, 4:5], op=ALU.subtract))
                            vbar(v, v.tensor_tensor(ptmp[:, 7:8], ptmp[:, 3:4],
                                                    ptmp[:, 6:7], op=ALU.add))
                            vbar(v, v.tensor_scalar(ptmp[:, 7:8], ptmp[:, 7:8], 1e-8,
                                                    None, op0=ALU.add))
                            vbar(v, v.reciprocal(ptmp[:, 8:9], ptmp[:, 7:8]))
                            v.tensor_tensor(v01[:, 2 * ti:2 * ti + 1], ptmp[:, 3:4],
                                            ptmp[:, 8:9], op=ALU.mult)
                            v.tensor_tensor(v01[:, 2 * ti + 1:2 * ti + 2], ptmp[:, 6:7],
                                            ptmp[:, 8:9], op=ALU.mult)\
                                .then_inc(dv.h, dv.tick())
                    assert dv.n == dv_tl[-1]

                    @blk.scalar
                    def _(sc):
                        for ti in range(NB):
                            lg = lg_tm[:, ti * 4: ti * 4 + NE]
                            sc.wait_ge(dv.h, dv_nm[ti])
                            if ti >= 1:
                                sc.wait_ge(dv.h, dv_tl[ti - 1])
                            sc.activation(es_t[:, 0:NE], lg, AF.Exp,
                                          bias=ptmp[:, 0:1]).then_inc(ac.h, ac.tick())
                    assert ac.n == ac_exp[-1]

                    @blk.vector
                    def _(v):
                        v.wait_ge(dv.h, dv_tl[NB - 1])
                        r0 = sc_tm[0:1, 0:NE]
                        vbar(v, v.reduce_max(ptmp[0:1, 9:10], r0, axis=AX))
                        vbar(v, v.tensor_reduce(ptmp[0:1, 10:11], r0, axis=AX,
                                                op=ALU.min))
                        vbar(v, v.tensor_scalar(oh[0:1, 0:NE], r0, ptmp[0:1, 9:10], None,
                                                op0=ALU.is_equal))
                        vbar(v, v.tensor_scalar(oh[0:1, 4:4 + NE], r0, ptmp[0:1, 10:11],
                                                None, op0=ALU.is_equal))
                        vbar(v, v.tensor_tensor(oh[0:1, 8:8 + NE], oh[0:1, 0:NE],
                                                oh[0:1, 4:4 + NE], op=ALU.add))
                        vbar(v, v.tensor_scalar(oh[0:1, 8:8 + NE], oh[0:1, 8:8 + NE],
                                                1.0, -1.0, op0=ALU.subtract,
                                                op1=ALU.mult))
                    dv_oh = dv.n

                    @blk.sync
                    def _(s):
                        s.wait_ge(dv.h, dv_oh)
                        s.dma_start(out=m2_sb[0:1, 0:NE], in_=oh[0:1, 0:NE])\
                            .then_inc(ll.h, ll.tick(16))
                        s.dma_start(out=m2_sb[1:2, 0:NE], in_=oh[0:1, 8:8 + NE])\
                            .then_inc(ll.h, ll.tick(16))
                    ll_m2 = ll.n

                    pe1 = pe.n
                    pe_vtr = [pe1 + (ti + 1) for ti in range(NB)]
                    ac2 = ac.n
                    ac_vev = [ac2 + (ti + 1) for ti in range(NB)]

                    @blk.tensor
                    def _(t):
                        for ti in range(NB):
                            ps = [ps_va, ps_vb][ti % 2]
                            t.wait_ge(dv.h, dv_tl[ti])
                            if ti >= 2:
                                t.wait_ge(ac.h, ac_vev[ti - 2])
                            t.transpose(ps[0:2, :], v01[:, 2 * ti:2 * ti + 2],
                                        eye_f[:, :]).then_inc(pe.h, pe.tick())
                    assert pe.n == pe_vtr[-1]

                    @blk.scalar
                    def _(sc):
                        for ti in range(NB):
                            ps = [ps_va, ps_vb][ti % 2]
                            sc.wait_ge(pe.h, pe_vtr[ti])
                            sc.copy(vrows[0:2, ti * 128:(ti + 1) * 128], ps[0:2, :])\
                                .then_inc(ac.h, ac.tick())
                    assert ac.n == ac_vev[-1]

                    @blk.tensor
                    def _(t):
                        t.wait_ge(ll.h, ll_m2)
                        t.wait_ge(ac.h, ac_vev[-1])
                        mm = None
                        for tg in range(4):
                            mm = t.matmul(ps_c[0:NE, tg * 512:(tg + 1) * 512],
                                          m2_sb[0:2, 0:NE],
                                          vrows[0:2, tg * 512:(tg + 1) * 512],
                                          start=True, stop=True)
                        mm.then_inc(pe.h, pe.tick())
                    pe_c = pe.n

                    @blk.vector
                    def _(v):
                        v.wait_ge(pe.h, pe_c)
                        v.tensor_copy(csb2[0:NE, :], ps_c[0:NE, :])
                        v.memset(cxt[0:1, :], 1.0).then_inc(dv.h, dv.tick())
                    dv_c = dv.n

                    @blk.sync
                    def _(s):
                        s.wait_ge(dv.h, dv_c)
                        s.dma_start(out=cxt[1:4, :], in_=csb2[0:NE, :])\
                            .then_inc(ll.h, ll.tick(16))
                        s.wait_ge(ll.h, ll.n)
                        s.dma_start(out=cx_hbm[0:4, :], in_=cxt[0:4, :])\
                            .then_inc(so.h, so.tick(16))
                        s.dma_start(out=cx_hbm[4:7, :], in_=csb2[0:NE, :])\
                            .then_inc(so.h, so.tick(16))
                        if dbg_small is not None:
                            s.dma_start(out=dbg_small[24:28, 0:T], in_=cxt[0:4, :])\
                                .then_inc(so.h, so.tick(16))
                            s.dma_start(out=dbg_small[32:34, 0:16], in_=oh[0:2, :])\
                                .then_inc(so.h, so.tick(16))
                            s.dma_start(out=dbg_small[36:38, 0:T], in_=vrows[0:2, :])\
                                .then_inc(so.h, so.tick(16))
                            s.dma_start(out=dbg_small[40:41, 0:64], in_=sc_tm[0:1, :])\
                                .then_inc(so.h, so.tick(16))

        if upto <= 2:
            xn_stack.close()
            _final_wait(nc, so)
            return nc

        p3 = ExitStack()
        with p3:
            csb_row = p3.enter_context(nc.sbuf_tensor("csb_row", [1, NE * T], F32))
            w1g = [p3.enter_context(nc.sbuf_tensor(f"w1g{i}", [128, KC * 128], BF16))
                   for i in range(2)]
            cbc = p3.enter_context(nc.sbuf_tensor("cbc", [128, T], F32))
            stg = [p3.enter_context(nc.sbuf_tensor(f"stg{i}", [128, T], F32))
                   for i in range(2)]
            abf = [p3.enter_context(nc.sbuf_tensor(f"abf{i}", [128, T], BF16))
                   for i in range(2)]
            FUNC_E = [AF.Tanh, AF.Gelu, AF.Silu]

            snap3 = snapshot()
            with nc.psum_tensor("ps_f0", [128, T], F32) as ps_f0, \
                 nc.psum_tensor("ps_f1", [128, T], F32) as ps_f1, \
                 nc.Block("p3") as blk:
                ps_f = [ps_f0, ps_f1]

                wl0, pe0, ac0, dv0, so0 = wl.n, pe.n, ac.n, dv.n, so.n
                ll_csb = ll.n + 48
                wl_w1, pe_ch, pe_cbc, dv_cbc, ac_ev, dv_tt, so_st = [], [], [], [], [], [], []
                wn, pn, an, dn, sn = wl0, pe0, ac0, dv0, so0
                for e in range(NE):
                    pn += 1
                    pe_cbc.append(pn)
                    dn += 1
                    dv_cbc.append(dn)
                    for fc in range(KC):
                        wn += 32
                        wl_w1.append(wn)
                        pn += 1
                        pe_ch.append(pn)
                        an += 1
                        ac_ev.append(an)
                        dn += 1
                        dv_tt.append(dn)
                        sn += 16
                        so_st.append(sn)

                @blk.gpsimd
                def _(g):
                    wait_snap(g, snap3)
                    for e in range(NE):
                        for fc in range(KC):
                            i = e * KC + fc
                            b = i % 2
                            if i >= 2:
                                g.wait_ge(pe.h, pe_ch[i - 2])
                            for hh in range(2):
                                src = w1_e[e][hh * 2048:(hh + 1) * 2048,
                                              fc * 128:(fc + 1) * 128]
                                src = src.rearrange("(kc p) c -> p kc c", p=128)
                                dst = w1g[b][:, hh * 16 * 128:(hh + 1) * 16 * 128]
                                dst = dst.rearrange("p (kc c) -> p kc c", kc=16)
                                g.dma_start(out=dst, in_=src).then_inc(wl.h, wl.tick(16))
                assert wl.n == wl_w1[-1]

                @blk.tensor
                def _(t):
                    wait_snap(t, snap3)
                    t.wait_ge(ll.h, ll_csb)
                    for e in range(NE):
                        if e > 0:
                            t.wait_ge(ac.h, ac_ev[e * KC - 1])
                            t.wait_ge(ac.h, ac_ev[e * KC - 2])
                        mm = None
                        for tg in range(4):
                            mm = t.matmul(ps_f0[:, tg * 512:(tg + 1) * 512], oner_f[0:1, :],
                                          csb_row[0:1, e * T + tg * 512: e * T + (tg + 1) * 512],
                                          start=True, stop=True)
                        mm.then_inc(pe.h, pe.tick())
                        assert pe.n == pe_cbc[e]
                        for fc in range(KC):
                            i = e * KC + fc
                            b = i % 2
                            t.wait_ge(wl.h, wl_w1[i])
                            if fc == 0:
                                t.wait_ge(dv.h, dv_cbc[e])
                            if i >= 2:
                                t.wait_ge(ac.h, ac_ev[i - 2])
                            mm = None
                            for kc in range(KC):
                                for tg in range(4):
                                    mm = t.matmul(
                                        ps_f[b][:, tg * 512:(tg + 1) * 512],
                                        w1g[b][:, kc * 128:(kc + 1) * 128],
                                        xn[:, kc * XNW + 2 + tg * 512:
                                           kc * XNW + 2 + (tg + 1) * 512],
                                        start=(kc == 0), stop=(kc == KC - 1))
                            mm.then_inc(pe.h, pe.tick())
                            assert pe.n == pe_ch[i]

                @blk.vector
                def _(v):
                    for e in range(NE):
                        v.wait_ge(pe.h, pe_cbc[e])
                        v.tensor_copy(cbc[:], ps_f0[:]).then_inc(dv.h, dv.tick())
                        v.wait_ge(dv.h, dv.n)
                        assert dv.n == dv_cbc[e]
                        for fc in range(KC):
                            i = e * KC + fc
                            b = i % 2
                            v.wait_ge(ac.h, ac_ev[i])
                            if i >= 2:
                                v.wait_ge(so.h, so_st[i - 2])
                            v.tensor_tensor(abf[b][:], stg[b][:], cbc[:], op=ALU.mult)\
                                .then_inc(dv.h, dv.tick())
                            assert dv.n == dv_tt[i]

                @blk.scalar
                def _(sc):
                    for e in range(NE):
                        for fc in range(KC):
                            i = e * KC + fc
                            b = i % 2
                            sc.wait_ge(pe.h, pe_ch[i])
                            if i >= 2:
                                sc.wait_ge(dv.h, dv_tt[i - 2])
                            sc.activation(stg[b][:], ps_f[b][:], FUNC_E[e],
                                          bias=b1t_sb[:, e * KC + fc: e * KC + fc + 1])\
                                .then_inc(ac.h, ac.tick())
                            assert ac.n == ac_ev[i]

                @blk.sync
                def _(s):
                    wait_snap(s, snap3)
                    for e in range(NE):
                        s.dma_start(out=csb_row[0:1, e * T:(e + 1) * T],
                                    in_=cx_hbm[4 + e:5 + e, :])\
                            .then_inc(ll.h, ll.tick(16))
                    assert ll.n == ll_csb
                    for e in range(NE):
                        for fc in range(KC):
                            i = e * KC + fc
                            b = i % 2
                            s.wait_ge(dv.h, dv_tt[i])
                            s.dma_start(out=act_hbm[(e * KC + fc) * 128:
                                                    (e * KC + fc + 1) * 128, :],
                                        in_=abf[b][:]).then_inc(so.h, so.tick(16))
                            assert so.n == so_st[i]

        if upto <= 3:
            xn_stack.close()
            _final_wait(nc, so)
            return nc

        p4 = ExitStack()
        with p4:
            cw = [p4.enter_context(nc.sbuf_tensor(f"cw{i}", [128, 3 * KC * 128], BF16))
                  for i in range(2)]
            qbf = [p4.enter_context(nc.sbuf_tensor(f"qbf{i}", [128, T], BF16))
                   for i in range(2)]

            snap4 = snapshot()
            with nc.psum_tensor("ps_c0", [128, T], F32) as ps_c0, \
                 nc.psum_tensor("ps_c1", [128, T], F32) as ps_c1, \
                 nc.Block("p4") as blk:
                ps_f = [ps_c0, ps_c1]

                wl0, pe0, dv0, so0 = wl.n, pe.n, dv.n, so.n
                wl_cw = [wl0 + 96 * (i + 1) for i in range(QC)]
                pe_ch4 = [pe0 + (i + 1) for i in range(QC)]
                dv_ev4 = [dv0 + (i + 1) for i in range(QC)]
                so_st4 = [so0 + 16 * (i + 1) for i in range(QC)]

                @blk.gpsimd
                def _(g):
                    wait_snap(g, snap4)
                    for fc in range(QC):
                        b = fc % 2
                        if fc >= 2:
                            g.wait_ge(pe.h, pe_ch4[fc - 2])
                        for tap in range(3):
                            for hh in range(2):
                                src = conv_wt[tap][hh * 2048:(hh + 1) * 2048,
                                                   fc * 128:(fc + 1) * 128]
                                src = src.rearrange("(kc p) c -> p kc c", p=128)
                                dst = cw[b][:, (tap * KC + hh * 16) * 128:
                                            (tap * KC + hh * 16 + 16) * 128]
                                dst = dst.rearrange("p (kc c) -> p kc c", kc=16)
                                g.dma_start(out=dst, in_=src).then_inc(wl.h, wl.tick(16))
                assert wl.n == wl_cw[-1]

                @blk.tensor
                def _(t):
                    wait_snap(t, snap4)
                    for fc in range(QC):
                        b = fc % 2
                        t.wait_ge(wl.h, wl_cw[fc])
                        if fc >= 2:
                            t.wait_ge(dv.h, dv_ev4[fc - 2])
                        mm = None
                        for kc in range(KC):
                            for tap in range(3):
                                for tg in range(4):
                                    mm = t.matmul(
                                        ps_f[b][:, tg * 512:(tg + 1) * 512],
                                        cw[b][:, (tap * KC + kc) * 128:
                                              (tap * KC + kc + 1) * 128],
                                        xn[:, kc * XNW + 1 + tap + tg * 512:
                                           kc * XNW + 1 + tap + (tg + 1) * 512],
                                        start=(kc == 0 and tap == 0),
                                        stop=(kc == KC - 1 and tap == 2))
                        mm.then_inc(pe.h, pe.tick())
                assert pe.n == pe_ch4[-1]

                @blk.vector
                def _(v):
                    for fc in range(QC):
                        b = fc % 2
                        v.wait_ge(pe.h, pe_ch4[fc])
                        if fc >= 2:
                            v.wait_ge(so.h, so_st4[fc - 2])
                        v.tensor_scalar(qbf[b][:], ps_f[b][:], conv_bt_sb[:, fc:fc + 1],
                                        None, op0=ALU.add).then_inc(dv.h, dv.tick())
                assert dv.n == dv_ev4[-1]

                @blk.sync
                def _(s):
                    for fc in range(QC):
                        b = fc % 2
                        s.wait_ge(dv.h, dv_ev4[fc])
                        s.dma_start(out=qkv_hbm[fc * 128:(fc + 1) * 128, :],
                                    in_=qbf[b][:]).then_inc(so.h, so.tick(16))
                assert so.n == so_st4[-1]

        if upto <= 4:
            xn_stack.close()
            _final_wait(nc, so)
            return nc

        xn_stack.close()
        p5 = ExitStack()
        with p5:
            p5sb = lambda n_, s_, d_: p5.enter_context(nc.sbuf_tensor(n_, list(s_), d_))
            src_sb = p5sb("src_sb", (128, WC * T), BF16)
            qh_sb = p5sb("qh_sb", (128, WC * T), BF16)
            kh_sb = p5sb("kh_sb", (128, WC * T), BF16)
            vh_sb = p5sb("vh_sb", (128, NB * W), BF16)
            wbuf = [p5sb(f"wbuf{i}", (128, WC * W), BF16) for i in range(2)]
            mask_sb = p5sb("mask_sb", (128, 512), F32)
            bv_b = p5sb("bv_b", (1, W), BF16)
            s_sb = [p5sb(f"s_sb{i}", (128, 512), F32) for i in range(2)]
            e_sb = [p5sb(f"e_sb{i}", (128, 512), F32) for i in range(2)]
            p_f = [p5sb(f"p_f{i}", (128, 512), F32) for i in range(2)]
            pT_sb = [p5sb(f"pT_sb{i}", (128, 512), BF16) for i in range(2)]
            atmp = p5sb("atmp", (128, 12), F32)
            obf = [p5sb(f"obf{i}", (128, T), BF16) for i in range(2)]

            def oslice(h, blki):
                return src_sb[(h % 2) * 64:(h % 2) * 64 + 64,
                              (h // 2) * T + blki * 128:(h // 2) * T + blki * 128 + 128]

            def hslice(hsb, h, blki):
                return hsb[(h % 2) * 64:(h % 2) * 64 + 64,
                           (h // 2) * T + blki * 128:(h // 2) * T + blki * 128 + 128]

            snap5 = snapshot()
            with nc.psum_tensor("ps_q0", [128, T], F32) as ps_q0, \
                 nc.psum_tensor("ps_q1", [128, T], F32) as ps_q1, \
                 nc.Block("p5abc") as blk:
                ps_f = [ps_q0, ps_q1]

                ll0, wl0, pe0 = ll.n, wl.n, pe.n
                ac0b = ac.n
                ll_mask = ll0 + 16
                ll_src = [ll_mask + 16 * 8 * (p + 1) for p in range(3)]
                wl_w = [wl0 + 16 * (p + 2) for p in range(3)]
                dv0b = dv.n
                pe_q = [pe0 + (i + 1) for i in range(WC)]
                pe_k = [pe_q[-1] + (i + 1) for i in range(WC)]
                pe_v = [pe_k[-1] + (i + 1) for i in range(NB)]
                dv_seq = [dv0b + (i + 1) for i in range(2 * WC + NB)]

                @blk.sync
                def _(s):
                    wait_snap(s, snap5)
                    s.dma_start(out=mask_sb[:], in_=mask4_d[:, :]).then_inc(ll.h, ll.tick(16))
                    for part in range(3):
                        if part == 1:
                            s.wait_ge(pe.h, pe_q[-1])
                        if part == 2:
                            s.wait_ge(pe.h, pe_k[-1])
                        for kc in range(WC):
                            s.dma_start(out=src_sb[:, kc * T:(kc + 1) * T],
                                        in_=qkv_hbm[(part * WC + kc) * 128:
                                                    (part * WC + kc + 1) * 128, :])\
                                .then_inc(ll.h, ll.tick(16))
                assert ll.n == ll_src[-1]

                @blk.gpsimd
                def _(g):
                    wait_snap(g, snap5)
                    g.dma_start(out=bv_b[:], in_=bv_row[:, :]).then_inc(wl.h, wl.tick(16))
                    for part, wsrc in enumerate([wq_f, wk_f, wv_f]):
                        b = part % 2
                        if part == 2:
                            g.wait_ge(pe.h, pe_q[-1])
                        src = wsrc[:, :].rearrange("(kc p) c -> p kc c", p=128)
                        dst = wbuf[b][:].rearrange("p (kc c) -> p kc c", kc=WC)
                        g.dma_start(out=dst, in_=src).then_inc(wl.h, wl.tick(16))
                assert wl.n == wl_w[-1]

                @blk.tensor
                def _(t):
                    for part in range(2):
                        t.wait_ge(ll.h, ll_src[part])
                        t.wait_ge(wl.h, wl_w[part])
                        base_pe = [pe_q, pe_k][part]
                        for fc in range(WC):
                            i = part * WC + fc
                            b = i % 2
                            if i >= 2:
                                t.wait_ge(dv.h, dv_seq[i - 2])
                            mm = None
                            for kc in range(WC):
                                for tg in range(4):
                                    mm = t.matmul(
                                        ps_f[b][:, tg * 512:(tg + 1) * 512],
                                        wbuf[part][:, (kc * WC + fc) * 128:
                                                   (kc * WC + fc + 1) * 128],
                                        src_sb[:, kc * T + tg * 512:
                                               kc * T + (tg + 1) * 512],
                                        start=(kc == 0), stop=(kc == WC - 1))
                            mm.then_inc(pe.h, pe.tick())
                            assert pe.n == base_pe[fc]
                    t.wait_ge(ll.h, ll_src[2])
                    t.wait_ge(wl.h, wl_w[2])
                    for blki in range(NB):
                        i = 2 * WC + blki
                        b = i % 2
                        t.wait_ge(dv.h, dv_seq[i - 2])
                        mm = None
                        for nh in range(2):
                            for kc in range(WC):
                                t.matmul(ps_f[b][:, nh * 512:(nh + 1) * 512],
                                         src_sb[:, kc * T + blki * 128:
                                                kc * T + blki * 128 + 128],
                                         wbuf[0][:, kc * W + nh * 512:
                                                 kc * W + (nh + 1) * 512],
                                         start=(kc == 0), stop=False)
                            mm = t.matmul(ps_f[b][:, nh * 512:(nh + 1) * 512],
                                          oner_b[0:1, :],
                                          bv_b[0:1, nh * 512:(nh + 1) * 512],
                                          start=False, stop=True)
                        mm.then_inc(pe.h, pe.tick())
                        assert pe.n == pe_v[blki]

                @blk.vector
                def _(v):
                    for part in range(2):
                        base_pe = [pe_q, pe_k][part]
                        dst = [qh_sb, kh_sb][part]
                        for fc in range(WC):
                            i = part * WC + fc
                            b = i % 2
                            v.wait_ge(pe.h, base_pe[fc])
                            if part == 0:
                                v.tensor_scalar(dst[:, fc * T:(fc + 1) * T], ps_f[b][:],
                                                0.125, bqs_sb[:, fc:fc + 1],
                                                op0=ALU.mult, op1=ALU.add)\
                                    .then_inc(dv.h, dv.tick())
                            else:
                                v.tensor_scalar(dst[:, fc * T:(fc + 1) * T], ps_f[b][:],
                                                bkt_sb[:, fc:fc + 1], None, op0=ALU.add)\
                                    .then_inc(dv.h, dv.tick())
                            assert dv.n == dv_seq[i]
                    for blki in range(NB):
                        i = 2 * WC + blki
                        b = i % 2
                        v.wait_ge(pe.h, pe_v[blki])
                        v.tensor_copy(vh_sb[:, blki * W:(blki + 1) * W], ps_f[b][:, 0:W])\
                            .then_inc(dv.h, dv.tick())
                        assert dv.n == dv_seq[i]

            if dbg_qh is not None:
                with nc.Block("p5dbg") as blk:
                    @blk.sync
                    def _(s):
                        s.wait_ge(dv.h, dv.n)
                        s.dma_start(out=dbg_qh[:, 0:WC * T], in_=qh_sb[:])\
                            .then_inc(so.h, so.tick(16))
                        s.dma_start(out=dbg_qh[:, WC * T:2 * WC * T], in_=kh_sb[:])\
                            .then_inc(so.h, so.tick(16))
                        s.dma_start(out=dbg_qh[:, 2 * WC * T:3 * WC * T], in_=vh_sb[:])\
                            .then_inc(so.h, so.tick(16))

            if upto <= 4.5:
                _final_wait(nc, so)
                return nc

            with nc.psum_tensor("ps_a0", [128, 128], F32) as ps_a0, \
                 nc.psum_tensor("ps_a1", [128, 128], F32) as ps_a1, \
                 nc.psum_tensor("ps_a2", [128, 128], F32) as ps_a2, \
                 nc.psum_tensor("ps_a3", [128, 128], F32) as ps_a3, \
                 nc.psum_tensor("ps_b0", [128, 128], F32) as ps_b0, \
                 nc.psum_tensor("ps_b1", [128, 128], F32) as ps_b1, \
                 nc.psum_tensor("ps_b2", [128, 128], F32) as ps_b2, \
                 nc.psum_tensor("ps_b3", [128, 128], F32) as ps_b3, \
                 nc.Block("p5d") as blk:
                ps_set = [[ps_a0, ps_a1, ps_a2, ps_a3], [ps_b0, ps_b1, ps_b2, ps_b3]]

                NG = NB * 4 if ng is None else ng
                pe0b, ac0b, dv0b = pe.n, ac.n, dv.n
                pe_sc, pe_tr, pe_oc = {}, {}, {}
                c = pe0b
                for i in range(NG):
                    c += 1
                    pe_sc[i] = c
                    if i >= 1:
                        g = i - 1
                        pe_tr[g] = [c + 1 + j for j in range(4)]
                        c += 5
                        pe_oc[g] = c
                g = NG - 1
                pe_tr[g] = [c + 1 + j for j in range(4)]
                c += 5
                pe_oc[g] = c
                ac_ex, ac_t1, ac_t2, ac_oe = {}, {}, {}, {}
                c = ac0b
                for i in range(NG):
                    c += 1
                    ac_ex[i] = c
                    if i >= 1:
                        g = i - 1
                        ac_t1[g] = c + 1
                        ac_t2[g] = c + 2
                        ac_oe[g] = c + 3
                        c += 3
                g = NG - 1
                ac_t1[g] = c + 1
                ac_t2[g] = c + 2
                ac_oe[g] = c + 3
                c += 3
                dv_mk = [dv0b + 4 * i + 2 for i in range(NG)]
                dv_pr = [dv0b + 4 * i + 4 for i in range(NG)]
                dv_last_v = dv_seq[-1]

                @blk.tensor
                def _(t):
                    t.wait_ge(dv.h, dv_last_v)

                    def emit_tro(g):
                        Sg = ps_set[g % 2]
                        blkg, hgg = g // 4, g % 4
                        t.wait_ge(dv.h, dv_pr[g])
                        for j in range(4):
                            t.transpose(Sg[j][:, :], p_f[g % 2][:, j * 128:(j + 1) * 128],
                                        eye_f[:, :]).then_inc(pe.h, pe.tick())
                            assert pe.n == pe_tr[g][j]
                        t.wait_ge(ac.h, ac_t2[g])
                        mm = None
                        for j in range(4):
                            h = hgg * 4 + j
                            mm = t.matmul(Sg[j][0:64, 0:128],
                                          vh_sb[:, blkg * W + h * 64:
                                                blkg * W + h * 64 + 64],
                                          pT_sb[g % 2][:, j * 128:(j + 1) * 128],
                                          start=True, stop=True)
                        mm.then_inc(pe.h, pe.tick())
                        assert pe.n == pe_oc[g]

                    for i in range(NG):
                        blki, hg = i // 4, i % 4
                        S = ps_set[i % 2]
                        if i >= 2:
                            t.wait_ge(ac.h, ac_oe[i - 2])
                        mm = None
                        for j in range(4):
                            h = hg * 4 + j
                            mm = t.matmul(S[j][:, :],
                                          hslice(qh_sb, h, blki), hslice(kh_sb, h, blki),
                                          start=True, stop=True)
                        mm.then_inc(pe.h, pe.tick())
                        assert pe.n == pe_sc[i]
                        if i >= 1:
                            emit_tro(i - 1)
                    emit_tro(NG - 1)

                @blk.vector
                def _(v):
                    v.wait_ge(ll.h, ll_mask)
                    for i in range(NG):
                        b = i % 2
                        S = ps_set[i % 2]
                        v.wait_ge(pe.h, pe_sc[i])
                        if i >= 2:
                            v.wait_ge(ac.h, ac_ex[i - 2])
                        tt = None
                        for j in range(4):
                            tt = v.tensor_tensor(s_sb[b][:, j * 128:(j + 1) * 128],
                                                 S[j][:, :], mask_sb[:, 0:128],
                                                 op=ALU.add)
                        vbar(v, tt)
                        rm = None
                        for j in range(4):
                            rm = v.reduce_max(atmp[:, j:j + 1],
                                              s_sb[b][:, j * 128:(j + 1) * 128],
                                              axis=AX, negate=True)
                        rm.then_inc(dv.h, dv.tick())
                        assert dv.n == dv_mk[i]
                        v.wait_ge(ac.h, ac_ex[i])
                        if i >= 2:
                            v.wait_ge(pe.h, pe_tr[i - 2][3])
                        v.reciprocal(atmp[:, 8:12], atmp[:, 4:8])\
                            .then_inc(dv.h, dv.tick())
                        v.wait_ge(dv.h, dv.n)
                        mmv = None
                        for j in range(4):
                            mmv = v.tensor_scalar(p_f[b][:, j * 128:(j + 1) * 128],
                                                  e_sb[b][:, j * 128:(j + 1) * 128],
                                                  atmp[:, 8 + j:9 + j], None, op0=ALU.mult)
                        mmv.then_inc(dv.h, dv.tick())
                        assert dv.n == dv_pr[i]

                @blk.scalar
                def _(sc):
                    def emit_evacs(g):
                        Sg = ps_set[g % 2]
                        blkg, hgg = g // 4, g % 4
                        for jp in range(2):
                            sc.wait_ge(pe.h, pe_tr[g][2 * jp + 1])
                            cc = None
                            for j in (2 * jp, 2 * jp + 1):
                                cc = sc.copy(pT_sb[g % 2][:, j * 128:(j + 1) * 128],
                                             Sg[j][:, :])
                            cc.then_inc(ac.h, ac.tick())
                        assert ac.n == ac_t2[g]
                        sc.wait_ge(pe.h, pe_oc[g])
                        cc = None
                        for j in range(4):
                            h = hgg * 4 + j
                            cc = sc.copy(oslice(h, blkg), Sg[j][0:64, 0:128])
                        cc.then_inc(ac.h, ac.tick())
                        assert ac.n == ac_oe[g]

                    for i in range(NG):
                        b = i % 2
                        sc.wait_ge(dv.h, dv_mk[i])
                        if i >= 2:
                            sc.wait_ge(dv.h, dv_pr[i - 2])
                        ee = None
                        for j in range(4):
                            ee = sc.activation(e_sb[b][:, j * 128:(j + 1) * 128],
                                               s_sb[b][:, j * 128:(j + 1) * 128],
                                               AF.Exp, bias=atmp[:, j:j + 1],
                                               accum_out=atmp[:, 4 + j:5 + j])
                        ee.then_inc(ac.h, ac.tick())
                        assert ac.n == ac_ex[i]
                        if i >= 1:
                            emit_evacs(i - 1)
                    emit_evacs(NG - 1)

            if dbg_o is not None:
                with nc.Block("p5dbg2") as blk:
                    @blk.sync
                    def _(s):
                        s.wait_ge(ac.h, ac.n)
                        s.dma_start(out=dbg_o[:, :], in_=src_sb[:])\
                            .then_inc(so.h, so.tick(16))

            if upto <= 4.7:
                _final_wait(nc, so)
                return nc

            with nc.psum_tensor("ps_w0", [128, T], F32) as ps_w0, \
                 nc.psum_tensor("ps_w1", [128, T], F32) as ps_w1, \
                 nc.Block("p5e") as blk:
                ps_f = [ps_w0, ps_w1]
                wl0, pe0, dv0c, so0 = wl.n, pe.n, dv.n, so.n
                pe_w = [pe0 + (i + 1) for i in range(WC)]
                ac_w = [dv0c + (i + 1) for i in range(WC)]
                so_w = [so0 + 16 * (i + 1) for i in range(WC)]

                @blk.gpsimd
                def _(g):
                    g.wait_ge(pe.h, pe0)
                    src = wo_f[:, :].rearrange("(kc p) c -> p kc c", p=128)
                    dst = wbuf[1][:].rearrange("p (kc c) -> p kc c", kc=WC)
                    g.dma_start(out=dst, in_=src).then_inc(wl.h, wl.tick(16))
                wl_wo = wl.n

                @blk.tensor
                def _(t):
                    t.wait_ge(wl.h, wl_wo)
                    t.wait_ge(ac.h, ac.n)
                    for fc in range(WC):
                        b = fc % 2
                        if fc >= 2:
                            t.wait_ge(dv.h, ac_w[fc - 2])
                        mm = None
                        for kc in range(WC):
                            for tg in range(4):
                                mm = t.matmul(ps_f[b][:, tg * 512:(tg + 1) * 512],
                                              wbuf[1][:, (kc * WC + fc) * 128:
                                                      (kc * WC + fc + 1) * 128],
                                              src_sb[:, kc * T + tg * 512:
                                                     kc * T + (tg + 1) * 512],
                                              start=(kc == 0), stop=(kc == WC - 1))
                        mm.then_inc(pe.h, pe.tick())
                        assert pe.n == pe_w[fc]

                @blk.vector
                def _(v):
                    for fc in range(WC):
                        b = fc % 2
                        v.wait_ge(pe.h, pe_w[fc])
                        if fc >= 2:
                            v.wait_ge(so.h, so_w[fc - 2])
                        v.tensor_scalar(obf[b][:], ps_f[b][:], bot_sb[:, fc:fc + 1],
                                        None, op0=ALU.add).then_inc(dv.h, dv.tick())
                        assert dv.n == ac_w[fc]

                @blk.sync
                def _(s):
                    for fc in range(WC):
                        b = fc % 2
                        s.wait_ge(dv.h, ac_w[fc])
                        s.dma_start(out=o2_hbm[fc * 128:(fc + 1) * 128, :],
                                    in_=obf[b][:]).then_inc(so.h, so.tick(16))
                assert so.n == so_w[-1]

        if upto <= 5:
            _final_wait(nc, so)
            return nc

        p7 = ExitStack()
        with p7:
            p7sb = lambda n_, s_, d_: p7.enter_context(nc.sbuf_tensor(n_, list(s_), d_))
            lhsb = [p7sb(f"lhsb{i}", (128, 8 * T), BF16) for i in range(2)]
            rhsb = [p7sb(f"rhsb{i}", (128, 8 * W), BF16) for i in range(2)]
            acc = p7sb("acc", (128, NB * W), F32)
            xbuf = [p7sb(f"xbuf{i}", (128, W), F32) for i in range(2)]
            oxb = [p7sb(f"oxb{i}", (128, W), F32) for i in range(2)]
            B_sb = [p7sb(f"B_sb{i}", (4, W), BF16) for i in range(2)]
            cxp = p7sb("cxp", (4, T), BF16)

            NGRP = 14
            snap7 = snapshot()
            with nc.psum_tensor("ps_t0", [128, W], F32) as ps_t0, \
                 nc.psum_tensor("ps_t1", [128, W], F32) as ps_t1, \
                 nc.psum_tensor("ps_t2", [128, W], F32) as ps_t2, \
                 nc.Block("p7") as blk:
                ps_t = [ps_t0, ps_t1, ps_t2]

                sl0, wl0, ll0, pe0, dv0, so0 = sl.n, wl.n, ll.n, pe.n, dv.n, so.n
                sl_lhs, wl_rhs = {}, {}
                pe_g, dv_g = {}, {}
                ll_B, ll_x, dv_fin, so_out = {}, {}, {}, {}
                sln, wln, lln, pen, dvn, son = sl0, wl0, ll0, pe0, dv0, so0
                wln += 16
                wl_cxp = wln
                gidx = 0
                for fh in range(4):
                    wln += 16
                    ll_B[fh] = wln
                    for g in range(13):
                        sln += 16 * 8
                        sl_lhs[(fh, g)] = sln
                        wln += 16
                        wl_rhs[(fh, g)] = wln
                    for g in range(NGRP):
                        for tt in range(NB):
                            pen += 1
                            pe_g[gidx] = pen
                            gidx += 1
                    for tt in range(NB):
                        lln += 16
                        ll_x[(fh, tt)] = lln
                        son += 16
                        so_out[(fh, tt)] = son
                NGI = gidx
                for tt in range(NB):
                    dvn += 1
                    dv_g[(0 * NGRP + 0) * NB + tt] = dvn
                for fh in range(4):
                    for g in range(1, NGRP):
                        for tt in range(NB):
                            dvn += 1
                            dv_g[(fh * NGRP + g) * NB + tt] = dvn
                    for tt in range(NB):
                        dvn += 1
                        dv_fin[(fh, tt)] = dvn
                        if fh < 3:
                            dvn += 1
                            dv_g[((fh + 1) * NGRP + 0) * NB + tt] = dvn

                def grp_src(g):
                    if g < 12:
                        e, kk = g // 4, g % 4
                        lhs_rows = [(e * KC + kk * 8 + i) * 128 for i in range(8)]
                        return ("act", lhs_rows, w2_e[e], kk * 8)
                    else:
                        lhs_rows = [i * 128 for i in range(8)]
                        return ("o2", lhs_rows, proj_w, 0)

                @blk.scalar
                def _(sc):
                    wait_snap(sc, snap7)
                    for fh in range(4):
                        for g in range(13):
                            lg = fh * 13 + g
                            gb = lg % 2
                            if lg >= 2:
                                fh2, g2 = divmod(lg - 2, 13)
                                sc.wait_ge(pe.h, pe_g[(fh2 * NGRP + g2) * NB + NB - 1])
                            kind, rows_, _, _ = grp_src(g)
                            srcten = act_hbm if kind == "act" else o2_hbm
                            for i2, r0 in enumerate(rows_):
                                sc.dma_start(out=lhsb[gb][:, i2 * T:(i2 + 1) * T],
                                             in_=srcten[r0:r0 + 128, :])\
                                    .then_inc(sl.h, sl.tick(16))
                            assert sl.n == sl_lhs[(fh, g)]

                @blk.gpsimd
                def _(g_):
                    wait_snap(g_, snap7)
                    g_.dma_start(out=cxp[0:4, :], in_=cx_hbm[0:4, :])\
                        .then_inc(wl.h, wl.tick(16))
                    assert wl.n == wl_cxp
                    for fh in range(4):
                        if fh >= 2:
                            g_.wait_ge(pe.h, pe_g[((fh - 2) * NGRP + 13) * NB + NB - 1])
                        g_.dma_start(out=B_sb[fh % 2][0:4, :],
                                     in_=Bmat[0:4, fh * W:(fh + 1) * W])\
                            .then_inc(wl.h, wl.tick(16))
                        assert wl.n == ll_B[fh]
                        for g in range(13):
                            lg = fh * 13 + g
                            gb = lg % 2
                            if lg >= 2:
                                fh2, g2 = divmod(lg - 2, 13)
                                g_.wait_ge(pe.h, pe_g[(fh2 * NGRP + g2) * NB + NB - 1])
                            _, _, wten, kbase = grp_src(g)
                            src = wten[kbase * 128:(kbase + 8) * 128,
                                       fh * W:(fh + 1) * W]
                            src = src.rearrange("(kc p) c -> p kc c", p=128)
                            dst = rhsb[gb][:].rearrange("p (kc c) -> p kc c", kc=8)
                            g_.dma_start(out=dst, in_=src).then_inc(wl.h, wl.tick(16))
                            assert wl.n == wl_rhs[(fh, g)]

                @blk.sync
                def _(s):
                    wait_snap(s, snap7)
                    for fh in range(4):
                        for tt in range(NB):
                            bx = tt % 2
                            if fh > 0 or tt >= 2:
                                pf, pt = (fh, tt - 2) if tt >= 2 else (fh - 1, tt + NB - 2)
                                s.wait_ge(dv.h, dv_fin[(pf, pt)])
                            s.dma_start(out=xbuf[bx][:],
                                        in_=x_tm[tt * 128:(tt + 1) * 128,
                                                 fh * W:(fh + 1) * W])\
                                .then_inc(ll.h, ll.tick(16))
                            assert ll.n == ll_x[(fh, tt)]
                            s.wait_ge(dv.h, dv_fin[(fh, tt)])
                            s.dma_start(out=out[tt * 128:(tt + 1) * 128,
                                                fh * W:(fh + 1) * W],
                                        in_=oxb[bx][:]).then_inc(so.h, so.tick(16))
                            assert so.n == so_out[(fh, tt)]

                @blk.tensor
                def _(t):
                    wait_snap(t, snap7)
                    t.wait_ge(wl.h, wl_cxp)
                    for fh in range(4):
                        for g in range(NGRP):
                            gb = (fh * 13 + g) % 2 if g < 13 else 0
                            if g < 13:
                                t.wait_ge(sl.h, sl_lhs[(fh, g)])
                                t.wait_ge(wl.h, wl_rhs[(fh, g)])
                            else:
                                t.wait_ge(wl.h, ll_B[fh])
                            for tt in range(NB):
                                gi = (fh * NGRP + g) * NB + tt
                                slot = gi % 3
                                if gi >= 3:
                                    t.wait_ge(dv.h, dv_g[gi - 3])
                                mm = None
                                if g < 13:
                                    for kc in range(8):
                                        for nh in range(2):
                                            mm = t.matmul(
                                                ps_t[slot][:, nh * 512:(nh + 1) * 512],
                                                lhsb[gb][:, kc * T + tt * 128:
                                                         kc * T + tt * 128 + 128],
                                                rhsb[gb][:, kc * W + nh * 512:
                                                         kc * W + (nh + 1) * 512],
                                                start=(kc == 0), stop=(kc == 7))
                                else:
                                    for nh in range(2):
                                        mm = t.matmul(
                                            ps_t[slot][:, nh * 512:(nh + 1) * 512],
                                            cxp[0:4, tt * 128:(tt + 1) * 128],
                                            B_sb[fh % 2][0:4, nh * 512:(nh + 1) * 512],
                                            start=True, stop=True)
                                mm.then_inc(pe.h, pe.tick())
                                assert pe.n == pe_g[gi]

                @blk.vector
                def _(v):
                    wait_snap(v, snap7)

                    def g0_copy(fh2, tt):
                        gi = (fh2 * NGRP + 0) * NB + tt
                        slot = gi % 3
                        v.wait_ge(pe.h, pe_g[gi])
                        v.tensor_copy(acc[:, tt * W:(tt + 1) * W], ps_t[slot][:])\
                            .then_inc(dv.h, dv.tick())
                        assert dv.n == dv_g[gi]

                    for tt in range(NB):
                        g0_copy(0, tt)
                    for fh in range(4):
                        for g in range(1, NGRP):
                            for tt in range(NB):
                                gi = (fh * NGRP + g) * NB + tt
                                slot = gi % 3
                                v.wait_ge(pe.h, pe_g[gi])
                                aslice = acc[:, tt * W:(tt + 1) * W]
                                v.tensor_tensor(aslice, ps_t[slot][:], aslice,
                                                op=ALU.add).then_inc(dv.h, dv.tick())
                                assert dv.n == dv_g[gi]
                        for tt in range(NB):
                            bx = tt % 2
                            v.wait_ge(ll.h, ll_x[(fh, tt)])
                            if fh > 0 or tt >= 2:
                                pf, pt = (fh, tt - 2) if tt >= 2 else (fh - 1, tt + NB - 2)
                                v.wait_ge(so.h, so_out[(pf, pt)])
                            v.tensor_tensor(oxb[bx][:], acc[:, tt * W:(tt + 1) * W],
                                            xbuf[bx][:], op=ALU.add)\
                                .then_inc(dv.h, dv.tick())
                            assert dv.n == dv_fin[(fh, tt)]
                            if fh < 3:
                                g0_copy(fh + 1, tt)

        _final_wait(nc, so)
        return nc


def _final_wait(nc, so):
    with nc.Block("fin") as blk:
        @blk.sync
        def _(s):
            s.wait_ge(so.h, so.n)



def _prep_common(inputs):
    f = np.float32
    cw = np.ascontiguousarray(np.asarray(inputs["L0_attn_conv_w"], f))
    conv_b = np.asarray(inputs["L0_attn_conv_b"], f)
    wq = np.asarray(inputs["L0_attn_wq"], f).reshape(W, W)
    wk = np.asarray(inputs["L0_attn_wk"], f).reshape(W, W)
    wv = np.asarray(inputs["L0_attn_wv"], f).reshape(W, W)
    bq = np.asarray(inputs["L0_attn_bq"], f).reshape(-1)
    bk = np.asarray(inputs["L0_attn_bk"], f).reshape(-1)
    bv = np.asarray(inputs["L0_attn_bv"], f).reshape(-1)
    wo = np.asarray(inputs["L0_attn_wo"], f).reshape(W, W)
    bo = np.asarray(inputs["L0_attn_bo"], f).reshape(-1)
    pw = np.ascontiguousarray(np.asarray(inputs["L0_attn_proj_w"], f))
    pb = np.asarray(inputs["L0_attn_proj_b"], f).reshape(-1)
    gw = np.asarray(inputs["L0_ffn_gate_w"], f)
    gb = np.asarray(inputs["L0_ffn_gate_b"], f).reshape(-1)
    w1 = np.asarray(inputs["L0_ffn_w1"], f)
    b1 = np.asarray(inputs["L0_ffn_b1"], f)
    w2 = np.asarray(inputs["L0_ffn_w2"], f)
    b2 = np.asarray(inputs["L0_ffn_b2"], f)

    tril = np.tril(np.ones((128, 128), dtype=bool))
    mask = np.where(tril, np.float32(0.0), np.float32(-1.0e9))

    common = {
        "conv_w0": np.ascontiguousarray(cw[0]),
        "conv_w1": np.ascontiguousarray(cw[1]),
        "conv_w2": np.ascontiguousarray(cw[2]),
        "conv_bt": np.ascontiguousarray(conv_b.reshape(QC, 128).T),
        "wq_f": wq, "wk_f": wk, "wv_f": wv, "wo_f": wo,
        "bqt": np.ascontiguousarray(bq.reshape(WC, 128).T),
        "bkt": np.ascontiguousarray(bk.reshape(WC, 128).T),
        "bot": np.ascontiguousarray(bo.reshape(WC, 128).T),
        "bv_row": np.ascontiguousarray(bv.reshape(1, W)),
        "proj_w": pw,
        "w1_0": np.ascontiguousarray(w1[0]),
        "w1_1": np.ascontiguousarray(w1[1]),
        "w1_2": np.ascontiguousarray(w1[2]),
        "w2_0": np.ascontiguousarray(w2[0]),
        "w2_1": np.ascontiguousarray(w2[1]),
        "w2_2": np.ascontiguousarray(w2[2]),
        "b1t": np.ascontiguousarray(
            b1.reshape(NE, KC, 128).transpose(2, 0, 1).reshape(128, NE * KC)),
        "gwa_t": np.ascontiguousarray(
            gw[:D].reshape(KC, 128, NE).transpose(1, 0, 2).reshape(128, KC * NE)),
        "gwb_col": np.ascontiguousarray(gw[D].reshape(NE, 1)),
        "gate_b_col": np.ascontiguousarray(gb.reshape(NE, 1)),
        "Bmat": np.ascontiguousarray(np.stack([pb, b2[0], b2[1], b2[2]])),
        "eye": np.eye(128, dtype=f),
        "mask4": np.ascontiguousarray(np.tile(mask, (1, 4))),
        "ones_col": np.ones((128, 1), f),
        "ones_row": np.ones((1, 128), f),
    }
    return common


def kernel(**inputs):
    x = np.ascontiguousarray(np.asarray(inputs["L0_attn_inputs"], np.float32))
    common = _prep_common(inputs)
    in_maps = []
    for c in range(NCORE):
        m = dict(common)
        m["x_tm"] = np.ascontiguousarray(x[c])
        m["x_fm"] = np.ascontiguousarray(x[c].T)
        in_maps.append(m)
    nc = build_nc()
    res = run_bass_kernel_spmd(nc, in_maps, core_ids=list(range(NCORE)))
    return np.stack([res.results[c]["out"] for c in range(NCORE)], axis=0)


# version 05 (final)
# speedup vs baseline: 1.0758x; 1.0244x over previous
import numpy as np
from contextlib import ExitStack

import concourse.bass as bass
import concourse.mybir as mybir
from concourse.bass_utils import run_bass_kernel_spmd

F32 = mybir.dt.float32
BF16 = mybir.dt.bfloat16
AF = mybir.ActivationFunctionType
ALU = mybir.AluOpType
AX = mybir.AxisListType.X

T = 2048
D = 4096
KC = 32
W = 1024
WC = 8
QC = 24
NB = 16
H = 16
DH = 64
NE = 3
VIS = 614
XNW = T + 4
NCORE = 8


class SemC:
    def __init__(self, h):
        self.h = h
        self.n = 0

    def inc(self, k=1):
        self.n += k
        return self.n

    def tick(self, k=1):
        self.n += k
        return k


def build_nc(upto=7, dbg=(), ng=None):
    dbg = set(dbg)
    nc = bass.Bass("TRN2", target_bir_lowering=False, debug=False)

    def din(name, shape, dtype=F32):
        return nc.dram_tensor(name, list(shape), dtype, kind="ExternalInput").ap()

    x_tm = din("x_tm", (T, D))
    x_fm = din("x_fm", (D, T))
    conv_wt = [din(f"conv_w{i}", (D, 3 * W)) for i in range(3)]
    conv_bt = din("conv_bt", (128, QC))
    wq_f = din("wq_f", (W, W))
    wk_f = din("wk_f", (W, W))
    wv_f = din("wv_f", (W, W))
    wo_f = din("wo_f", (W, W))
    bqt = din("bqt", (128, WC))
    bkt = din("bkt", (128, WC))
    bot = din("bot", (128, WC))
    bv_row = din("bv_row", (1, W))
    proj_w = din("proj_w", (W, D))
    w1_e = [din(f"w1_{e}", (D, D)) for e in range(NE)]
    w2_e = [din(f"w2_{e}", (D, D)) for e in range(NE)]
    b1t = din("b1t", (128, NE * KC))
    gwa_t = din("gwa_t", (128, KC * NE))
    gwb_col = din("gwb_col", (NE, 1))
    gate_b_col = din("gate_b_col", (NE, 1))
    Bmat = din("Bmat", (4, D))
    eye = din("eye", (128, 128))
    mask4_d = din("mask4", (128, 512))
    ones_col = din("ones_col", (128, 1))
    ones_row = din("ones_row", (1, 128))

    out = nc.dram_tensor("out", [T, D], F32, kind="ExternalOutput").ap()

    def scratch(name, shape, dtype):
        kind = "ExternalOutput" if name in dbg else "Internal"
        return nc.dram_tensor(name, list(shape), dtype, kind=kind).ap()

    qkv_hbm = scratch("qkv_hbm", (3 * W, T), BF16)
    act_hbm = scratch("act_hbm", (NE * D, T), BF16)
    o2_hbm = scratch("o2_hbm", (W, T), BF16)
    cx_hbm = scratch("cx_hbm", (8, T), F32)
    dbg_small = None
    if "small" in dbg:
        dbg_small = nc.dram_tensor("dbg_small", [128, 4096], F32, kind="ExternalOutput").ap()
    dbg_xn = None
    if "xn" in dbg:
        dbg_xn = nc.dram_tensor("dbg_xn", [128, KC * XNW], BF16, kind="ExternalOutput").ap()
    dbg_qh = None
    if "qh" in dbg:
        dbg_qh = nc.dram_tensor("dbg_qh", [128, 3 * WC * T], BF16, kind="ExternalOutput").ap()
    dbg_o = None
    if "o" in dbg:
        dbg_o = nc.dram_tensor("dbg_o", [128, WC * T], BF16, kind="ExternalOutput").ap()

    es = ExitStack()
    with es:
        wl = SemC(es.enter_context(nc.semaphore("wl")))
        ll = SemC(es.enter_context(nc.semaphore("ll")))
        sl = SemC(es.enter_context(nc.semaphore("sl")))
        pe = SemC(es.enter_context(nc.semaphore("pe")))
        ac = SemC(es.enter_context(nc.semaphore("ac")))
        dv = SemC(es.enter_context(nc.semaphore("dv")))
        so = SemC(es.enter_context(nc.semaphore("so")))
        pl = SemC(es.enter_context(nc.semaphore("pl")))
        all_sems = [wl, ll, sl, pe, ac, dv, so, pl]

        def snapshot():
            return [(s_.h, s_.n) for s_ in all_sems]

        def vbar(v, inst):
            inst.then_inc(dv.h, dv.tick())
            v.wait_ge(dv.h, dv.n)
            return inst

        def wait_snap(eng, snap):
            for h_, n_ in snap:
                if n_ > 0:
                    eng.wait_ge(h_, n_)

        sbuf = lambda name, shape, dt_: es.enter_context(nc.sbuf_tensor(name, list(shape), dt_))

        eye_f = sbuf("eye_f", (128, 128), F32)
        eye_b = sbuf("eye_b", (128, 128), BF16)
        onec_f = sbuf("onec_f", (128, 1), F32)
        onec_b = sbuf("onec_b", (128, 1), BF16)
        oner_f = sbuf("oner_f", (1, 128), F32)
        oner_b = sbuf("oner_b", (1, 128), BF16)
        conv_bt_sb = sbuf("conv_bt_sb", (128, QC), F32)
        bqs_sb = sbuf("bqs_sb", (128, WC), F32)
        bkt_sb = sbuf("bkt_sb", (128, WC), F32)
        bot_sb = sbuf("bot_sb", (128, WC), F32)
        b1t_sb = sbuf("b1t_sb", (128, NE * KC), F32)
        gwb_sb = sbuf("gwb_sb", (NE, 1), F32)
        gb_sb = sbuf("gb_sb", (NE, 1), F32)
        xn_stack = ExitStack()
        xn = xn_stack.enter_context(nc.sbuf_tensor("xn", [128, KC * XNW], BF16))

        with nc.sbuf_tensor("bqt_tmp", [128, WC], F32) as bqt_tmp, \
             nc.Block("p0") as blk:

            @blk.gpsimd
            def _(g):
                g.dma_start(out=eye_b[:], in_=eye[:, :]).then_inc(wl.h, wl.tick(16))
                g.dma_start(out=onec_b[:], in_=ones_col[:, :]).then_inc(wl.h, wl.tick(16))
                g.dma_start(out=oner_b[:], in_=ones_row[:, :]).then_inc(wl.h, wl.tick(16))

            @blk.sync
            def _(s):
                for dst, src in [(eye_f, eye), (onec_f, ones_col), (oner_f, ones_row),
                                 (conv_bt_sb, conv_bt), (bqt_tmp, bqt), (bkt_sb, bkt),
                                 (bot_sb, bot), (b1t_sb, b1t), (gwb_sb, gwb_col),
                                 (gb_sb, gate_b_col)]:
                    s.dma_start(out=dst[:], in_=src[:, :]).then_inc(ll.h, ll.tick(16))

            @blk.vector
            def _(v):
                v.wait_ge(ll.h, ll.n)
                v.tensor_scalar(bqs_sb[:], bqt_tmp[:], 0.125, None, op0=ALU.mult)
                last = None
                for kc in range(KC):
                    v.memset(xn[:, kc * XNW: kc * XNW + 2], 0.0)
                    last = v.memset(xn[:, kc * XNW + 2 + T: (kc + 1) * XNW], 0.0)
                last.then_inc(dv.h, dv.tick())

        p1 = ExitStack()
        with p1:
            p1sb = lambda name, shape, dt_: p1.enter_context(
                nc.sbuf_tensor(name, list(shape), dt_))
            logits = p1sb("logits", (NE, T), F32)
            macc = p1sb("macc", (128, 4), F32)
            mrow = p1sb("mrow", (128, 8), F32)
            p1w = ExitStack()
            p1wsb = lambda name, shape, dt_: p1w.enter_context(
                nc.sbuf_tensor(name, list(shape), dt_))
            bc_a = p1wsb("bc_a", (128, T), F32)
            bc_b = p1wsb("bc_b", (128, T), F32)
            gwa_sb = p1wsb("gwa_sb", (128, KC * NE), F32)
            p1a_bufs = ExitStack()
            rows = p1a_bufs.enter_context(nc.sbuf_tensor("rows", [1, 3 * T], F32))
            xb = [p1a_bufs.enter_context(nc.sbuf_tensor(f"xb{i}", [128, T], BF16))
                  for i in range(2)]
            sq = [p1a_bufs.enter_context(nc.sbuf_tensor(f"sq{i}", [128, T], BF16))
                  for i in range(2)]

            wl0, pe0, ac0 = wl.n, pe.n, ac.n
            wl_xb = [wl0 + 16 * (k + 1) for k in range(KC)]
            pe_st = [pe0 + (k + 1) for k in range(KC)]
            ac_sq = [ac0 + (k + 1) for k in range(KC)]

            snap1a = snapshot()
            with nc.psum_tensor("ps_sums", [1, T], F32) as ps_sums, \
                 nc.psum_tensor("ps_sumsq", [1, T], F32) as ps_sumsq, \
                 nc.Block("p1a") as blk:

                @blk.gpsimd
                def _(g):
                    wait_snap(g, snap1a)
                    for kc in range(KC):
                        b = kc % 2
                        if kc >= 2:
                            g.wait_ge(pe.h, pe_st[kc - 2])
                        g.dma_start(out=xb[b][:], in_=x_fm[kc * 128:(kc + 1) * 128, :])\
                            .then_inc(wl.h, wl.tick(16))
                assert wl.n == wl_xb[-1]

                @blk.sync
                def _(s):
                    s.dma_start(out=gwa_sb[:], in_=gwa_t[:, :]).then_inc(ll.h, ll.tick(16))

                @blk.scalar
                def _(sc):
                    for kc in range(KC):
                        b = kc % 2
                        sc.wait_ge(wl.h, wl_xb[kc])
                        if kc >= 2:
                            sc.wait_ge(pe.h, pe_st[kc - 2])
                        sc.activation(sq[b][:], xb[b][:], AF.Square)\
                            .then_inc(ac.h, ac.tick())
                assert ac.n == ac_sq[-1]

                @blk.tensor
                def _(t):
                    for kc in range(KC):
                        b = kc % 2
                        t.wait_ge(wl.h, wl_xb[kc])
                        for tg in range(4):
                            t.matmul(ps_sums[0:1, tg * 512:(tg + 1) * 512],
                                     onec_b[:, 0:1], xb[b][:, tg * 512:(tg + 1) * 512],
                                     start=(kc == 0), stop=(kc == KC - 1))
                        t.wait_ge(ac.h, ac_sq[kc])
                        mm = None
                        for tg in range(4):
                            mm = t.matmul(ps_sumsq[0:1, tg * 512:(tg + 1) * 512],
                                          onec_b[:, 0:1], sq[b][:, tg * 512:(tg + 1) * 512],
                                          start=(kc == 0), stop=(kc == KC - 1))
                        mm.then_inc(pe.h, pe.tick())
                assert pe.n == pe_st[-1]

                @blk.scalar
                def _(sc):
                    sc.wait_ge(pe.h, pe_st[-1])
                    sc.mul(rows[0:1, 0:T], ps_sums[0:1, :], 1.0 / D)
                    sc.mul(rows[0:1, T:2 * T], ps_sumsq[0:1, :], 1.0 / D)\
                        .then_inc(ac.h, ac.tick())
            ac_stats = ac.n

            with nc.psum_tensor("ps_bca", [128, T], F32) as ps_bca, \
                 nc.psum_tensor("ps_bcb", [128, T], F32) as ps_bcb, \
                 nc.Block("p1a2") as blk:

                @blk.vector
                def _(v):
                    v.wait_ge(ac.h, ac_stats)
                    vbar(v, v.tensor_tensor(rows[0:1, 2 * T:3 * T], rows[0:1, 0:T],
                                            rows[0:1, 0:T], op=ALU.mult))
                    vbar(v, v.tensor_tensor(rows[0:1, T:2 * T], rows[0:1, T:2 * T],
                                            rows[0:1, 2 * T:3 * T], op=ALU.subtract))
                    vbar(v, v.tensor_scalar(rows[0:1, T:2 * T], rows[0:1, T:2 * T],
                                            1e-5, None, op0=ALU.add))
                dv_var = dv.n

                @blk.scalar
                def _(sc):
                    sc.wait_ge(dv.h, dv_var)
                    sc.activation(rows[0:1, 2 * T:3 * T], rows[0:1, T:2 * T],
                                  AF.Sqrt).then_inc(ac.h, ac.tick())
                ac_sd = ac.n

                @blk.vector
                def _(v):
                    v.wait_ge(ac.h, ac_sd)
                    vbar(v, v.reciprocal(rows[0:1, T:2 * T], rows[0:1, 2 * T:3 * T]))
                    vbar(v, v.scalar_tensor_tensor(rows[0:1, 2 * T:3 * T], rows[0:1, 0:T],
                                                   -1.0, rows[0:1, T:2 * T],
                                                   op0=ALU.mult, op1=ALU.mult))
                dv_rstd = dv.n

                @blk.tensor
                def _(t):
                    t.wait_ge(dv.h, dv_rstd)
                    t.wait_ge(ac.h, ac_stats)
                    for tg in range(4):
                        t.matmul(ps_bca[:, tg * 512:(tg + 1) * 512], oner_f[0:1, :],
                                 rows[0:1, T + tg * 512: T + (tg + 1) * 512],
                                 start=True, stop=True)
                    mm = None
                    for tg in range(4):
                        mm = t.matmul(ps_bcb[:, tg * 512:(tg + 1) * 512], oner_f[0:1, :],
                                      rows[0:1, 2 * T + tg * 512: 2 * T + (tg + 1) * 512],
                                      start=True, stop=True)
                    mm.then_inc(pe.h, pe.tick())
                pe_bc = pe.n

                @blk.vector
                def _(v):
                    v.wait_ge(pe.h, pe_bc)
                    v.tensor_copy(bc_a[:], ps_bca[:])
                    v.tensor_copy(bc_b[:], ps_bcb[:]).then_inc(dv.h, dv.tick())
            dv_bc = dv.n

            p1a_bufs.close()
            xf = [p1w.enter_context(nc.sbuf_tensor(f"xf{i}", [128, T], F32))
                  for i in range(2)]
            xnf = [p1w.enter_context(nc.sbuf_tensor(f"xnf{i}", [128, T], F32))
                   for i in range(2)]
            abjunk = p1w.enter_context(nc.sbuf_tensor("abjunk", [128, T], BF16))
            ll0, dv0, ac0, pe0 = ll.n, dv.n, ac.n, pe.n
            ll_xf = [ll0 + 16 * (k + 1) for k in range(KC)]
            dv_tt1 = [dv0 + 3 * k + 1 for k in range(KC)]
            dv_tt2 = [dv0 + 3 * k + 2 for k in range(KC)]
            dv_mod = [dv0 + 3 * k + 3 for k in range(KC)]
            ac_cp = [ac0 + 2 * k + 1 for k in range(KC)]
            ac_abs = [ac0 + 2 * k + 2 for k in range(KC)]
            pe_gate = [pe0 + (k + 1) for k in range(KC)]

            with nc.psum_tensor("ps_gate", [NE, T], F32) as ps_gate, \
                 nc.psum_tensor("ps_mv", [1, 1], F32) as ps_mv, \
                 nc.psum_tensor("ps_mt", [1, 1], F32) as ps_mt, \
                 nc.psum_tensor("ps_r3", [NE, 1], F32) as ps_r3, \
                 nc.Block("p1b") as blk:

                @blk.sync
                def _(s):
                    s.wait_ge(pe.h, pe_bc)
                    for kc in range(KC):
                        b = kc % 2
                        if kc >= 2:
                            s.wait_ge(dv.h, dv_tt1[kc - 2])
                        s.dma_start(out=xf[b][:], in_=x_fm[kc * 128:(kc + 1) * 128, :])\
                            .then_inc(ll.h, ll.tick(16))
                assert ll.n == ll_xf[-1]

                @blk.vector
                def _(v):
                    v.wait_ge(dv.h, dv_bc)
                    v.memset(macc[:], 0.0)
                    for kc in range(KC):
                        b = kc % 2
                        v.wait_ge(ll.h, ll_xf[kc])
                        if kc >= 2:
                            v.wait_ge(ac.h, ac_abs[kc - 2])
                            v.wait_ge(pe.h, pe_gate[kc - 2])
                        vbar(v, v.tensor_tensor(xnf[b][:], xf[b][:], bc_a[:],
                                                op=ALU.mult))
                        vbar(v, v.tensor_tensor(xnf[b][:], xnf[b][:], bc_b[:],
                                                op=ALU.add))
                        v.wait_ge(ac.h, ac_abs[kc])
                        v.tensor_tensor(macc[:, 0:1], macc[:, 0:1], mrow[:, 2:3],
                                        op=ALU.add)
                        v.tensor_tensor(macc[:, 1:2], macc[:, 1:2], mrow[:, 3:4],
                                        op=ALU.add).then_inc(dv.h, dv.tick())
                assert dv.n == dv_mod[-1]

                @blk.scalar
                def _(sc):
                    for kc in range(KC):
                        b = kc % 2
                        sc.wait_ge(dv.h, dv_tt2[kc])
                        sc.copy(xn[:, kc * XNW + 2: kc * XNW + 2 + T], xnf[b][:])\
                            .then_inc(ac.h, ac.tick())
                        assert ac.n == ac_cp[kc]
                        sc.activation(abjunk[:, 0:VIS], xnf[b][:, 0:VIS], AF.Abs,
                                      accum_out=mrow[:, 2:3])
                        sc.activation(abjunk[:, VIS:T], xnf[b][:, VIS:T], AF.Abs,
                                      accum_out=mrow[:, 3:4]).then_inc(ac.h, ac.tick())
                        assert ac.n == ac_abs[kc]

                @blk.tensor
                def _(t):
                    for kc in range(KC):
                        b = kc % 2
                        t.wait_ge(dv.h, dv_tt2[kc])
                        mm = None
                        for tg in range(4):
                            mm = t.matmul(ps_gate[0:NE, tg * 512:(tg + 1) * 512],
                                          gwa_sb[:, kc * NE:(kc + 1) * NE],
                                          xnf[b][:, tg * 512:(tg + 1) * 512],
                                          start=(kc == 0), stop=(kc == KC - 1))
                        mm.then_inc(pe.h, pe.tick())
                assert pe.n == pe_gate[-1]

                @blk.tensor
                def _(t):
                    t.wait_ge(dv.h, dv_mod[-1])
                    t.matmul(ps_mv[0:1, 0:1], macc[:, 0:1], onec_f[:, 0:1],
                             start=True, stop=True)
                    t.matmul(ps_mt[0:1, 0:1], macc[:, 1:2], onec_f[:, 0:1],
                             start=True, stop=True).then_inc(pe.h, pe.tick())
                pe_mtot = pe.n

                @blk.scalar
                def _(sc):
                    sc.wait_ge(pe.h, pe_mtot)
                    sc.mul(mrow[0:1, 4:5], ps_mt[0:1, 0:1], 1.0 / ((T - VIS) * D))
                    sc.mul(mrow[0:1, 5:6], ps_mv[0:1, 0:1], 1.0 / (VIS * D))\
                        .then_inc(ac.h, ac.tick())
                ac_mm = ac.n

                @blk.vector
                def _(v):
                    v.wait_ge(ac.h, ac_mm)
                    v.tensor_scalar(mrow[0:1, 6:7], mrow[0:1, 5:6], 1e-8, None,
                                    op0=ALU.add).then_inc(dv.h, dv.tick())
                    v.wait_ge(dv.h, dv.n)
                    v.reciprocal(mrow[0:1, 7:8], mrow[0:1, 6:7]).then_inc(dv.h, dv.tick())
                    v.wait_ge(dv.h, dv.n)
                    v.tensor_tensor(mrow[0:1, 0:1], mrow[0:1, 4:5], mrow[0:1, 7:8],
                                    op=ALU.mult).then_inc(dv.h, dv.tick())
                dv_ratio = dv.n

                @blk.tensor
                def _(t):
                    t.wait_ge(dv.h, dv_ratio)
                    t.matmul(ps_r3[0:NE, 0:1], oner_f[0:1, 0:NE], mrow[0:1, 0:1],
                             start=True, stop=True).then_inc(pe.h, pe.tick())
                pe_r3 = pe.n

                @blk.vector
                def _(v):
                    v.wait_ge(pe.h, pe_r3)
                    v.tensor_copy(mrow[0:NE, 1:2], ps_r3[0:NE, 0:1])\
                        .then_inc(dv.h, dv.tick())
                    v.wait_ge(dv.h, dv.n)
                    v.scalar_tensor_tensor(mrow[0:NE, 2:3], gwb_sb[0:NE, 0:1],
                                           mrow[0:NE, 1:2], gb_sb[0:NE, 0:1],
                                           op0=ALU.mult, op1=ALU.add)\
                        .then_inc(dv.h, dv.tick())
                dv_b3 = dv.n

                @blk.vector
                def _(v):
                    v.wait_ge(dv.h, dv_b3)
                    v.wait_ge(pe.h, pe_gate[-1])
                    v.tensor_scalar(logits[0:NE, :], ps_gate[0:NE, :], mrow[0:NE, 2:3],
                                    None, op0=ALU.add).then_inc(dv.h, dv.tick())
                dv_logits = dv.n

                if dbg_small is not None:
                    @blk.sync
                    def _(s):
                        s.wait_ge(ac.h, ac.n)
                        s.wait_ge(dv.h, dv.n)
                        s.dma_start(out=dbg_small[0:NE, 0:T], in_=logits[:])\
                            .then_inc(so.h, so.tick(16))
                        s.dma_start(out=dbg_small[8:9, 0:8], in_=mrow[0:1, :])\
                            .then_inc(so.h, so.tick(16))
                        s.dma_start(out=dbg_small[16:17, 0:T], in_=bc_a[0:1, :])\
                            .then_inc(so.h, so.tick(16))
                        s.dma_start(out=dbg_small[17:18, 0:T], in_=bc_b[0:1, :])\
                            .then_inc(so.h, so.tick(16))

                if dbg_xn is not None:
                    @blk.sync
                    def _(s):
                        s.wait_ge(ac.h, ac_cp[-1])
                        s.dma_start(out=dbg_xn[:, :], in_=xn[:]).then_inc(so.h, so.tick(16))

            p1w.close()
            p2 = ExitStack()
            with p2:
                p2sb = lambda n_, s_, d_: p2.enter_context(nc.sbuf_tensor(n_, list(s_), d_))
                lg_tm = p2sb("lg_tm", (128, 64), F32)
                sc_tm = p2sb("sc_tm", (128, 64), F32)
                v01 = p2sb("v01", (128, 32), F32)
                ptmp = p2sb("ptmp", (128, 16), F32)
                es_t = p2sb("es_t", (128, 4), F32)
                oh = p2sb("oh", (2, 16), F32)
                m2_sb = p2sb("m2_sb", (2, 4), F32)
                vrows = p2sb("vrows", (2, T), F32)
                cxt = p2sb("cxt", (4, T), F32)
                csb2 = p2sb("csb2", (NE, T), F32)

                with nc.psum_tensor("ps_ta", [128, 4], F32) as ps_ta, \
                     nc.psum_tensor("ps_tb", [128, 4], F32) as ps_tb, \
                     nc.psum_tensor("ps_va", [2, 128], F32) as ps_va, \
                     nc.psum_tensor("ps_vb", [2, 128], F32) as ps_vb, \
                     nc.psum_tensor("ps_c", [NE, T], F32) as ps_c, \
                     nc.Block("p2") as blk:

                    pe0, ac0 = pe.n, ac.n
                    pe_ltr = [pe0 + (ti + 1) for ti in range(NB)]
                    ac_lev = [ac0 + (ti + 1) for ti in range(NB)]

                    @blk.tensor
                    def _(t):
                        t.wait_ge(dv.h, dv_logits)
                        for ti in range(NB):
                            ps = [ps_ta, ps_tb][ti % 2]
                            if ti >= 2:
                                t.wait_ge(ac.h, ac_lev[ti - 2])
                            t.transpose(ps[:, 0:NE], logits[0:NE, ti * 128:(ti + 1) * 128],
                                        eye_f[0:NE, 0:NE]).then_inc(pe.h, pe.tick())
                    assert pe.n == pe_ltr[-1]

                    @blk.scalar
                    def _(sc):
                        for ti in range(NB):
                            ps = [ps_ta, ps_tb][ti % 2]
                            sc.wait_ge(pe.h, pe_ltr[ti])
                            sc.copy(lg_tm[:, ti * 4: ti * 4 + NE], ps[:, 0:NE])\
                                .then_inc(ac.h, ac.tick())
                    assert ac.n == ac_lev[-1]

                    dv1 = dv.n
                    dv_nm = [dv1 + 11 * ti + 1 for ti in range(NB)]
                    dv_tl = [dv1 + 11 * ti + 11 for ti in range(NB)]
                    ac1 = ac.n
                    ac_exp = [ac1 + (ti + 1) for ti in range(NB)]

                    @blk.vector
                    def _(v):
                        for ti in range(NB):
                            lg = lg_tm[:, ti * 4: ti * 4 + NE]
                            v.wait_ge(ac.h, ac_lev[ti])
                            v.reduce_max(ptmp[:, 0:1], lg, axis=AX, negate=True)\
                                .then_inc(dv.h, dv.tick())
                            v.wait_ge(ac.h, ac_exp[ti])
                            vbar(v, v.reduce_sum(ptmp[:, 1:2], es_t[:, 0:NE], axis=AX))
                            vbar(v, v.reciprocal(ptmp[:, 2:3], ptmp[:, 1:2]))
                            vbar(v, v.tensor_scalar(sc_tm[:, ti * 4: ti * 4 + NE],
                                                    es_t[:, 0:NE], ptmp[:, 2:3], None,
                                                    op0=ALU.mult))
                            s3 = sc_tm[:, ti * 4: ti * 4 + NE]
                            v.reduce_max(ptmp[:, 3:4], s3, axis=AX)
                            v.tensor_reduce(ptmp[:, 4:5], s3, axis=AX, op=ALU.min)
                            vbar(v, v.reduce_sum(ptmp[:, 5:6], s3, axis=AX))
                            vbar(v, v.tensor_tensor(ptmp[:, 6:7], ptmp[:, 5:6],
                                                    ptmp[:, 3:4], op=ALU.subtract))
                            vbar(v, v.tensor_tensor(ptmp[:, 6:7], ptmp[:, 6:7],
                                                    ptmp[:, 4:5], op=ALU.subtract))
                            vbar(v, v.tensor_tensor(ptmp[:, 7:8], ptmp[:, 3:4],
                                                    ptmp[:, 6:7], op=ALU.add))
                            vbar(v, v.tensor_scalar(ptmp[:, 7:8], ptmp[:, 7:8], 1e-8,
                                                    None, op0=ALU.add))
                            vbar(v, v.reciprocal(ptmp[:, 8:9], ptmp[:, 7:8]))
                            v.tensor_tensor(v01[:, 2 * ti:2 * ti + 1], ptmp[:, 3:4],
                                            ptmp[:, 8:9], op=ALU.mult)
                            v.tensor_tensor(v01[:, 2 * ti + 1:2 * ti + 2], ptmp[:, 6:7],
                                            ptmp[:, 8:9], op=ALU.mult)\
                                .then_inc(dv.h, dv.tick())
                    assert dv.n == dv_tl[-1]

                    @blk.scalar
                    def _(sc):
                        for ti in range(NB):
                            lg = lg_tm[:, ti * 4: ti * 4 + NE]
                            sc.wait_ge(dv.h, dv_nm[ti])
                            if ti >= 1:
                                sc.wait_ge(dv.h, dv_tl[ti - 1])
                            sc.activation(es_t[:, 0:NE], lg, AF.Exp,
                                          bias=ptmp[:, 0:1]).then_inc(ac.h, ac.tick())
                    assert ac.n == ac_exp[-1]

                    @blk.vector
                    def _(v):
                        v.wait_ge(dv.h, dv_tl[NB - 1])
                        r0 = sc_tm[0:1, 0:NE]
                        vbar(v, v.reduce_max(ptmp[0:1, 9:10], r0, axis=AX))
                        vbar(v, v.tensor_reduce(ptmp[0:1, 10:11], r0, axis=AX,
                                                op=ALU.min))
                        vbar(v, v.tensor_scalar(oh[0:1, 0:NE], r0, ptmp[0:1, 9:10], None,
                                                op0=ALU.is_equal))
                        vbar(v, v.tensor_scalar(oh[0:1, 4:4 + NE], r0, ptmp[0:1, 10:11],
                                                None, op0=ALU.is_equal))
                        vbar(v, v.tensor_tensor(oh[0:1, 8:8 + NE], oh[0:1, 0:NE],
                                                oh[0:1, 4:4 + NE], op=ALU.add))
                        vbar(v, v.tensor_scalar(oh[0:1, 8:8 + NE], oh[0:1, 8:8 + NE],
                                                1.0, -1.0, op0=ALU.subtract,
                                                op1=ALU.mult))
                    dv_oh = dv.n

                    @blk.sync
                    def _(s):
                        s.wait_ge(dv.h, dv_oh)
                        s.dma_start(out=m2_sb[0:1, 0:NE], in_=oh[0:1, 0:NE])\
                            .then_inc(ll.h, ll.tick(16))
                        s.dma_start(out=m2_sb[1:2, 0:NE], in_=oh[0:1, 8:8 + NE])\
                            .then_inc(ll.h, ll.tick(16))
                    ll_m2 = ll.n

                    pe1 = pe.n
                    pe_vtr = [pe1 + (ti + 1) for ti in range(NB)]
                    ac2 = ac.n
                    ac_vev = [ac2 + (ti + 1) for ti in range(NB)]

                    @blk.tensor
                    def _(t):
                        for ti in range(NB):
                            ps = [ps_va, ps_vb][ti % 2]
                            t.wait_ge(dv.h, dv_tl[ti])
                            if ti >= 2:
                                t.wait_ge(ac.h, ac_vev[ti - 2])
                            t.transpose(ps[0:2, :], v01[:, 2 * ti:2 * ti + 2],
                                        eye_f[:, :]).then_inc(pe.h, pe.tick())
                    assert pe.n == pe_vtr[-1]

                    @blk.scalar
                    def _(sc):
                        for ti in range(NB):
                            ps = [ps_va, ps_vb][ti % 2]
                            sc.wait_ge(pe.h, pe_vtr[ti])
                            sc.copy(vrows[0:2, ti * 128:(ti + 1) * 128], ps[0:2, :])\
                                .then_inc(ac.h, ac.tick())
                    assert ac.n == ac_vev[-1]

                    @blk.tensor
                    def _(t):
                        t.wait_ge(ll.h, ll_m2)
                        t.wait_ge(ac.h, ac_vev[-1])
                        mm = None
                        for tg in range(4):
                            mm = t.matmul(ps_c[0:NE, tg * 512:(tg + 1) * 512],
                                          m2_sb[0:2, 0:NE],
                                          vrows[0:2, tg * 512:(tg + 1) * 512],
                                          start=True, stop=True)
                        mm.then_inc(pe.h, pe.tick())
                    pe_c = pe.n

                    @blk.vector
                    def _(v):
                        v.wait_ge(pe.h, pe_c)
                        v.tensor_copy(csb2[0:NE, :], ps_c[0:NE, :])
                        v.memset(cxt[0:1, :], 1.0).then_inc(dv.h, dv.tick())
                    dv_c = dv.n

                    @blk.sync
                    def _(s):
                        s.wait_ge(dv.h, dv_c)
                        s.dma_start(out=cxt[1:4, :], in_=csb2[0:NE, :])\
                            .then_inc(ll.h, ll.tick(16))
                        s.wait_ge(ll.h, ll.n)
                        s.dma_start(out=cx_hbm[0:4, :], in_=cxt[0:4, :])\
                            .then_inc(so.h, so.tick(16))
                        s.dma_start(out=cx_hbm[4:7, :], in_=csb2[0:NE, :])\
                            .then_inc(so.h, so.tick(16))
                        if dbg_small is not None:
                            s.dma_start(out=dbg_small[24:28, 0:T], in_=cxt[0:4, :])\
                                .then_inc(so.h, so.tick(16))
                            s.dma_start(out=dbg_small[32:34, 0:16], in_=oh[0:2, :])\
                                .then_inc(so.h, so.tick(16))
                            s.dma_start(out=dbg_small[36:38, 0:T], in_=vrows[0:2, :])\
                                .then_inc(so.h, so.tick(16))
                            s.dma_start(out=dbg_small[40:41, 0:64], in_=sc_tm[0:1, :])\
                                .then_inc(so.h, so.tick(16))

        if upto <= 2:
            xn_stack.close()
            _final_wait(nc, so)
            return nc

        p3 = ExitStack()
        with p3:
            csb_row = p3.enter_context(nc.sbuf_tensor("csb_row", [1, NE * T], F32))
            w1g = [p3.enter_context(nc.sbuf_tensor(f"w1g{i}", [128, KC * 128], BF16))
                   for i in range(2)]
            cbc = p3.enter_context(nc.sbuf_tensor("cbc", [128, T], F32))
            stg = [p3.enter_context(nc.sbuf_tensor(f"stg{i}", [128, T], F32))
                   for i in range(2)]
            abf = [p3.enter_context(nc.sbuf_tensor(f"abf{i}", [128, T], BF16))
                   for i in range(2)]
            FUNC_E = [AF.Tanh, AF.Gelu, AF.Silu]

            snap3 = snapshot()
            with nc.psum_tensor("ps_f0", [128, T], F32) as ps_f0, \
                 nc.psum_tensor("ps_f1", [128, T], F32) as ps_f1, \
                 nc.Block("p3") as blk:
                ps_f = [ps_f0, ps_f1]

                wl0, pe0, ac0, dv0, so0 = wl.n, pe.n, ac.n, dv.n, so.n
                ll_csb = ll.n + 48
                wl_w1, pe_ch, pe_cbc, dv_cbc, ac_ev, dv_tt, so_st = [], [], [], [], [], [], []
                wn, pn, an, dn, sn = wl0, pe0, ac0, dv0, so0
                for e in range(NE):
                    pn += 1
                    pe_cbc.append(pn)
                    dn += 1
                    dv_cbc.append(dn)
                    for fc in range(KC):
                        wn += 32
                        wl_w1.append(wn)
                        pn += 1
                        pe_ch.append(pn)
                        an += 1
                        ac_ev.append(an)
                        dn += 1
                        dv_tt.append(dn)
                        sn += 16
                        so_st.append(sn)

                @blk.gpsimd
                def _(g):
                    wait_snap(g, snap3)
                    for e in range(NE):
                        for fc in range(KC):
                            i = e * KC + fc
                            b = i % 2
                            if i >= 2:
                                g.wait_ge(pe.h, pe_ch[i - 2])
                            for hh in range(2):
                                src = w1_e[e][hh * 2048:(hh + 1) * 2048,
                                              fc * 128:(fc + 1) * 128]
                                src = src.rearrange("(kc p) c -> p kc c", p=128)
                                dst = w1g[b][:, hh * 16 * 128:(hh + 1) * 16 * 128]
                                dst = dst.rearrange("p (kc c) -> p kc c", kc=16)
                                g.dma_start(out=dst, in_=src).then_inc(wl.h, wl.tick(16))
                assert wl.n == wl_w1[-1]

                @blk.tensor
                def _(t):
                    wait_snap(t, snap3)
                    t.wait_ge(ll.h, ll_csb)
                    for e in range(NE):
                        if e > 0:
                            t.wait_ge(ac.h, ac_ev[e * KC - 1])
                            t.wait_ge(ac.h, ac_ev[e * KC - 2])
                        mm = None
                        for tg in range(4):
                            mm = t.matmul(ps_f0[:, tg * 512:(tg + 1) * 512], oner_f[0:1, :],
                                          csb_row[0:1, e * T + tg * 512: e * T + (tg + 1) * 512],
                                          start=True, stop=True)
                        mm.then_inc(pe.h, pe.tick())
                        assert pe.n == pe_cbc[e]
                        for fc in range(KC):
                            i = e * KC + fc
                            b = i % 2
                            t.wait_ge(wl.h, wl_w1[i])
                            if fc == 0:
                                t.wait_ge(dv.h, dv_cbc[e])
                            if i >= 2:
                                t.wait_ge(ac.h, ac_ev[i - 2])
                            mm = None
                            for kc in range(KC):
                                for tg in range(4):
                                    mm = t.matmul(
                                        ps_f[b][:, tg * 512:(tg + 1) * 512],
                                        w1g[b][:, kc * 128:(kc + 1) * 128],
                                        xn[:, kc * XNW + 2 + tg * 512:
                                           kc * XNW + 2 + (tg + 1) * 512],
                                        start=(kc == 0), stop=(kc == KC - 1))
                            mm.then_inc(pe.h, pe.tick())
                            assert pe.n == pe_ch[i]

                @blk.vector
                def _(v):
                    for e in range(NE):
                        v.wait_ge(pe.h, pe_cbc[e])
                        v.tensor_copy(cbc[:], ps_f0[:]).then_inc(dv.h, dv.tick())
                        v.wait_ge(dv.h, dv.n)
                        assert dv.n == dv_cbc[e]
                        for fc in range(KC):
                            i = e * KC + fc
                            b = i % 2
                            v.wait_ge(ac.h, ac_ev[i])
                            if i >= 2:
                                v.wait_ge(so.h, so_st[i - 2])
                            v.tensor_tensor(abf[b][:], stg[b][:], cbc[:], op=ALU.mult)\
                                .then_inc(dv.h, dv.tick())
                            assert dv.n == dv_tt[i]

                @blk.scalar
                def _(sc):
                    for e in range(NE):
                        for fc in range(KC):
                            i = e * KC + fc
                            b = i % 2
                            sc.wait_ge(pe.h, pe_ch[i])
                            if i >= 2:
                                sc.wait_ge(dv.h, dv_tt[i - 2])
                            sc.activation(stg[b][:], ps_f[b][:], FUNC_E[e],
                                          bias=b1t_sb[:, e * KC + fc: e * KC + fc + 1])\
                                .then_inc(ac.h, ac.tick())
                            assert ac.n == ac_ev[i]

                @blk.sync
                def _(s):
                    wait_snap(s, snap3)
                    for e in range(NE):
                        s.dma_start(out=csb_row[0:1, e * T:(e + 1) * T],
                                    in_=cx_hbm[4 + e:5 + e, :])\
                            .then_inc(ll.h, ll.tick(16))
                    assert ll.n == ll_csb
                    for e in range(NE):
                        for fc in range(KC):
                            i = e * KC + fc
                            b = i % 2
                            s.wait_ge(dv.h, dv_tt[i])
                            s.dma_start(out=act_hbm[(e * KC + fc) * 128:
                                                    (e * KC + fc + 1) * 128, :],
                                        in_=abf[b][:]).then_inc(so.h, so.tick(16))
                            assert so.n == so_st[i]

        if upto <= 3:
            xn_stack.close()
            _final_wait(nc, so)
            return nc

        p4 = ExitStack()
        with p4:
            cw = [p4.enter_context(nc.sbuf_tensor(f"cw{i}", [128, 3 * KC * 128], BF16))
                  for i in range(2)]
            qbf = [p4.enter_context(nc.sbuf_tensor(f"qbf{i}", [128, T], BF16))
                   for i in range(2)]

            snap4 = snapshot()
            with nc.psum_tensor("ps_c0", [128, T], F32) as ps_c0, \
                 nc.psum_tensor("ps_c1", [128, T], F32) as ps_c1, \
                 nc.Block("p4") as blk:
                ps_f = [ps_c0, ps_c1]

                wl0, pe0, dv0, so0 = wl.n, pe.n, dv.n, so.n
                wl_cw = [wl0 + 96 * (i + 1) for i in range(QC)]
                pe_ch4 = [pe0 + (i + 1) for i in range(QC)]
                dv_ev4 = [dv0 + (i + 1) for i in range(QC)]
                so_st4 = [so0 + 16 * (i + 1) for i in range(QC)]

                @blk.gpsimd
                def _(g):
                    wait_snap(g, snap4)
                    for fc in range(QC):
                        b = fc % 2
                        if fc >= 2:
                            g.wait_ge(pe.h, pe_ch4[fc - 2])
                        for tap in range(3):
                            for hh in range(2):
                                src = conv_wt[tap][hh * 2048:(hh + 1) * 2048,
                                                   fc * 128:(fc + 1) * 128]
                                src = src.rearrange("(kc p) c -> p kc c", p=128)
                                dst = cw[b][:, (tap * KC + hh * 16) * 128:
                                            (tap * KC + hh * 16 + 16) * 128]
                                dst = dst.rearrange("p (kc c) -> p kc c", kc=16)
                                g.dma_start(out=dst, in_=src).then_inc(wl.h, wl.tick(16))
                assert wl.n == wl_cw[-1]

                @blk.tensor
                def _(t):
                    wait_snap(t, snap4)
                    for fc in range(QC):
                        b = fc % 2
                        t.wait_ge(wl.h, wl_cw[fc])
                        if fc >= 2:
                            t.wait_ge(dv.h, dv_ev4[fc - 2])
                        mm = None
                        for kc in range(KC):
                            for tap in range(3):
                                for tg in range(4):
                                    mm = t.matmul(
                                        ps_f[b][:, tg * 512:(tg + 1) * 512],
                                        cw[b][:, (tap * KC + kc) * 128:
                                              (tap * KC + kc + 1) * 128],
                                        xn[:, kc * XNW + 1 + tap + tg * 512:
                                           kc * XNW + 1 + tap + (tg + 1) * 512],
                                        start=(kc == 0 and tap == 0),
                                        stop=(kc == KC - 1 and tap == 2))
                        mm.then_inc(pe.h, pe.tick())
                assert pe.n == pe_ch4[-1]

                @blk.vector
                def _(v):
                    for fc in range(QC):
                        b = fc % 2
                        v.wait_ge(pe.h, pe_ch4[fc])
                        if fc >= 2:
                            v.wait_ge(so.h, so_st4[fc - 2])
                        v.tensor_scalar(qbf[b][:], ps_f[b][:], conv_bt_sb[:, fc:fc + 1],
                                        None, op0=ALU.add).then_inc(dv.h, dv.tick())
                assert dv.n == dv_ev4[-1]

                @blk.sync
                def _(s):
                    for fc in range(QC):
                        b = fc % 2
                        s.wait_ge(dv.h, dv_ev4[fc])
                        s.dma_start(out=qkv_hbm[fc * 128:(fc + 1) * 128, :],
                                    in_=qbf[b][:]).then_inc(so.h, so.tick(16))
                assert so.n == so_st4[-1]

        if upto <= 4:
            xn_stack.close()
            _final_wait(nc, so)
            return nc

        xn_stack.close()
        p5 = ExitStack()
        with p5:
            p5sb = lambda n_, s_, d_: p5.enter_context(nc.sbuf_tensor(n_, list(s_), d_))
            src_sb = p5sb("src_sb", (128, WC * T), BF16)
            src_k = src_sb
            qh_sb = p5sb("qh_sb", (128, WC * T), BF16)
            kh_sb = p5sb("kh_sb", (128, WC * T), BF16)
            vh_sb = p5sb("vh_sb", (128, NB * W), BF16)
            wbuf = [p5sb(f"wbuf{i}", (128, WC * W), BF16) for i in range(2)]
            mask_sb = p5sb("mask_sb", (128, 512), F32)
            bv_b = p5sb("bv_b", (1, W), BF16)
            s_sb = [p5sb(f"s_sb{i}", (128, 512), F32) for i in range(2)]
            e_sb = [p5sb(f"e_sb{i}", (128, 512), F32) for i in range(2)]
            p_f = [p5sb(f"p_f{i}", (128, 512), F32) for i in range(2)]
            pT_sb = [p5sb(f"pT_sb{i}", (128, 512), BF16) for i in range(2)]
            atmp = p5sb("atmp", (128, 12), F32)
            obf = [p5sb(f"obf{i}", (128, T), BF16) for i in range(2)]

            def oslice(h, blki):
                return src_sb[(h % 2) * 64:(h % 2) * 64 + 64,
                              (h // 2) * T + blki * 128:(h // 2) * T + blki * 128 + 128]

            def hslice(hsb, h, blki):
                return hsb[(h % 2) * 64:(h % 2) * 64 + 64,
                           (h // 2) * T + blki * 128:(h // 2) * T + blki * 128 + 128]

            snap5 = snapshot()
            with nc.psum_tensor("ps_q0", [128, T], F32) as ps_q0, \
                 nc.psum_tensor("ps_q1", [128, T], F32) as ps_q1, \
                 nc.Block("p5abc") as blk:
                ps_f = [ps_q0, ps_q1]

                ll0, wl0, pe0 = ll.n, wl.n, pe.n
                ac0b = ac.n
                ll_mask = ll0 + 16
                ll_src = [ll_mask + 16 * 8 * (p + 1) for p in range(3)]
                wl_w = [wl0 + 16 * (p + 2) for p in range(3)]
                dv0b = dv.n
                pe_q = [pe0 + (i + 1) for i in range(WC)]
                pe_k = [pe_q[-1] + (i + 1) for i in range(WC)]
                pe_v = [pe_k[-1] + (i + 1) for i in range(NB)]
                dv_seq = [dv0b + (i + 1) for i in range(2 * WC + NB)]

                @blk.sync
                def _(s):
                    wait_snap(s, snap5)
                    s.dma_start(out=mask_sb[:], in_=mask4_d[:, :]).then_inc(ll.h, ll.tick(16))
                    for part in range(3):
                        dst = [src_sb, src_k, src_sb][part]
                        if part == 1:
                            s.wait_ge(pe.h, pe_q[-1])
                        if part == 2:
                            s.wait_ge(pe.h, pe_k[-1])
                        for kc in range(WC):
                            s.dma_start(out=dst[:, kc * T:(kc + 1) * T],
                                        in_=qkv_hbm[(part * WC + kc) * 128:
                                                    (part * WC + kc + 1) * 128, :])\
                                .then_inc(ll.h, ll.tick(16))
                assert ll.n == ll_src[-1]

                @blk.gpsimd
                def _(g):
                    wait_snap(g, snap5)
                    g.dma_start(out=bv_b[:], in_=bv_row[:, :]).then_inc(wl.h, wl.tick(16))
                    for part, wsrc in enumerate([wq_f, wk_f, wv_f]):
                        b = part % 2
                        if part == 2:
                            g.wait_ge(pe.h, pe_q[-1])
                        src = wsrc[:, :].rearrange("(kc p) c -> p kc c", p=128)
                        dst = wbuf[b][:].rearrange("p (kc c) -> p kc c", kc=WC)
                        g.dma_start(out=dst, in_=src).then_inc(wl.h, wl.tick(16))
                assert wl.n == wl_w[-1]

                @blk.tensor
                def _(t):
                    for part in range(2):
                        t.wait_ge(ll.h, ll_src[part])
                        t.wait_ge(wl.h, wl_w[part])
                        base_pe = [pe_q, pe_k][part]
                        psrc = [src_sb, src_k][part]
                        for fc in range(WC):
                            i = part * WC + fc
                            b = i % 2
                            if i >= 2:
                                t.wait_ge(dv.h, dv_seq[i - 2])
                            mm = None
                            for kc in range(WC):
                                for tg in range(4):
                                    mm = t.matmul(
                                        ps_f[b][:, tg * 512:(tg + 1) * 512],
                                        wbuf[part][:, (kc * WC + fc) * 128:
                                                   (kc * WC + fc + 1) * 128],
                                        psrc[:, kc * T + tg * 512:
                                             kc * T + (tg + 1) * 512],
                                        start=(kc == 0), stop=(kc == WC - 1))
                            mm.then_inc(pe.h, pe.tick())
                            assert pe.n == base_pe[fc]
                    t.wait_ge(ll.h, ll_src[2])
                    t.wait_ge(wl.h, wl_w[2])
                    for blki in range(NB):
                        i = 2 * WC + blki
                        b = i % 2
                        t.wait_ge(dv.h, dv_seq[i - 2])
                        mm = None
                        for nh in range(2):
                            for kc in range(WC):
                                t.matmul(ps_f[b][:, nh * 512:(nh + 1) * 512],
                                         src_sb[:, kc * T + blki * 128:
                                                kc * T + blki * 128 + 128],
                                         wbuf[0][:, kc * W + nh * 512:
                                                 kc * W + (nh + 1) * 512],
                                         start=(kc == 0), stop=False)
                            mm = t.matmul(ps_f[b][:, nh * 512:(nh + 1) * 512],
                                          oner_b[0:1, :],
                                          bv_b[0:1, nh * 512:(nh + 1) * 512],
                                          start=False, stop=True)
                        mm.then_inc(pe.h, pe.tick())
                        assert pe.n == pe_v[blki]

                @blk.vector
                def _(v):
                    for part in range(2):
                        base_pe = [pe_q, pe_k][part]
                        dst = [qh_sb, kh_sb][part]
                        for fc in range(WC):
                            i = part * WC + fc
                            b = i % 2
                            v.wait_ge(pe.h, base_pe[fc])
                            if part == 0:
                                v.tensor_scalar(dst[:, fc * T:(fc + 1) * T], ps_f[b][:],
                                                0.125, bqs_sb[:, fc:fc + 1],
                                                op0=ALU.mult, op1=ALU.add)\
                                    .then_inc(dv.h, dv.tick())
                            else:
                                v.tensor_scalar(dst[:, fc * T:(fc + 1) * T], ps_f[b][:],
                                                bkt_sb[:, fc:fc + 1], None, op0=ALU.add)\
                                    .then_inc(dv.h, dv.tick())
                            assert dv.n == dv_seq[i]
                    for blki in range(NB):
                        i = 2 * WC + blki
                        b = i % 2
                        v.wait_ge(pe.h, pe_v[blki])
                        v.tensor_copy(vh_sb[:, blki * W:(blki + 1) * W], ps_f[b][:, 0:W])\
                            .then_inc(dv.h, dv.tick())
                        assert dv.n == dv_seq[i]

            if dbg_qh is not None:
                with nc.Block("p5dbg") as blk:
                    @blk.sync
                    def _(s):
                        s.wait_ge(dv.h, dv.n)
                        s.dma_start(out=dbg_qh[:, 0:WC * T], in_=qh_sb[:])\
                            .then_inc(so.h, so.tick(16))
                        s.dma_start(out=dbg_qh[:, WC * T:2 * WC * T], in_=kh_sb[:])\
                            .then_inc(so.h, so.tick(16))
                        s.dma_start(out=dbg_qh[:, 2 * WC * T:3 * WC * T], in_=vh_sb[:])\
                            .then_inc(so.h, so.tick(16))

            if upto <= 4.5:
                _final_wait(nc, so)
                return nc

            with nc.psum_tensor("ps_a0", [128, 128], F32) as ps_a0, \
                 nc.psum_tensor("ps_a1", [128, 128], F32) as ps_a1, \
                 nc.psum_tensor("ps_a2", [128, 128], F32) as ps_a2, \
                 nc.psum_tensor("ps_a3", [128, 128], F32) as ps_a3, \
                 nc.psum_tensor("ps_b0", [128, 128], F32) as ps_b0, \
                 nc.psum_tensor("ps_b1", [128, 128], F32) as ps_b1, \
                 nc.psum_tensor("ps_b2", [128, 128], F32) as ps_b2, \
                 nc.psum_tensor("ps_b3", [128, 128], F32) as ps_b3, \
                 nc.Block("p5d") as blk:
                ps_set = [[ps_a0, ps_a1, ps_a2, ps_a3], [ps_b0, ps_b1, ps_b2, ps_b3]]

                NG = NB * 4 if ng is None else ng
                pe0b, ac0b, dv0b = pe.n, ac.n, dv.n
                pe_sc, pe_tr, pe_oc = {}, {}, {}
                c = pe0b
                for i in range(NG):
                    c += 1
                    pe_sc[i] = c
                    if i >= 1:
                        g = i - 1
                        pe_tr[g] = [c + 1 + j for j in range(4)]
                        c += 5
                        pe_oc[g] = c
                g = NG - 1
                pe_tr[g] = [c + 1 + j for j in range(4)]
                c += 5
                pe_oc[g] = c
                ac_ex, ac_t1, ac_t2, ac_oe = {}, {}, {}, {}
                c = ac0b
                for i in range(NG):
                    c += 1
                    ac_ex[i] = c
                    if i >= 1:
                        g = i - 1
                        ac_t1[g] = c + 1
                        ac_t2[g] = c + 2
                        ac_oe[g] = c + 3
                        c += 3
                g = NG - 1
                ac_t1[g] = c + 1
                ac_t2[g] = c + 2
                ac_oe[g] = c + 3
                c += 3
                dv_mk = [dv0b + 4 * i + 2 for i in range(NG)]
                dv_pr = [dv0b + 4 * i + 4 for i in range(NG)]
                dv_last_v = dv_seq[-1]

                @blk.tensor
                def _(t):
                    t.wait_ge(dv.h, dv_last_v)

                    def emit_tro(g):
                        Sg = ps_set[g % 2]
                        blkg, hgg = g // 4, g % 4
                        t.wait_ge(dv.h, dv_pr[g])
                        for j in range(4):
                            t.transpose(Sg[j][:, :], p_f[g % 2][:, j * 128:(j + 1) * 128],
                                        eye_f[:, :]).then_inc(pe.h, pe.tick())
                            assert pe.n == pe_tr[g][j]
                        t.wait_ge(ac.h, ac_t2[g])
                        mm = None
                        for j in range(4):
                            h = hgg * 4 + j
                            mm = t.matmul(Sg[j][0:64, 0:128],
                                          vh_sb[:, blkg * W + h * 64:
                                                blkg * W + h * 64 + 64],
                                          pT_sb[g % 2][:, j * 128:(j + 1) * 128],
                                          start=True, stop=True)
                        mm.then_inc(pe.h, pe.tick())
                        assert pe.n == pe_oc[g]

                    for i in range(NG):
                        blki, hg = i // 4, i % 4
                        S = ps_set[i % 2]
                        if i >= 2:
                            t.wait_ge(ac.h, ac_oe[i - 2])
                        mm = None
                        for j in range(4):
                            h = hg * 4 + j
                            mm = t.matmul(S[j][:, :],
                                          hslice(qh_sb, h, blki), hslice(kh_sb, h, blki),
                                          start=True, stop=True)
                        mm.then_inc(pe.h, pe.tick())
                        assert pe.n == pe_sc[i]
                        if i >= 1:
                            emit_tro(i - 1)
                    emit_tro(NG - 1)

                @blk.vector
                def _(v):
                    v.wait_ge(ll.h, ll_mask)
                    for i in range(NG):
                        b = i % 2
                        S = ps_set[i % 2]
                        v.wait_ge(pe.h, pe_sc[i])
                        if i >= 2:
                            v.wait_ge(ac.h, ac_ex[i - 2])
                        tt = None
                        for j in range(4):
                            tt = v.tensor_tensor(s_sb[b][:, j * 128:(j + 1) * 128],
                                                 S[j][:, :], mask_sb[:, 0:128],
                                                 op=ALU.add)
                        vbar(v, tt)
                        rm = None
                        for j in range(4):
                            rm = v.reduce_max(atmp[:, j:j + 1],
                                              s_sb[b][:, j * 128:(j + 1) * 128],
                                              axis=AX, negate=True)
                        rm.then_inc(dv.h, dv.tick())
                        assert dv.n == dv_mk[i]
                        v.wait_ge(ac.h, ac_ex[i])
                        if i >= 2:
                            v.wait_ge(pe.h, pe_tr[i - 2][3])
                        v.reciprocal(atmp[:, 8:12], atmp[:, 4:8])\
                            .then_inc(dv.h, dv.tick())
                        v.wait_ge(dv.h, dv.n)
                        mmv = None
                        for j in range(4):
                            mmv = v.tensor_scalar(p_f[b][:, j * 128:(j + 1) * 128],
                                                  e_sb[b][:, j * 128:(j + 1) * 128],
                                                  atmp[:, 8 + j:9 + j], None, op0=ALU.mult)
                        mmv.then_inc(dv.h, dv.tick())
                        assert dv.n == dv_pr[i]

                @blk.scalar
                def _(sc):
                    def emit_evacs(g):
                        Sg = ps_set[g % 2]
                        blkg, hgg = g // 4, g % 4
                        for jp in range(2):
                            sc.wait_ge(pe.h, pe_tr[g][2 * jp + 1])
                            cc = None
                            for j in (2 * jp, 2 * jp + 1):
                                cc = sc.copy(pT_sb[g % 2][:, j * 128:(j + 1) * 128],
                                             Sg[j][:, :])
                            cc.then_inc(ac.h, ac.tick())
                        assert ac.n == ac_t2[g]
                        sc.wait_ge(pe.h, pe_oc[g])
                        cc = None
                        for j in range(4):
                            h = hgg * 4 + j
                            cc = sc.copy(oslice(h, blkg), Sg[j][0:64, 0:128])
                        cc.then_inc(ac.h, ac.tick())
                        assert ac.n == ac_oe[g]

                    for i in range(NG):
                        b = i % 2
                        sc.wait_ge(dv.h, dv_mk[i])
                        if i >= 2:
                            sc.wait_ge(dv.h, dv_pr[i - 2])
                        ee = None
                        for j in range(4):
                            ee = sc.activation(e_sb[b][:, j * 128:(j + 1) * 128],
                                               s_sb[b][:, j * 128:(j + 1) * 128],
                                               AF.Exp, bias=atmp[:, j:j + 1],
                                               accum_out=atmp[:, 4 + j:5 + j])
                        ee.then_inc(ac.h, ac.tick())
                        assert ac.n == ac_ex[i]
                        if i >= 1:
                            emit_evacs(i - 1)
                    emit_evacs(NG - 1)

            if dbg_o is not None:
                with nc.Block("p5dbg2") as blk:
                    @blk.sync
                    def _(s):
                        s.wait_ge(ac.h, ac.n)
                        s.dma_start(out=dbg_o[:, :], in_=src_sb[:])\
                            .then_inc(so.h, so.tick(16))

            if upto <= 4.7:
                _final_wait(nc, so)
                return nc

            with nc.psum_tensor("ps_w0", [128, T], F32) as ps_w0, \
                 nc.psum_tensor("ps_w1", [128, T], F32) as ps_w1, \
                 nc.Block("p5e") as blk:
                ps_f = [ps_w0, ps_w1]
                wl0, pe0, dv0c, so0 = wl.n, pe.n, dv.n, so.n
                pe_w = [pe0 + (i + 1) for i in range(WC)]
                ac_w = [dv0c + (i + 1) for i in range(WC)]
                so_w = [so0 + 16 * (i + 1) for i in range(WC)]

                @blk.gpsimd
                def _(g):
                    g.wait_ge(pe.h, pe0)
                    src = wo_f[:, :].rearrange("(kc p) c -> p kc c", p=128)
                    dst = wbuf[1][:].rearrange("p (kc c) -> p kc c", kc=WC)
                    g.dma_start(out=dst, in_=src).then_inc(wl.h, wl.tick(16))
                wl_wo = wl.n

                @blk.tensor
                def _(t):
                    t.wait_ge(wl.h, wl_wo)
                    t.wait_ge(ac.h, ac.n)
                    for fc in range(WC):
                        b = fc % 2
                        if fc >= 2:
                            t.wait_ge(dv.h, ac_w[fc - 2])
                        mm = None
                        for kc in range(WC):
                            for tg in range(4):
                                mm = t.matmul(ps_f[b][:, tg * 512:(tg + 1) * 512],
                                              wbuf[1][:, (kc * WC + fc) * 128:
                                                      (kc * WC + fc + 1) * 128],
                                              src_sb[:, kc * T + tg * 512:
                                                     kc * T + (tg + 1) * 512],
                                              start=(kc == 0), stop=(kc == WC - 1))
                        mm.then_inc(pe.h, pe.tick())
                        assert pe.n == pe_w[fc]

                @blk.vector
                def _(v):
                    for fc in range(WC):
                        b = fc % 2
                        v.wait_ge(pe.h, pe_w[fc])
                        if fc >= 2:
                            v.wait_ge(so.h, so_w[fc - 2])
                        v.tensor_scalar(obf[b][:], ps_f[b][:], bot_sb[:, fc:fc + 1],
                                        None, op0=ALU.add).then_inc(dv.h, dv.tick())
                        assert dv.n == ac_w[fc]

                @blk.sync
                def _(s):
                    for fc in range(WC):
                        b = fc % 2
                        s.wait_ge(dv.h, ac_w[fc])
                        s.dma_start(out=o2_hbm[fc * 128:(fc + 1) * 128, :],
                                    in_=obf[b][:]).then_inc(so.h, so.tick(16))
                assert so.n == so_w[-1]

        if upto <= 5:
            _final_wait(nc, so)
            return nc

        p7 = ExitStack()
        with p7:
            p7sb = lambda n_, s_, d_: p7.enter_context(nc.sbuf_tensor(n_, list(s_), d_))
            lhsb = [p7sb(f"lhsb{i}", (128, 8 * T), BF16) for i in range(2)]
            rhsb = [p7sb(f"rhsb{i}", (128, 8 * W), BF16) for i in range(2)]
            acc = p7sb("acc", (128, NB * W), F32)
            xbuf = [p7sb(f"xbuf{i}", (128, W), F32) for i in range(2)]
            oxb = [p7sb(f"oxb{i}", (128, W), F32) for i in range(2)]
            B_sb = [p7sb(f"B_sb{i}", (4, W), BF16) for i in range(2)]
            cxp = p7sb("cxp", (4, T), BF16)

            NGRP = 14
            snap7 = snapshot()
            with nc.psum_tensor("ps_t0", [128, W], F32) as ps_t0, \
                 nc.psum_tensor("ps_t1", [128, W], F32) as ps_t1, \
                 nc.psum_tensor("ps_t2", [128, W], F32) as ps_t2, \
                 nc.Block("p7") as blk:
                ps_t = [ps_t0, ps_t1, ps_t2]

                sl0, wl0, ll0, pe0, dv0, so0 = sl.n, wl.n, ll.n, pe.n, dv.n, so.n
                sl_lhs, wl_rhs = {}, {}
                pe_g, dv_g = {}, {}
                ll_B, ll_x, dv_fin, so_out = {}, {}, {}, {}
                sln, wln, lln, pen, dvn, son = sl0, wl0, ll0, pe0, dv0, so0
                wln += 16
                wl_cxp = wln
                gidx = 0
                for fh in range(4):
                    wln += 16
                    ll_B[fh] = wln
                    for g in range(1, 14):
                        sln += 16 * 8
                        sl_lhs[(fh, g)] = sln
                        wln += 16
                        wl_rhs[(fh, g)] = wln
                    for g in range(NGRP):
                        for tt in range(NB):
                            pen += 1
                            pe_g[gidx] = pen
                            gidx += 1
                    for tt in range(NB):
                        lln += 16
                        ll_x[(fh, tt)] = lln
                        son += 16
                        so_out[(fh, tt)] = son
                NGI = gidx
                ac_g0 = {}
                for tt in range(NB):
                    dvn += 1
                    dv_g[(0 * NGRP + 0) * NB + tt] = dvn
                for fh in range(4):
                    for g in range(1, NGRP):
                        for tt in range(NB):
                            dvn += 1
                            dv_g[(fh * NGRP + g) * NB + tt] = dvn
                    for tt in range(NB):
                        dvn += 1
                        dv_fin[(fh, tt)] = dvn
                        if fh < 3:
                            dvn += 1
                            dv_g[((fh + 1) * NGRP + 0) * NB + tt] = dvn

                def grp_src(g):
                    if g <= 12:
                        e, kk = (g - 1) // 4, (g - 1) % 4
                        lhs_rows = [(e * KC + kk * 8 + i) * 128 for i in range(8)]
                        return ("act", lhs_rows, w2_e[e], kk * 8)
                    else:
                        lhs_rows = [i * 128 for i in range(8)]
                        return ("o2", lhs_rows, proj_w, 0)

                @blk.scalar
                def _(sc):
                    wait_snap(sc, snap7)
                    for fh in range(4):
                        for g in range(1, 14):
                            lg = fh * 13 + (g - 1)
                            gb = lg % 2
                            if lg >= 2:
                                fh2, g2 = divmod(lg - 2, 13)
                                sc.wait_ge(pe.h, pe_g[(fh2 * NGRP + g2 + 1) * NB + NB - 1])
                            kind, rows_, _, _ = grp_src(g)
                            srcten = act_hbm if kind == "act" else o2_hbm
                            for i2, r0 in enumerate(rows_):
                                sc.dma_start(out=lhsb[gb][:, i2 * T:(i2 + 1) * T],
                                             in_=srcten[r0:r0 + 128, :])\
                                    .then_inc(sl.h, sl.tick(16))
                            assert sl.n == sl_lhs[(fh, g)]

                @blk.gpsimd
                def _(g_):
                    wait_snap(g_, snap7)
                    g_.dma_start(out=cxp[0:4, :], in_=cx_hbm[0:4, :])\
                        .then_inc(wl.h, wl.tick(16))
                    assert wl.n == wl_cxp
                    for fh in range(4):
                        if fh >= 2:
                            g_.wait_ge(pe.h, pe_g[((fh - 2) * NGRP + 0) * NB + NB - 1])
                        g_.dma_start(out=B_sb[fh % 2][0:4, :],
                                     in_=Bmat[0:4, fh * W:(fh + 1) * W])\
                            .then_inc(wl.h, wl.tick(16))
                        assert wl.n == ll_B[fh]
                        for g in range(1, 14):
                            lg = fh * 13 + (g - 1)
                            gb = lg % 2
                            if lg >= 2:
                                fh2, g2 = divmod(lg - 2, 13)
                                g_.wait_ge(pe.h, pe_g[(fh2 * NGRP + g2 + 1) * NB + NB - 1])
                            _, _, wten, kbase = grp_src(g)
                            src = wten[kbase * 128:(kbase + 8) * 128,
                                       fh * W:(fh + 1) * W]
                            src = src.rearrange("(kc p) c -> p kc c", p=128)
                            dst = rhsb[gb][:].rearrange("p (kc c) -> p kc c", kc=8)
                            g_.dma_start(out=dst, in_=src).then_inc(wl.h, wl.tick(16))
                            assert wl.n == wl_rhs[(fh, g)]

                @blk.sync
                def _(s):
                    wait_snap(s, snap7)
                    for fh in range(4):
                        for tt in range(NB):
                            bx = tt % 2
                            if fh > 0 or tt >= 2:
                                pf, pt = (fh, tt - 2) if tt >= 2 else (fh - 1, tt + NB - 2)
                                s.wait_ge(dv.h, dv_fin[(pf, pt)])
                            s.dma_start(out=xbuf[bx][:],
                                        in_=x_tm[tt * 128:(tt + 1) * 128,
                                                 fh * W:(fh + 1) * W])\
                                .then_inc(ll.h, ll.tick(16))
                            assert ll.n == ll_x[(fh, tt)]
                            s.wait_ge(dv.h, dv_fin[(fh, tt)])
                            s.dma_start(out=out[tt * 128:(tt + 1) * 128,
                                                fh * W:(fh + 1) * W],
                                        in_=oxb[bx][:]).then_inc(so.h, so.tick(16))
                            assert so.n == so_out[(fh, tt)]

                @blk.tensor
                def _(t):
                    wait_snap(t, snap7)
                    t.wait_ge(wl.h, wl_cxp)
                    for fh in range(4):
                        for g in range(NGRP):
                            gb = (fh * 13 + g - 1) % 2 if g >= 1 else 0
                            if g >= 1:
                                t.wait_ge(sl.h, sl_lhs[(fh, g)])
                                t.wait_ge(wl.h, wl_rhs[(fh, g)])
                            else:
                                t.wait_ge(wl.h, ll_B[fh])
                            for tt in range(NB):
                                gi = (fh * NGRP + g) * NB + tt
                                slot = gi % 3
                                if gi >= 3:
                                    t.wait_ge(dv.h, dv_g[gi - 3])
                                mm = None
                                if g >= 1:
                                    for kc in range(8):
                                        for nh in range(2):
                                            mm = t.matmul(
                                                ps_t[slot][:, nh * 512:(nh + 1) * 512],
                                                lhsb[gb][:, kc * T + tt * 128:
                                                         kc * T + tt * 128 + 128],
                                                rhsb[gb][:, kc * W + nh * 512:
                                                         kc * W + (nh + 1) * 512],
                                                start=(kc == 0), stop=(kc == 7))
                                else:
                                    for nh in range(2):
                                        mm = t.matmul(
                                            ps_t[slot][:, nh * 512:(nh + 1) * 512],
                                            cxp[0:4, tt * 128:(tt + 1) * 128],
                                            B_sb[fh % 2][0:4, nh * 512:(nh + 1) * 512],
                                            start=True, stop=True)
                                mm.then_inc(pe.h, pe.tick())
                                assert pe.n == pe_g[gi]

                @blk.vector
                def _(v):
                    wait_snap(v, snap7)

                    def g0_copy(fh2, tt):
                        gi = (fh2 * NGRP + 0) * NB + tt
                        slot = gi % 3
                        v.wait_ge(pe.h, pe_g[gi])
                        v.tensor_copy(acc[:, tt * W:(tt + 1) * W], ps_t[slot][:])\
                            .then_inc(dv.h, dv.tick())
                        assert dv.n == dv_g[gi]

                    for tt in range(NB):
                        g0_copy(0, tt)
                    for fh in range(4):
                        for g in range(1, NGRP):
                            for tt in range(NB):
                                gi = (fh * NGRP + g) * NB + tt
                                slot = gi % 3
                                v.wait_ge(pe.h, pe_g[gi])
                                aslice = acc[:, tt * W:(tt + 1) * W]
                                v.tensor_tensor(aslice, ps_t[slot][:], aslice,
                                                op=ALU.add).then_inc(dv.h, dv.tick())
                                assert dv.n == dv_g[gi]
                        for tt in range(NB):
                            bx = tt % 2
                            v.wait_ge(ll.h, ll_x[(fh, tt)])
                            if fh > 0 or tt >= 2:
                                pf, pt = (fh, tt - 2) if tt >= 2 else (fh - 1, tt + NB - 2)
                                v.wait_ge(so.h, so_out[(pf, pt)])
                            v.tensor_tensor(oxb[bx][:], acc[:, tt * W:(tt + 1) * W],
                                            xbuf[bx][:], op=ALU.add)\
                                .then_inc(dv.h, dv.tick())
                            assert dv.n == dv_fin[(fh, tt)]
                            if fh < 3:
                                g0_copy(fh + 1, tt)

        _final_wait(nc, so)
        return nc


def _final_wait(nc, so):
    with nc.Block("fin") as blk:
        @blk.sync
        def _(s):
            s.wait_ge(so.h, so.n)



def _prep_common(inputs):
    f = np.float32
    cw = np.ascontiguousarray(np.asarray(inputs["L0_attn_conv_w"], f))
    conv_b = np.asarray(inputs["L0_attn_conv_b"], f)
    wq = np.asarray(inputs["L0_attn_wq"], f).reshape(W, W)
    wk = np.asarray(inputs["L0_attn_wk"], f).reshape(W, W)
    wv = np.asarray(inputs["L0_attn_wv"], f).reshape(W, W)
    bq = np.asarray(inputs["L0_attn_bq"], f).reshape(-1)
    bk = np.asarray(inputs["L0_attn_bk"], f).reshape(-1)
    bv = np.asarray(inputs["L0_attn_bv"], f).reshape(-1)
    wo = np.asarray(inputs["L0_attn_wo"], f).reshape(W, W)
    bo = np.asarray(inputs["L0_attn_bo"], f).reshape(-1)
    pw = np.ascontiguousarray(np.asarray(inputs["L0_attn_proj_w"], f))
    pb = np.asarray(inputs["L0_attn_proj_b"], f).reshape(-1)
    gw = np.asarray(inputs["L0_ffn_gate_w"], f)
    gb = np.asarray(inputs["L0_ffn_gate_b"], f).reshape(-1)
    w1 = np.asarray(inputs["L0_ffn_w1"], f)
    b1 = np.asarray(inputs["L0_ffn_b1"], f)
    w2 = np.asarray(inputs["L0_ffn_w2"], f)
    b2 = np.asarray(inputs["L0_ffn_b2"], f)

    tril = np.tril(np.ones((128, 128), dtype=bool))
    mask = np.where(tril, np.float32(0.0), np.float32(-1.0e9))

    common = {
        "conv_w0": np.ascontiguousarray(cw[0]),
        "conv_w1": np.ascontiguousarray(cw[1]),
        "conv_w2": np.ascontiguousarray(cw[2]),
        "conv_bt": np.ascontiguousarray(conv_b.reshape(QC, 128).T),
        "wq_f": wq, "wk_f": wk, "wv_f": wv, "wo_f": wo,
        "bqt": np.ascontiguousarray(bq.reshape(WC, 128).T),
        "bkt": np.ascontiguousarray(bk.reshape(WC, 128).T),
        "bot": np.ascontiguousarray(bo.reshape(WC, 128).T),
        "bv_row": np.ascontiguousarray(bv.reshape(1, W)),
        "proj_w": pw,
        "w1_0": np.ascontiguousarray(w1[0]),
        "w1_1": np.ascontiguousarray(w1[1]),
        "w1_2": np.ascontiguousarray(w1[2]),
        "w2_0": np.ascontiguousarray(w2[0]),
        "w2_1": np.ascontiguousarray(w2[1]),
        "w2_2": np.ascontiguousarray(w2[2]),
        "b1t": np.ascontiguousarray(
            b1.reshape(NE, KC, 128).transpose(2, 0, 1).reshape(128, NE * KC)),
        "gwa_t": np.ascontiguousarray(
            gw[:D].reshape(KC, 128, NE).transpose(1, 0, 2).reshape(128, KC * NE)),
        "gwb_col": np.ascontiguousarray(gw[D].reshape(NE, 1)),
        "gate_b_col": np.ascontiguousarray(gb.reshape(NE, 1)),
        "Bmat": np.ascontiguousarray(np.stack([pb, b2[0], b2[1], b2[2]])),
        "eye": np.eye(128, dtype=f),
        "mask4": np.ascontiguousarray(np.tile(mask, (1, 4))),
        "ones_col": np.ones((128, 1), f),
        "ones_row": np.ones((1, 128), f),
    }
    return common


def kernel(**inputs):
    x = np.ascontiguousarray(np.asarray(inputs["L0_attn_inputs"], np.float32))
    common = _prep_common(inputs)
    in_maps = []
    for c in range(NCORE):
        m = dict(common)
        m["x_tm"] = np.ascontiguousarray(x[c])
        m["x_fm"] = np.ascontiguousarray(x[c].T)
        in_maps.append(m)
    nc = build_nc()
    res = run_bass_kernel_spmd(nc, in_maps, core_ids=list(range(NCORE)))
    return np.stack([res.results[c]["out"] for c in range(NCORE)], axis=0)
